# version 03
# baseline (speedup 1.0000x reference)
import numpy as np
import ml_dtypes
from contextlib import ExitStack
import concourse.bass as bass
import concourse.mybir as mybir
from concourse.bass_utils import run_bass_kernel_spmd

F32 = mybir.dt.float32
BF16 = mybir.dt.bfloat16
U32 = mybir.dt.uint32
AF = mybir.ActivationFunctionType
ALU = mybir.AluOpType
AX = mybir.AxisListType

MAGIC = 1.5 * 2.0 ** 23
NCORES = 8
T = 8192
TC = T // NCORES
D = 2048
NHL = 2
EPS = 1e-5
ISQ = float(1.0 / np.sqrt(128.0))

DEBUG = False


class S:
    def __init__(self, stack, nc, name):
        self.h = stack.enter_context(nc.semaphore(name))
        self.n = 0

    def inc(self, instr, k=1):
        instr.then_inc(self.h, k)
        self.n += k
        return self.n

    def wait(self, eng, val=None):
        eng.wait_ge(self.h, self.n if val is None else val)


def build_program(debug=False):
    nc = bass.Bass("TRN2", target_bir_lowering=False, debug=False)

    def din(name, shape, dt=F32):
        return nc.dram_tensor(name, shape, dt, kind="ExternalInput").ap()

    x_in = din("x", [TC, D])
    wq_in = din("wq", [NHL, D, 128])
    wk_in = din("wk", [NHL, D, 128])
    wv_in = din("wv", [NHL, D, 128])
    wo_in = din("wo", [NHL, 128, D])
    bq_in = din("bq", [NHL, 128])
    bk_in = din("bk", [NHL, 128])
    bv_in = din("bv", [NHL, 128])
    bo_in = din("bo", [1, D])
    rw_in = din("rw", [NCORES, D])
    wg_in = din("wg", [D, D])
    wu_in = din("wu", [D, D])
    wd_in = din("wd", [D, D])
    cos_in = din("cos", [128, T])
    sin_in = din("sin", [128, T])
    masks_in = din("masks", [4, 128, 512], BF16)
    identb_in = din("identb", [128, 128], BF16)
    identf_in = din("identf", [128, 128], F32)
    ones_k_in = din("ones_k", [128, 1], BF16)
    ones_m_in = din("ones_m", [1, 128], BF16)
    ones_mf_in = din("ones_mf", [1, 128], F32)
    ones_kf_in = din("ones_kf", [128, 1], F32)
    eid_in = din("eid", [1, 1], F32)
    swapi_in = din("swapi", [128, 128], BF16)

    out = nc.dram_tensor("out", [TC, D], F32, kind="ExternalOutput").ap()

    def dint(name, shape, dt):
        return nc.dram_tensor(name, shape, dt, kind="Internal").ap()

    ag1_src = dint("ag1_src", [D, TC], BF16)
    ag1_dst = dint("ag1_dst", [NCORES, D, TC], BF16)
    part_src = dint("part_src", [T, D], F32)
    rs1_dst = dint("rs1_dst", [TC, D], F32)
    r2_buf = dint("r2_buf", [TC, D], F32)
    ag2_src = dint("ag2_src", [D, TC], BF16)
    ag2_dst = dint("ag2_dst", [NCORES, D, TC], BF16)
    meta_src = dint("meta_src", [2, TC], F32)
    meta_dst = dint("meta_dst", [NCORES, 2, TC], F32)
    wqt = {m: dint(f"wqt_{m}", [4, D, 512], BF16) for m in "gud"}
    y_src = dint("y_src", [T, D], F32)
    rs2_dst = dint("rs2_dst", [TC, D], F32)

    dbg = {}
    if debug:
        def dout(name, shape, dt):
            dbg[name] = nc.dram_tensor("dbg_" + name, shape, dt, kind="ExternalOutput").ap()
        dout("ag1", [NCORES, D, TC], BF16)
        dout("ag1s", [D, TC], BF16)
        dout("q", [NHL, 128, T], BF16)
        dout("k", [NHL, 128, T], BF16)
        dout("v", [NHL, 128, T], BF16)
        dout("negb", [NHL, 128], F32)
        dout("z", [NHL, 128, T], BF16)
        dout("rs1", [TC, D], F32)
        dout("r2", [TC, D], F32)
        dout("ag2", [NCORES, D, TC], BF16)
        dout("meta", [NCORES, 2, TC], F32)
        dout("wqtg", [4, D, 512], BF16)
        dout("wqtd", [4, D, 512], BF16)
        dout("winv", [128, 4], F32)
        dout("ysrc", [TC, D], F32)
        dout("rs2", [TC, D], F32)

    stack = ExitStack()
    with stack:
        def sbuf(st, name, shape, dt):
            return st.enter_context(nc.sbuf_tensor(name, shape, dt))

        def sem(name):
            return S(stack, nc, name)

        pb = [stack.enter_context(nc.psum_tensor(f"pb{i}", [128, 512], F32))
              for i in range(8)]

        def pbb(i):
            return pb[i][:].bitcast(BF16)

        def cmaj(ap2d, p=128):
            c = ap2d.shape[0] // p
            return ap2d.rearrange("(c p) q -> c p q", p=p).transpose([1, 0, 2])

        def s3(ap2d, c):
            return ap2d.rearrange("p (c q) -> p c q", c=c)

        identb = sbuf(stack, "identb_s", [128, 128], BF16)
        identf = sbuf(stack, "identf_s", [128, 128], F32)
        ones_k = sbuf(stack, "ones_k_s", [128, 1], BF16)
        ones_m = sbuf(stack, "ones_m_s", [1, 128], BF16)
        ones_mf = sbuf(stack, "ones_mf_s", [1, 128], F32)
        ones_kf = sbuf(stack, "ones_kf_s", [128, 1], F32)
        mask_sb = sbuf(stack, "mask_s", [128, 4 * 512], BF16)
        bqkv = sbuf(stack, "bqkv", [128, 6], F32)
        bvrow = sbuf(stack, "bvrow", [1, 2 * 128], F32)
        bo_bc = sbuf(stack, "bo_bc", [128, D], F32)
        rwt_sb = sbuf(stack, "rwt", [128, 16 * 8], F32)
        wo_sb = sbuf(stack, "wo_s", [128, NHL * D], BF16)
        negb_sb = sbuf(stack, "negb_s", [128, NHL], F32)
        scal = sbuf(stack, "scal", [1, 16], F32)
        winv = sbuf(stack, "winv_s", [128, 4], F32)
        eid_sb = sbuf(stack, "eid_s", [1, 1], F32)
        eidb = sbuf(stack, "eidb", [128, 1], F32)
        p0 = ExitStack()
        bo_row = sbuf(p0, "bo_row_s", [1, D], F32)
        rw_sb = sbuf(p0, "rw_s", [NCORES, D], F32)

        s_ld = sem("s_ld")
        s_gld = sem("s_gld")
        s_sld = sem("s_sld")
        s_mm = sem("s_mm")
        s_v = sem("s_v")
        s_sc = sem("s_sc")
        s_cc = sem("s_cc")
        s_bar = sem("s_bar")

        blk = stack.enter_context(nc.Block())

        def sec(engname):
            def deco(f):
                getattr(blk, engname)(f)
            return deco

        def vf(e, op):
            s_v.inc(op, 1)
            s_v.wait(e)

        ENGS = ("sync", "scalar", "vector", "tensor", "gpsimd")

        def barrier():
            for en in ENGS:
                @sec(en)
                def _(e):
                    e.sem_inc(s_bar.h, 1)
            s_bar.n += len(ENGS)
            tgt = s_bar.n
            for en in ENGS:
                @sec(en)
                def _(e):
                    e.wait_ge(s_bar.h, tgt)

        @sec("sync")
        def _(e):
            for dst, src in ((identb, identb_in), (identf, identf_in),
                             (ones_k, ones_k_in), (ones_m, ones_m_in),
                             (ones_mf, ones_mf_in), (ones_kf, ones_kf_in),
                             (rw_sb, rw_in), (bo_row, bo_in), (eid_sb, eid_in)):
                s_ld.inc(e.dma_start(out=dst[:], in_=src[:, :]), 16)
            s_ld.inc(e.dma_start(out=s3(mask_sb[:], 4),
                                 in_=masks_in.transpose([1, 0, 2])), 16)
            for h in range(NHL):
                for j, b in ((0, bq_in), (2, bk_in), (4, bv_in)):
                    s_ld.inc(e.dma_start(out=bqkv[:, j + h:j + h + 1],
                                         in_=b[h].unsqueeze(1)), 16)
                s_ld.inc(e.dma_start(out=bvrow[0:1, h * 128:(h + 1) * 128],
                                     in_=bv_in[h].unsqueeze(0)), 16)

        @sec("gpsimd")
        def _(e):
            for h in range(NHL):
                s_gld.inc(e.dma_start(out=wo_sb[:, h * D:(h + 1) * D], in_=wo_in[h]), 16)

        @sec("tensor")
        def _(e):
            s_ld.wait(e)
            for cc in range(16):
                op = e.transpose(pb[0][:, cc * 8:(cc + 1) * 8],
                                 rw_sb[:, cc * 128:(cc + 1) * 128], identf[0:8, 0:8])
            s_mm.inc(op, 1)
            op = e.matmul(pb[2][:, 0:1], ones_mf[:], eid_sb[:], start=True, stop=True)
            s_mm.inc(op, 1)
            vbase = s_v.n
            for mc in range(4):
                op = e.matmul(pb[1][:, :], ones_mf[:], bo_row[:, mc * 512:(mc + 1) * 512],
                              start=True, stop=True)
                s_mm.inc(op, 1)
                e.wait_ge(s_v.h, vbase + mc + 1)

        @sec("vector")
        def _(e):
            s_mm.wait(e, s_mm.n - 4)
            e.tensor_copy(rwt_sb[:], pb[0][:, 0:128])
            e.tensor_copy(eidb[:], pb[2][:, 0:1])
            for mc in range(4):
                s_mm.wait(e, s_mm.n - 3 + mc)
                op = e.tensor_copy(bo_bc[:, mc * 512:(mc + 1) * 512], pb[1][:, :])
                s_v.inc(op, 1)

        barrier()
        p0.close()

        wst = ExitStack()
        wf = [sbuf(wst, f"wf{i}", [128, D], F32) for i in range(2)]
        wqrow = sbuf(wst, "wqrow", [128, D], BF16)
        wacc = sbuf(wst, "wacc", [128, 16], F32)
        waccs = sbuf(wst, "waccs", [128, 1], F32)
        wstage = sbuf(wst, "wstage", [128, 16 * 128], BF16)
        swb = sbuf(wst, "swb", [128, 1], F32)

        for mi, (mname, w_in) in enumerate((("g", wg_in), ("u", wu_in), ("d", wd_in))):
            @sec("sync")
            def _(e, w_in=w_in):
                e.wait_ge(s_v.h, s_v.n)
                vb = s_v.n
                for rc in range(16):
                    if rc >= 2:
                        e.wait_ge(s_v.h, vb + rc - 1)
                    s_ld.inc(e.dma_start(out=wf[rc % 2][:],
                                         in_=w_in[rc * 128:(rc + 1) * 128, :]), 16)

            @sec("vector")
            def _(e):
                for rc in range(16):
                    s_ld.wait(e, s_ld.n - 16 * (15 - rc))
                    op = e.reduce_sum(wacc[:, rc:rc + 1], wf[rc % 2][:], axis=AX.X,
                                      apply_absolute_value=True)
                    if rc == 15:
                        vf(e, op)
                    else:
                        s_v.inc(op, 1)
                vf(e, e.reduce_sum(waccs[:], wacc[:], axis=AX.X))

            @sec("tensor")
            def _(e):
                s_v.wait(e)
                op = e.matmul(pb[6][0:1, 0:1], waccs[:], ones_kf[:], start=True, stop=True)
                s_mm.inc(op, 1)

            @sec("vector")
            def _(e):
                s_mm.wait(e)
                vf(e, e.tensor_scalar(scal[:, 0:1], pb[6][0:1, 0:1], 1.0 / (D * D), None, ALU.mult))
                vf(e, e.tensor_scalar(scal[:, 1:2], scal[:, 0:1], EPS, None, ALU.max))
                op = e.reciprocal(scal[:, 2:3], scal[:, 1:2])
                s_v.inc(op, 1)

            @sec("tensor")
            def _(e, mi=mi):
                s_v.wait(e)
                op = e.matmul(pb[7][:, 0:1], ones_mf[:], scal[:, 2:3], start=True, stop=True)
                s_mm.inc(op, 1)
                op = e.matmul(pb[7][:, 1:2], ones_mf[:], scal[:, 1:2], start=True, stop=True)
                s_mm.inc(op, 1)

            @sec("vector")
            def _(e, mi=mi):
                s_mm.wait(e)
                e.tensor_copy(swb[:], pb[7][:, 0:1])
                op = e.tensor_copy(winv[:, mi:mi + 1], pb[7][:, 1:2])
                s_v.inc(op, 1)

            @sec("sync")
            def _(e, w_in=w_in):
                e.wait_ge(s_v.h, s_v.n)
                vb = s_v.n
                for rc in range(16):
                    if rc >= 2:
                        e.wait_ge(s_v.h, vb + 4 * (rc - 1))
                    s_ld.inc(e.dma_start(out=wf[rc % 2][:],
                                         in_=w_in[rc * 128:(rc + 1) * 128, :]), 16)

            for rc in range(16):
                @sec("vector")
                def _(e, rc=rc):
                    s_ld.wait(e, s_ld.n - 16 * (15 - rc))
                    if rc >= 1:
                        s_mm.wait(e)
                    vf(e, e.tensor_scalar(wf[rc % 2][:], wf[rc % 2][:], swb[:], None, ALU.mult))
                    vf(e, e.tensor_scalar(wf[rc % 2][:], wf[rc % 2][:], MAGIC, -MAGIC,
                                          ALU.add, ALU.add))
                    op = e.tensor_scalar(wqrow[:], wf[rc % 2][:], 1.0, -1.0,
                                         ALU.min, ALU.max)
                    s_v.inc(op, 1)

                @sec("tensor")
                def _(e, rc=rc):
                    s_v.wait(e)
                    base = 4 + (rc % 2) * 2
                    for cc in range(16):
                        op = e.transpose(
                            pbb(base + cc // 8)[:, (cc % 8) * 128:(cc % 8 + 1) * 128],
                            wqrow[:, cc * 128:(cc + 1) * 128], identb[:])
                    s_mm.inc(op, 1)

                @sec("vector")
                def _(e, rc=rc):
                    s_mm.wait(e)
                    if rc >= 1:
                        s_sld.wait(e)
                    base = 4 + (rc % 2) * 2
                    e.tensor_copy(wstage[:, 0:1024], pbb(base)[:, 0:1024])
                    op = e.tensor_copy(wstage[:, 1024:2048], pbb(base + 1)[:, 0:1024])
                    s_v.inc(op, 1)

                @sec("scalar")
                def _(e, rc=rc, wqt_m=wqt[mname]):
                    s_v.wait(e)
                    op = e.dma_start(
                        out=cmaj(wqt_m[rc // 4][:, (rc % 4) * 128:(rc % 4 + 1) * 128]),
                        in_=s3(wstage[:], 16))
                    s_sld.inc(op, 16)

        barrier()

        if debug:
            @sec("gpsimd")
            def _(e):
                s_gld.inc(e.dma_start(out=dbg["wqtg"][:, :, :], in_=wqt["g"][:, :, :]), 16)
                s_gld.inc(e.dma_start(out=dbg["wqtd"][:, :, :], in_=wqt["d"][:, :, :]), 16)
                s_gld.inc(e.dma_start(out=dbg["winv"][:, :], in_=winv[:]), 16)

        wst.close()

        p1 = ExitStack()
        xt = [sbuf(p1, f"xt{i}", [128, D], F32) for i in range(2)]
        xn1b = [sbuf(p1, f"xn1b{i}", [128, D], BF16) for i in range(2)]
        stage1 = [sbuf(p1, f"stage1_{i}", [128, D], BF16) for i in range(2)]
        mv_sb = [sbuf(p1, f"mv{i}", [128, 2], F32) for i in range(2)]
        ve_sb = [sbuf(p1, f"ve{i}", [128, 4], F32) for i in range(2)]

        for t in range(8):
            barrier()
            @sec("sync")
            def _(e, t=t):
                if t >= 2:
                    e.wait_ge(s_v.h, s_v.n)
                s_ld.inc(e.dma_start(out=xt[t % 2][:],
                                     in_=x_in[t * 128:(t + 1) * 128, :]), 16)

            @sec("vector")
            def _(e, t=t):
                s_ld.wait(e)
                if t >= 2:
                    s_mm.wait(e)
                vf(e, e.reduce_sum(ve_sb[t % 2][:, 3:4], xt[t % 2][:], axis=AX.X))
                e.tensor_scalar(mv_sb[t % 2][:, 0:1], ve_sb[t % 2][:, 3:4], 1.0 / D, None, ALU.mult)
                op = e.tensor_scalar(mv_sb[t % 2][:, 1:2], ve_sb[t % 2][:, 3:4], -1.0 / D, None, ALU.mult)
                s_v.inc(op, 1)

            @sec("scalar")
            def _(e, t=t):
                s_v.wait(e)
                op = e.activation(xn1b[t % 2][:], xt[t % 2][:], AF.Square,
                                  bias=mv_sb[t % 2][:, 1:2], accum_out=ve_sb[t % 2][:, 0:1])
                s_sc.inc(op, 1)

            @sec("vector")
            def _(e, t=t):
                s_sc.wait(e)
                op = e.tensor_scalar(ve_sb[t % 2][:, 0:1], ve_sb[t % 2][:, 0:1], 1.0 / D, EPS,
                                     ALU.mult, ALU.add)
                s_v.inc(op, 1)

            @sec("scalar")
            def _(e, t=t):
                s_v.wait(e)
                op = e.activation(ve_sb[t % 2][:, 1:2], ve_sb[t % 2][:, 0:1], AF.Sqrt)
                s_sc.inc(op, 1)

            @sec("vector")
            def _(e, t=t):
                s_sc.wait(e)
                vf(e, e.reciprocal(ve_sb[t % 2][:, 2:3], ve_sb[t % 2][:, 1:2]))
                op = e.tensor_scalar(xn1b[t % 2][:], xt[t % 2][:], mv_sb[t % 2][:, 0:1],
                                     ve_sb[t % 2][:, 2:3], ALU.subtract, ALU.mult)
                s_v.inc(op, 1)

            @sec("tensor")
            def _(e, t=t):
                s_v.wait(e)
                bank = 4 + (t % 2) * 2
                for cc in range(16):
                    op = e.transpose(
                        pbb(bank + cc // 8)[:, (cc % 8) * 128:(cc % 8 + 1) * 128],
                        xn1b[t % 2][:, cc * 128:(cc + 1) * 128], identb[:])
                s_mm.inc(op, 1)

            @sec("vector")
            def _(e, t=t):
                s_mm.wait(e)
                if t >= 2:
                    s_ld.wait(e)
                bank = 4 + (t % 2) * 2
                e.tensor_copy(stage1[t % 2][:, 0:1024], pbb(bank)[:, 0:1024])
                op = e.tensor_copy(stage1[t % 2][:, 1024:2048], pbb(bank + 1)[:, 0:1024])
                s_v.inc(op, 1)

            @sec("sync")
            def _(e, t=t):
                s_v.wait(e)
                op = e.dma_start(
                    out=cmaj(ag1_src[:, t * 128:(t + 1) * 128]),
                    in_=s3(stage1[t % 2][:], 16))
                s_ld.inc(op, 16)

        @sec("gpsimd")
        def _(e):
            e.wait_ge(s_ld.h, s_ld.n)
            op = e.collective_compute(
                "AllGather", ALU.bypass, replica_groups=[list(range(NCORES))],
                ins=[ag1_src[:, :]], outs=[ag1_dst[:, :, :]])
            s_cc.inc(op, 1)

        if debug:
            @sec("gpsimd")
            def _(e):
                s_cc.wait(e)
                s_gld.inc(e.dma_start(out=dbg["ag1"][:, :, :], in_=ag1_dst[:, :, :]), 16)
                s_gld.inc(e.dma_start(out=dbg["ag1s"][:, :], in_=ag1_src[:, :]), 16)

        p1.close()
        barrier()

        p2 = ExitStack()
        q_sb = sbuf(p2, "q_sb", [128, T], BF16)
        k_sb = sbuf(p2, "k_sb", [128, T], BF16)
        v_sb = sbuf(p2, "v_sb", [128, T], BF16)
        z_sb = [sbuf(p2, f"z{h}", [128, T], BF16) for h in range(NHL)]
        bvb = sbuf(p2, "bvb", [128, 128], F32)
        p_sb = [sbuf(p2, f"p{i}", [128, 512], BF16) for i in range(2)]
        rd_sb = sbuf(p2, "rd", [1, 512], F32)
        rdb_sb = sbuf(p2, "rdb", [128, 512], F32)
        p2q = ExitStack()
        cos_sb = sbuf(p2q, "cos_s", [128, T // 2], F32)
        sin_sb = sbuf(p2q, "sin_s", [128, T // 2], F32)
        swapi = sbuf(p2q, "swapi_s", [128, 128], BF16)
        tq_b = sbuf(p2q, "tq_b", [128, 512], BF16)
        wqkv_h = sbuf(p2q, "wqkv_h", [128, 3 * D], BF16)
        xn_sb = [sbuf(p2q, f"xn{i}", [128, 16 * 512], BF16) for i in range(2)]
        tmpq = sbuf(p2q, "tmpq", [128, 512], F32)
        rop = sbuf(p2q, "rop", [128, 512], F32)
        sqt = sbuf(p2q, "sqt", [128, 512], BF16)
        qn_row = sbuf(p2q, "qn_row", [1, 16], F32)
        kn_row = sbuf(p2q, "kn_row", [1, 16], F32)

        @sec("sync")
        def _(e):
            s_ld.inc(e.dma_start(out=swapi[:], in_=swapi_in[:, :]), 16)
            s_ld.inc(e.dma_start(out=cos_sb[:], in_=cos_in[:, 0:T // 2]), 16)
            s_ld.inc(e.dma_start(out=sin_sb[:], in_=sin_in[:, 0:T // 2]), 16)

        for h in range(NHL):
            @sec("gpsimd")
            def _(e, h=h):
                if h > 0:
                    e.wait_ge(s_mm.h, s_mm.n)
                for j, w in ((0, wq_in), (1, wk_in), (2, wv_in)):
                    s_gld.inc(e.dma_start(
                        out=s3(wqkv_h[:, j * D:(j + 1) * D], 16),
                        in_=cmaj(w[h])), 16)

            @sec("tensor")
            def _(e, h=h):
                s_ld.wait(e)
                op = e.matmul(pb[6][:, 0:128], ones_mf[:],
                              bvrow[0:1, h * 128:(h + 1) * 128], start=True, stop=True)
                s_mm.inc(op, 1)

            @sec("vector")
            def _(e, h=h):
                s_mm.wait(e)
                op = e.tensor_copy(bvb[:], pb[6][:, 0:128])
                s_v.inc(op, 1)

            for grp in range(16):
                r, wnd = grp // 2, grp % 2

                @sec("sync")
                def _(e, grp=grp, r=r, wnd=wnd, h=h):
                    s_cc.wait(e)
                    if grp >= 2:
                        e.wait_ge(s_mm.h, s_mm.n)
                    if (grp == 8) or (h > 0 and grp == 0):
                        half = 1 if grp == 8 else 0
                        e.wait_ge(s_v.h, s_v.n)
                        s_ld.inc(e.dma_start(out=cos_sb[:],
                                 in_=cos_in[:, half * (T // 2):(half + 1) * (T // 2)]), 16)
                        s_ld.inc(e.dma_start(out=sin_sb[:],
                                 in_=sin_in[:, half * (T // 2):(half + 1) * (T // 2)]), 16)
                    op = e.dma_start(
                        out=s3(xn_sb[grp % 2][:], 16),
                        in_=cmaj(ag1_dst[r, :, wnd * 512:(wnd + 1) * 512]))
                    s_ld.inc(op, 16)

                @sec("tensor")
                def _(e, grp=grp, h=h):
                    s_ld.wait(e)
                    s_gld.wait(e)
                    s_v.wait(e)
                    xb = xn_sb[grp % 2]
                    for cc in range(16):
                        st, sp = cc == 0, cc == 15
                        e.matmul(pb[0][:, :], wqkv_h[:, cc * 128:(cc + 1) * 128],
                                 xb[:, cc * 512:(cc + 1) * 512], start=st, stop=sp)
                        e.matmul(pb[1][:, :], wqkv_h[:, D + cc * 128:D + (cc + 1) * 128],
                                 xb[:, cc * 512:(cc + 1) * 512], start=st, stop=sp)
                        for j in range(4):
                            op = e.matmul(pb[2][:, j * 128:(j + 1) * 128],
                                          xb[:, cc * 512 + j * 128:cc * 512 + (j + 1) * 128],
                                          wqkv_h[:, 2 * D + cc * 128:2 * D + (cc + 1) * 128],
                                          start=(cc == 0 and j == 0),
                                          stop=(cc == 15 and j == 3))
                    s_mm.inc(op, 1)

                for psrc, dst_sb, brow, nrow in ((0, q_sb, 0, qn_row), (1, k_sb, 2, kn_row)):
                    @sec("vector")
                    def _(e, h=h, psrc=psrc, brow=brow):
                        s_mm.wait(e)
                        op = e.tensor_scalar(tq_b[:], pb[psrc][:, :],
                                             bqkv[:, brow + h:brow + h + 1], None, ALU.add)
                        s_v.inc(op, 1)

                    @sec("tensor")
                    def _(e):
                        s_v.wait(e)
                        op = e.matmul(pb[5][:, :], swapi[:], tq_b[:], start=True, stop=True)
                        s_mm.inc(op, 1)

                    @sec("vector")
                    def _(e, grp=grp, dst_sb=dst_sb):
                        s_mm.wait(e)
                        g0 = grp * 512
                        c0 = (grp % 8) * 512
                        e.tensor_tensor(rop[:], tq_b[:], cos_sb[:, c0:c0 + 512], ALU.mult)
                        vf(e, e.scalar_tensor_tensor(rdb_sb[:], pb[5][:, :], 0.0,
                                               sin_sb[:, c0:c0 + 512], ALU.bypass, ALU.mult))
                        vf(e, e.tensor_add(dst_sb[:, g0:g0 + 512], rop[:], rdb_sb[:]))
                        op = e.tensor_mul(sqt[:], dst_sb[:, g0:g0 + 512],
                                          dst_sb[:, g0:g0 + 512])
                        s_v.inc(op, 1)

                    @sec("tensor")
                    def _(e):
                        s_v.wait(e)
                        op = e.matmul(pb[3][0:1, :], ones_k[:], sqt[:], start=True, stop=True)
                        s_mm.inc(op, 1)

                    @sec("vector")
                    def _(e, grp=grp, nrow=nrow):
                        s_mm.wait(e)
                        op = e.reduce_max(nrow[0:1, grp:grp + 1], pb[3][0:1, :], axis=AX.X)
                        s_v.inc(op, 1)

                @sec("vector")
                def _(e, grp=grp):
                    g0 = grp * 512
                    for j in range(4):
                        op = e.tensor_add(v_sb[:, g0 + j * 128:g0 + (j + 1) * 128],
                                          pb[2][:, j * 128:(j + 1) * 128], bvb[:])
                    s_v.inc(op, 1)

            @sec("vector")
            def _(e):
                e.reduce_max(scal[:, 4:5], qn_row[:], axis=AX.X)
                vf(e, e.reduce_max(scal[:, 5:6], kn_row[:], axis=AX.X))
                op = e.tensor_tensor(scal[:, 6:7], scal[:, 4:5], scal[:, 5:6], ALU.mult)
                s_v.inc(op, 1)

            @sec("scalar")
            def _(e):
                s_v.wait(e)
                op = e.activation(scal[:, 7:8], scal[:, 6:7], AF.Sqrt)
                s_sc.inc(op, 1)

            @sec("vector")
            def _(e):
                s_sc.wait(e)
                op = e.tensor_scalar(scal[:, 8:9], scal[:, 7:8], -1.02 * ISQ, -1.0,
                                     ALU.mult, ALU.add)
                s_v.inc(op, 1)

            @sec("tensor")
            def _(e):
                s_v.wait(e)
                op = e.matmul(pb[3][:, 0:1], ones_mf[:], scal[:, 8:9], start=True, stop=True)
                s_mm.inc(op, 1)

            @sec("vector")
            def _(e, h=h):
                s_mm.wait(e)
                op = e.tensor_copy(negb_sb[:, h:h + 1], pb[3][:, 0:1])
                s_v.inc(op, 1)

            if debug:
                @sec("gpsimd")
                def _(e, h=h):
                    e.wait_ge(s_v.h, s_v.n)
                    s_gld.inc(e.dma_start(out=dbg["q"][h], in_=q_sb[:]), 16)
                    s_gld.inc(e.dma_start(out=dbg["k"][h], in_=k_sb[:]), 16)
                    s_gld.inc(e.dma_start(out=dbg["v"][h], in_=v_sb[:]), 16)
                    s_gld.inc(e.dma_start(out=dbg["negb"][h].unsqueeze(1),
                                          in_=negb_sb[:, h:h + 1]), 16)

            barrier()

            for qg in range(16):
                q0 = qg * 512
                nkb = 4 * (qg + 1)
                exp_base = s_sc.n
                for kb in range(nkb):
                    diag = kb >= 4 * qg

                    @sec("tensor")
                    def _(e, kb=kb, q0=q0, exp_base=exp_base):
                        if kb >= 2:
                            e.wait_ge(s_sc.h, exp_base + kb - 1)
                        op = e.matmul(pb[kb % 2][:, :], k_sb[:, kb * 128:(kb + 1) * 128],
                                      q_sb[:, q0:q0 + 512], start=True, stop=True)
                        s_mm.inc(op, 1)

                    @sec("scalar")
                    def _(e, kb=kb, h=h):
                        s_mm.wait(e)
                        op = e.activation(p_sb[kb % 2][:], pb[kb % 2][:, :], AF.Exp,
                                          bias=negb_sb[:, h:h + 1], scale=ISQ)
                        s_sc.inc(op, 1)

                    if diag:
                        @sec("vector")
                        def _(e, kb=kb, qg=qg):
                            s_sc.wait(e)
                            o = kb - 4 * qg
                            op = e.tensor_mul(p_sb[kb % 2][:], p_sb[kb % 2][:],
                                              mask_sb[:, o * 512:(o + 1) * 512])
                            s_v.inc(op, 1)

                    @sec("tensor")
                    def _(e, kb=kb, diag=diag, nkb=nkb):
                        if diag:
                            s_v.wait(e)
                        else:
                            s_sc.wait(e)
                            if kb == 0:
                                s_v.wait(e)
                        st, sp = kb == 0, kb == nkb - 1
                        e.matmul(pb[2][:, :], v_sb[:, kb * 128:(kb + 1) * 128],
                                 p_sb[kb % 2][:], start=st, stop=sp)
                        op = e.matmul(pb[3][0:1, :], ones_k[:], p_sb[kb % 2][:],
                                      start=st, stop=sp)
                        s_mm.inc(op, 1)

                @sec("vector")
                def _(e):
                    s_mm.wait(e)
                    op = e.reciprocal(rd_sb[:], pb[3][0:1, :])
                    s_v.inc(op, 1)

                @sec("tensor")
                def _(e):
                    s_v.wait(e)
                    op = e.matmul(pb[4][:, :], ones_mf[:], rd_sb[:], start=True, stop=True)
                    s_mm.inc(op, 1)

                @sec("vector")
                def _(e, h=h, q0=q0):
                    s_mm.wait(e)
                    vf(e, e.tensor_copy(rdb_sb[:], pb[4][:, :]))
                    op = e.tensor_mul(z_sb[h][:, q0:q0 + 512], pb[2][:, :], rdb_sb[:])
                    s_v.inc(op, 1)

            if debug:
                @sec("gpsimd")
                def _(e, h=h):
                    e.wait_ge(s_v.h, s_v.n)
                    s_gld.inc(e.dma_start(out=dbg["z"][h], in_=z_sb[h][:]), 16)

            barrier()

        p2q.close()
        po_sb = [sbuf(p2, f"po{i}", [128, D], F32) for i in range(2)]
        for tt in range(64):
            @sec("tensor")
            def _(e, tt=tt):
                if tt >= 2:
                    s_v.wait(e, s_v.n - 1)
                for mc in range(4):
                    bank = (tt % 2) * 4 + mc
                    for h in range(NHL):
                        op = e.matmul(pb[bank][:, :], z_sb[h][:, tt * 128:(tt + 1) * 128],
                                      wo_sb[:, h * D + mc * 512:h * D + (mc + 1) * 512],
                                      start=(h == 0), stop=(h == 1))
                s_mm.inc(op, 1)

            @sec("vector")
            def _(e, tt=tt):
                s_mm.wait(e)
                if tt >= 2:
                    s_ld.wait(e)
                for mc in range(4):
                    op = e.tensor_copy(po_sb[tt % 2][:, mc * 512:(mc + 1) * 512],
                                       pb[(tt % 2) * 4 + mc][:, :])
                s_v.inc(op, 1)

            @sec("sync")
            def _(e, tt=tt):
                s_v.wait(e)
                s_ld.inc(e.dma_start(out=part_src[tt * 128:(tt + 1) * 128, :],
                                     in_=po_sb[tt % 2][:]), 16)

        @sec("gpsimd")
        def _(e):
            e.wait_ge(s_ld.h, s_ld.n)
            op = e.collective_compute(
                "ReduceScatter", ALU.add, replica_groups=[list(range(NCORES))],
                ins=[part_src[:, :]], outs=[rs1_dst[:, :]])
            s_cc.inc(op, 1)

        if debug:
            @sec("gpsimd")
            def _(e):
                s_cc.wait(e)
                s_gld.inc(e.dma_start(out=dbg["rs1"][:, :], in_=rs1_dst[:, :]), 16)

        p2.close()
        barrier()

        p5 = ExitStack()
        xt2 = [sbuf(p5, f"xt2_{i}", [128, D], F32) for i in range(2)]
        at2 = [sbuf(p5, f"at2_{i}", [128, D], F32) for i in range(2)]
        r2t = [sbuf(p5, f"r2t{i}", [128, D], F32) for i in range(2)]
        xn2f = [sbuf(p5, f"xn2f{i}", [128, D], F32) for i in range(2)]
        xqf = sbuf(p5, "xqf", [128, D], F32)
        xqb = [sbuf(p5, f"xqb{i}", [128, D], BF16) for i in range(2)]
        stage2 = [sbuf(p5, f"stage2_{i}", [128, D], BF16) for i in range(2)]
        xn2fm = sbuf(p5, "xn2fm", [128, 16 * 128], F32)
        mv2 = [sbuf(p5, f"mv2{i}", [128, 2], F32) for i in range(2)]
        ve2 = [sbuf(p5, f"ve2{i}", [128, 4], F32) for i in range(2)]
        qs2 = sbuf(p5, "qs2", [128, 8], F32)
        lg_sb = sbuf(p5, "lg", [128, 8], F32)
        mx8 = sbuf(p5, "mx8", [128, 8], F32)
        mi8 = sbuf(p5, "mi8", [128, 8], U32)
        idxf = sbuf(p5, "idxf", [128, 1], F32)

        for t in range(8):
            @sec("sync")
            def _(e, t=t):
                s_cc.wait(e)
                if t >= 2:
                    e.wait_ge(s_v.h, s_v.n)
                s_ld.inc(e.dma_start(out=xt2[t % 2][:],
                                     in_=x_in[t * 128:(t + 1) * 128, :]), 16)
                s_ld.inc(e.dma_start(out=at2[t % 2][:],
                                     in_=rs1_dst[t * 128:(t + 1) * 128, :]), 16)

            @sec("vector")
            def _(e, t=t):
                s_ld.wait(e)
                if t >= 1:
                    s_sld.wait(e)
                    s_mm.wait(e)
                vf(e, e.tensor_add(r2t[t % 2][:], xt2[t % 2][:], at2[t % 2][:]))
                op = e.tensor_add(r2t[t % 2][:], r2t[t % 2][:], bo_bc[:])
                s_v.inc(op, 1)

            @sec("scalar")
            def _(e, t=t):
                s_v.wait(e)
                op = e.dma_start(out=r2_buf[t * 128:(t + 1) * 128, :], in_=r2t[t % 2][:])
                s_sld.inc(op, 16)

            @sec("vector")
            def _(e, t=t):
                vf(e, e.reduce_sum(ve2[t % 2][:, 3:4], r2t[t % 2][:], axis=AX.X))
                e.tensor_scalar(mv2[t % 2][:, 0:1], ve2[t % 2][:, 3:4], 1.0 / D, None, ALU.mult)
                op = e.tensor_scalar(mv2[t % 2][:, 1:2], ve2[t % 2][:, 3:4], -1.0 / D, None, ALU.mult)
                s_v.inc(op, 1)

            @sec("scalar")
            def _(e, t=t):
                s_v.wait(e)
                op = e.activation(xn2f[t % 2][:], r2t[t % 2][:], AF.Square,
                                  bias=mv2[t % 2][:, 1:2], accum_out=ve2[t % 2][:, 0:1])
                s_sc.inc(op, 1)

            @sec("vector")
            def _(e, t=t):
                s_sc.wait(e)
                op = e.tensor_scalar(ve2[t % 2][:, 0:1], ve2[t % 2][:, 0:1], 1.0 / D, EPS,
                                     ALU.mult, ALU.add)
                s_v.inc(op, 1)

            @sec("scalar")
            def _(e, t=t):
                s_v.wait(e)
                op = e.activation(ve2[t % 2][:, 1:2], ve2[t % 2][:, 0:1], AF.Sqrt)
                s_sc.inc(op, 1)

            @sec("vector")
            def _(e, t=t):
                s_sc.wait(e)
                vf(e, e.reciprocal(ve2[t % 2][:, 2:3], ve2[t % 2][:, 1:2]))
                vf(e, e.tensor_scalar(xn2f[t % 2][:], r2t[t % 2][:], mv2[t % 2][:, 0:1],
                                      ve2[t % 2][:, 2:3], ALU.subtract, ALU.mult))
                vf(e, e.reduce_max(qs2[:, 0:1], xn2f[t % 2][:], axis=AX.X,
                                   apply_absolute_value=True))
                vf(e, e.tensor_scalar(qs2[:, 1:2], qs2[:, 0:1], EPS, None, ALU.max))
                vf(e, e.reciprocal(qs2[:, 2:3], qs2[:, 1:2]))
                e.tensor_scalar(qs2[:, 3:4], qs2[:, 2:3], 127.0, None, ALU.mult)
                vf(e, e.tensor_scalar(qs2[:, 4:5], qs2[:, 1:2], 1.0 / 127.0, None, ALU.mult))
                vf(e, e.tensor_scalar(xqf[:], xn2f[t % 2][:], qs2[:, 3:4], None, ALU.mult))
                vf(e, e.tensor_scalar(xqf[:], xqf[:], MAGIC, -MAGIC, ALU.add, ALU.add))
                op = e.tensor_scalar(xqb[t % 2][:], xqf[:], 127.0, -128.0, ALU.min, ALU.max)
                s_v.inc(op, 1)

            @sec("scalar")
            def _(e, t=t):
                s_v.wait(e)
                op = e.dma_start(out=meta_src[0, t * 128:(t + 1) * 128].unsqueeze(1), in_=qs2[:, 4:5])
                s_sld.inc(op, 16)

            @sec("tensor")
            def _(e, t=t):
                s_v.wait(e)
                bank = 4 + (t % 2) * 2
                for cc in range(16):
                    op = e.transpose(
                        pbb(bank + cc // 8)[:, (cc % 8) * 128:(cc % 8 + 1) * 128],
                        xqb[t % 2][:, cc * 128:(cc + 1) * 128], identb[:])
                s_mm.inc(op, 1)

            @sec("vector")
            def _(e, t=t):
                s_mm.wait(e)
                if t >= 2:
                    s_ld.wait(e)
                bank = 4 + (t % 2) * 2
                e.tensor_copy(stage2[t % 2][:, 0:1024], pbb(bank)[:, 0:1024])
                op = e.tensor_copy(stage2[t % 2][:, 1024:2048], pbb(bank + 1)[:, 0:1024])
                s_v.inc(op, 1)

            @sec("sync")
            def _(e, t=t):
                s_v.wait(e)
                op = e.dma_start(
                    out=cmaj(ag2_src[:, t * 128:(t + 1) * 128]),
                    in_=s3(stage2[t % 2][:], 16))
                s_ld.inc(op, 16)

            @sec("tensor")
            def _(e, t=t):
                for cc4 in range(4):
                    for ccl in range(4):
                        cc = cc4 * 4 + ccl
                        op = e.transpose(pb[cc4][:, ccl * 128:(ccl + 1) * 128],
                                         xn2f[t % 2][:, cc * 128:(cc + 1) * 128], identf[:])
                s_mm.inc(op, 1)

            @sec("vector")
            def _(e, t=t):
                s_mm.wait(e)
                for cc4 in range(4):
                    op = e.tensor_copy(xn2fm[:, cc4 * 512:(cc4 + 1) * 512], pb[cc4][:, :])
                s_v.inc(op, 1)

            @sec("tensor")
            def _(e, t=t):
                s_v.wait(e)
                for cc in range(16):
                    op = e.matmul(pb[6][:, 0:8], xn2fm[:, cc * 128:(cc + 1) * 128],
                                  rwt_sb[:, cc * 8:(cc + 1) * 8],
                                  start=(cc == 0), stop=(cc == 15))
                s_mm.inc(op, 1)

            @sec("vector")
            def _(e, t=t):
                s_mm.wait(e)
                vf(e, e.tensor_copy(lg_sb[:], pb[6][:, 0:8]))
                vf(e, e.max(mx8[:], lg_sb[:]))
                vf(e, e.max_index(mi8[:], mx8[:], lg_sb[:]))
                op = e.tensor_copy(idxf[:], mi8[:, 0:1])
                s_v.inc(op, 1)

            @sec("scalar")
            def _(e, t=t):
                s_v.wait(e)
                op = e.dma_start(out=meta_src[1, t * 128:(t + 1) * 128].unsqueeze(1), in_=idxf[:])
                s_sld.inc(op, 16)

        @sec("gpsimd")
        def _(e):
            e.wait_ge(s_ld.h, s_ld.n)
            e.wait_ge(s_sld.h, s_sld.n)
            op = e.collective_compute(
                "AllGather", ALU.bypass, replica_groups=[list(range(NCORES))],
                ins=[meta_src[:, :]], outs=[meta_dst[:, :, :]])
            s_cc.inc(op, 1)
            op = e.collective_compute(
                "AllGather", ALU.bypass, replica_groups=[list(range(NCORES))],
                ins=[ag2_src[:, :]], outs=[ag2_dst[:, :, :]])
            s_cc.inc(op, 1)

        if debug:
            @sec("gpsimd")
            def _(e):
                s_cc.wait(e)
                s_gld.inc(e.dma_start(out=dbg["ag2"][:, :, :], in_=ag2_dst[:, :, :]), 16)
                s_gld.inc(e.dma_start(out=dbg["meta"][:, :, :], in_=meta_dst[:, :, :]), 16)
                s_gld.inc(e.dma_start(out=dbg["r2"][:, :], in_=r2_buf[:, :]), 16)

        p5.close()
        barrier()

        pf = ExitStack()
        act_sg = sbuf(pf, "act_sg", [128, 16 * 1024], BF16)
        h_sg = sbuf(pf, "h_sg", [128, 16 * 1024], F32)
        wblk = [sbuf(pf, f"wblk{i}", [128, 16 * 512], BF16) for i in range(2)]
        sxg = sbuf(pf, "sxg", [128, 1024], F32)
        sxu = sbuf(pf, "sxu", [128, 1024], F32)
        shb = sbuf(pf, "shb", [128, 1024], F32)
        hmax = sbuf(pf, "hmax", [128, 1024], F32)
        srow = sbuf(pf, "srow", [1, 1024], F32)
        irow = sbuf(pf, "irow", [1, 1024], F32)
        srow2 = sbuf(pf, "srow2", [1, 1024], F32)
        hmcl = sbuf(pf, "hmcl", [128, 8], F32)
        invsh = sbuf(pf, "invsh", [128, 8], F32)
        comb = sbuf(pf, "comb", [128, 8], F32)
        idxt = sbuf(pf, "idxt", [128, 8], F32)
        gtile = [sbuf(pf, f"gtile{i}", [128, 512], F32) for i in range(2)]
        utile = [sbuf(pf, f"utile{i}", [128, 512], F32) for i in range(2)]
        hmax2 = sbuf(pf, "hmax2", [128, 1024], F32)
        y_sb = [sbuf(pf, f"y{i}", [128, D], F32) for i in range(1)]

        wdv = h_sg[:].bitcast(BF16)

        for sg in range(8):
            @sec("sync")
            def _(e, sg=sg):
                s_cc.wait(e)
                if sg > 0:
                    e.wait_ge(s_mm.h, s_mm.n)
                op = e.dma_start(out=s3(act_sg[:], 16),
                                 in_=cmaj(ag2_dst[sg]))
                s_ld.inc(op, 16)
                op = e.dma_start(out=srow[:], in_=meta_dst[sg, 0].unsqueeze(0))
                s_ld.inc(op, 16)
                op = e.dma_start(out=irow[:], in_=meta_dst[sg, 1].unsqueeze(0))
                s_ld.inc(op, 16)

            @sec("tensor")
            def _(e, sg=sg):
                s_ld.wait(e)
                if sg > 0:
                    e.wait_ge(s_v.h, s_v.n)
                for j in range(2):
                    op = e.matmul(pb[6][:, :], ones_mf[:], srow[0:1, j * 512:(j + 1) * 512],
                                  start=True, stop=True)
                    s_mm.inc(op, 1)
                    e.wait_ge(s_v.h, s_v.n + j + 1)

            @sec("vector")
            def _(e, sg=sg):
                base = s_mm.n - 2
                e.memset(hmax[:], 0.0)
                e.memset(hmax2[:], 0.0)
                for j in range(2):
                    s_mm.wait(e, base + j + 1)
                    e.tensor_scalar(sxg[:, j * 512:(j + 1) * 512], pb[6][:, :],
                                    winv[:, 0:1], None, ALU.mult)
                    op = e.tensor_scalar(sxu[:, j * 512:(j + 1) * 512], pb[6][:, :],
                                         winv[:, 1:2], None, ALU.mult)
                    s_v.inc(op, 1)

            for ds in range(4):
                @sec("scalar")
                def _(e, sg=sg, ds=ds):
                    if not (sg == 0 and ds == 0):
                        e.wait_ge(s_mm.h, s_mm.n)
                    op = e.dma_start(out=s3(wblk[0][:], 16),
                                     in_=cmaj(wqt["g"][ds]))
                    s_sld.inc(op, 16)
                    op = e.dma_start(out=s3(wblk[1][:], 16),
                                     in_=cmaj(wqt["u"][ds]))
                    s_sld.inc(op, 16)

                for j in range(2):
                    @sec("tensor")
                    def _(e, ds=ds, j=j):
                        s_sld.wait(e)
                        s_v.wait(e)
                        for hc in range(16):
                            st, sp = hc == 0, hc == 15
                            for dc in range(4):
                                e.matmul(pb[dc][:, :],
                                         wblk[0][:, hc * 512 + dc * 128:hc * 512 + (dc + 1) * 128],
                                         act_sg[:, hc * 1024 + j * 512:hc * 1024 + (j + 1) * 512],
                                         start=st, stop=sp)
                                op = e.matmul(pb[4 + dc][:, :],
                                              wblk[1][:, hc * 512 + dc * 128:hc * 512 + (dc + 1) * 128],
                                              act_sg[:, hc * 1024 + j * 512:hc * 1024 + (j + 1) * 512],
                                              start=st, stop=sp)
                        s_mm.inc(op, 1)

                    @sec("vector")
                    def _(e, ds=ds, j=j):
                        s_mm.wait(e)
                        hm = (hmax, hmax2)
                        for dp in range(2):
                            dcs = (2 * dp, 2 * dp + 1)
                            for i, dc in enumerate(dcs):
                                e.tensor_tensor(gtile[i][:], pb[dc][:, :],
                                                sxg[:, j * 512:(j + 1) * 512], ALU.mult)
                            for i, dc in enumerate(dcs):
                                e.tensor_tensor(utile[i][:], pb[4 + dc][:, :],
                                                sxu[:, j * 512:(j + 1) * 512], ALU.mult)
                            for i, dc in enumerate(dcs):
                                e.scalar_tensor_tensor(gtile[i][:], gtile[i][:], 0.0,
                                                       gtile[i][:], ALU.max, ALU.mult)
                            for i, dc in enumerate(dcs):
                                col = (ds * 4 + dc) * 1024 + j * 512
                                e.tensor_tensor(h_sg[:, col:col + 512], gtile[i][:],
                                                utile[i][:], ALU.mult)
                            for i, dc in enumerate(dcs):
                                col = (ds * 4 + dc) * 1024 + j * 512
                                e.tensor_mul(gtile[i][:], h_sg[:, col:col + 512],
                                             h_sg[:, col:col + 512])
                            for i, dc in enumerate(dcs):
                                op = e.tensor_max(hm[i][:, j * 512:(j + 1) * 512],
                                                  hm[i][:, j * 512:(j + 1) * 512],
                                                  gtile[i][:])
                        s_v.inc(op, 1)

            @sec("vector")
            def _(e):
                op = e.tensor_max(hmax[:], hmax[:], hmax2[:])
                vf(e, op)

            for w in range(8):
                @sec("tensor")
                def _(e, w=w):
                    s_v.wait(e)
                    op = e.transpose(pb[5][:, (w % 4) * 128:(w % 4 + 1) * 128],
                                     hmax[:, w * 128:(w + 1) * 128], identf[:])
                    s_mm.inc(op, 1)

                @sec("vector")
                def _(e, w=w):
                    s_mm.wait(e)
                    op = e.reduce_max(hmcl[:, w:w + 1],
                                      pb[5][:, (w % 4) * 128:(w % 4 + 1) * 128], axis=AX.X)
                    s_v.inc(op, 1)

            @sec("scalar")
            def _(e):
                s_v.wait(e)
                op = e.activation(hmcl[:], hmcl[:], AF.Sqrt)
                s_sc.inc(op, 1)

            @sec("vector")
            def _(e):
                s_sc.wait(e)
                vf(e, e.tensor_scalar(hmcl[:], hmcl[:], EPS, None, ALU.max))
                op = e.tensor_scalar(invsh[:], hmcl[:], 1.0 / 127.0, None, ALU.mult)
                s_v.inc(op, 1)

            for w in range(8):
                @sec("tensor")
                def _(e, w=w):
                    s_v.wait(e)
                    op = e.transpose(pb[6 + w // 4][0:1, (w % 4) * 128:(w % 4 + 1) * 128],
                                     hmcl[:, w:w + 1], identf[:])
                    s_mm.inc(op, 1)

            @sec("vector")
            def _(e):
                s_mm.wait(e)
                e.tensor_copy(srow2[0:1, 0:512], pb[6][0:1, 0:512])
                vf(e, e.tensor_copy(srow2[0:1, 512:1024], pb[7][0:1, 0:512]))
                vf(e, e.reciprocal(srow2[:], srow2[:]))
                op = e.tensor_scalar(srow2[:], srow2[:], 127.0, None, ALU.mult)
                s_v.inc(op, 1)

            @sec("tensor")
            def _(e):
                s_v.wait(e)
                for j in range(2):
                    op = e.matmul(pb[7][:, :], ones_mf[:], srow2[0:1, j * 512:(j + 1) * 512],
                                  start=True, stop=True)
                    s_mm.inc(op, 1)
                    e.wait_ge(s_v.h, s_v.n + j + 1)

            @sec("vector")
            def _(e):
                base = s_mm.n - 2
                for j in range(2):
                    s_mm.wait(e, base + j + 1)
                    op = e.tensor_copy(shb[:, j * 512:(j + 1) * 512], pb[7][:, :])
                    s_v.inc(op, 1)

            @sec("vector")
            def _(e):
                for ch in range(16):
                    for j in range(2):
                        col = ch * 1024 + j * 512
                        e.tensor_tensor(gtile[j][:], h_sg[:, col:col + 512],
                                        shb[:, j * 512:(j + 1) * 512], ALU.mult)
                    for j in range(2):
                        e.tensor_scalar(gtile[j][:], gtile[j][:], MAGIC, -MAGIC,
                                        ALU.add, ALU.add)
                    for j in range(2):
                        col = ch * 1024 + j * 512
                        op = e.tensor_scalar(act_sg[:, col:col + 512], gtile[j][:],
                                             127.0, -128.0, ALU.min, ALU.max)
                s_v.inc(op, 1)

            for w in range(8):
                @sec("tensor")
                def _(e, w=w):
                    if w == 0:
                        s_ld.wait(e)
                    op = e.transpose(pb[4][:, w * 4:w * 4 + 1],
                                     irow[0:1, w * 128:(w + 1) * 128], identf[0:1, 0:1])
                    s_mm.inc(op, 1)

            @sec("vector")
            def _(e):
                s_mm.wait(e)
                for w in range(8):
                    e.tensor_copy(idxt[:, w:w + 1], pb[4][:, w * 4:w * 4 + 1])
                for w in range(8):
                    e.tensor_tensor(comb[:, w:w + 1], idxt[:, w:w + 1], eidb[:],
                                    ALU.is_equal)
                for w in range(8):
                    e.tensor_tensor(comb[:, w:w + 1], comb[:, w:w + 1],
                                    invsh[:, w:w + 1], ALU.mult)
                for w in range(8):
                    op = e.tensor_scalar(comb[:, w:w + 1], comb[:, w:w + 1],
                                         winv[:, 2:3], None, ALU.mult)
                s_v.inc(op, 1)

            @sec("scalar")
            def _(e, sg=sg):
                s_v.wait(e)
                for hfg in range(4):
                    op = e.dma_start(out=s3(wdv[:, hfg * 8192:(hfg + 1) * 8192], 16),
                                     in_=cmaj(wqt["d"][hfg]))
                    s_sld.inc(op, 16)

            for tt in range(8):
                @sec("tensor")
                def _(e, sg=sg, tt=tt):
                    s_sld.wait(e)
                    s_v.wait(e)
                    if tt >= 2:
                        s_v.wait(e)
                    for hfg in range(4):
                        bank = (tt % 2) * 4 + hfg
                        for dc in range(16):
                            op = e.matmul(pb[bank][:, :],
                                          act_sg[:, dc * 1024 + tt * 128:dc * 1024 + (tt + 1) * 128],
                                          wdv[:, hfg * 8192 + dc * 512:hfg * 8192 + (dc + 1) * 512],
                                          start=(dc == 0), stop=(dc == 15))
                    s_mm.inc(op, 1)

                @sec("vector")
                def _(e, tt=tt):
                    s_mm.wait(e)
                    if tt >= 2:
                        s_ld.wait(e)
                    for hfg in range(4):
                        op = e.tensor_scalar(y_sb[0][:, hfg * 512:(hfg + 1) * 512],
                                             pb[(tt % 2) * 4 + hfg][:, :],
                                             comb[:, tt:tt + 1], None, ALU.mult)
                    s_v.inc(op, 1)

                @sec("sync")
                def _(e, sg=sg, tt=tt):
                    s_v.wait(e)
                    s_ld.inc(e.dma_start(
                        out=y_src[sg * 1024 + tt * 128:sg * 1024 + (tt + 1) * 128, :],
                        in_=y_sb[0][:]), 16)

        @sec("gpsimd")
        def _(e):
            e.wait_ge(s_ld.h, s_ld.n)
            op = e.collective_compute(
                "ReduceScatter", ALU.add, replica_groups=[list(range(NCORES))],
                ins=[y_src[:, :]], outs=[rs2_dst[:, :]])
            s_cc.inc(op, 1)

        if debug:
            @sec("gpsimd")
            def _(e):
                s_cc.wait(e)
                s_gld.inc(e.dma_start(out=dbg["ysrc"][:, :], in_=y_src[0:TC, :]), 16)
                s_gld.inc(e.dma_start(out=dbg["rs2"][:, :], in_=rs2_dst[:, :]), 16)

        pf.close()
        barrier()

        p8 = ExitStack()
        f1 = [sbuf(p8, f"f1_{i}", [128, D], F32) for i in range(2)]
        f2 = [sbuf(p8, f"f2_{i}", [128, D], F32) for i in range(2)]
        fo = [sbuf(p8, f"fo_{i}", [128, D], F32) for i in range(2)]
        for t in range(8):
            @sec("sync")
            def _(e, t=t):
                s_cc.wait(e)
                if t >= 2:
                    e.wait_ge(s_v.h, s_v.n)
                s_ld.inc(e.dma_start(out=f1[t % 2][:],
                                     in_=r2_buf[t * 128:(t + 1) * 128, :]), 16)
                s_ld.inc(e.dma_start(out=f2[t % 2][:],
                                     in_=rs2_dst[t * 128:(t + 1) * 128, :]), 16)

            @sec("vector")
            def _(e, t=t):
                s_ld.wait(e)
                if t >= 2:
                    s_sld.wait(e)
                op = e.tensor_add(fo[t % 2][:], f1[t % 2][:], f2[t % 2][:])
                s_v.inc(op, 1)

            @sec("scalar")
            def _(e, t=t):
                s_v.wait(e)
                op = e.dma_start(out=out[t * 128:(t + 1) * 128, :], in_=fo[t % 2][:])
                s_sld.inc(op, 16)

        @sec("scalar")
        def _(e):
            e.wait_ge(s_sld.h, s_sld.n)
        @sec("sync")
        def _(e):
            e.wait_ge(s_sld.h, s_sld.n)

        p8.close()

    return nc, dbg



def _host_inputs(inputs):
    x = np.ascontiguousarray(np.asarray(inputs["L0_attn_normalized_resid_pre"], np.float32)[0])
    WQ = np.asarray(inputs["L0_attn_W_Q"], np.float32)
    WK = np.asarray(inputs["L0_attn_W_K"], np.float32)
    WV = np.asarray(inputs["L0_attn_W_V"], np.float32)
    WO = np.asarray(inputs["L0_attn_W_O"], np.float32)
    bQ = np.asarray(inputs["L0_attn_b_Q"], np.float32)
    bK = np.asarray(inputs["L0_attn_b_K"], np.float32)
    bV = np.asarray(inputs["L0_attn_b_V"], np.float32)
    bO = np.asarray(inputs["L0_attn_b_O"], np.float32)
    RW = np.asarray(inputs["L0_ffn_router_w"], np.float32)
    WG = np.asarray(inputs["L0_ffn_wg"], np.float32)
    WU = np.asarray(inputs["L0_ffn_wu"], np.float32)
    WD = np.asarray(inputs["L0_ffn_wd"], np.float32)

    perm = np.concatenate([np.arange(0, 128, 2), np.arange(1, 128, 2)])
    inv_freq = 1.0 / (10000.0 ** (np.arange(0, 128, 2, dtype=np.float32) / 128.0))
    m = np.arange(T, dtype=np.float32)
    freqs = np.outer(m, inv_freq)
    cos64 = np.cos(freqs).T.astype(np.float32)
    sin64 = np.sin(freqs).T.astype(np.float32)
    cos_t = np.ascontiguousarray(np.concatenate([cos64, cos64], axis=0))
    sin_t = np.ascontiguousarray(np.concatenate([-sin64, sin64], axis=0))
    swapi = np.ascontiguousarray(np.roll(np.eye(128, dtype=ml_dtypes.bfloat16), 64, axis=0))

    masks = np.zeros((4, 128, 512), dtype=ml_dtypes.bfloat16)
    for o in range(4):
        kk = np.arange(128)[:, None] + 128 * o
        qq = np.arange(512)[None, :]
        masks[o] = (qq >= kk).astype(ml_dtypes.bfloat16)

    common = {
        "rw": RW, "cos": cos_t, "sin": sin_t, "masks": masks,
        "identb": np.eye(128, dtype=ml_dtypes.bfloat16),
        "identf": np.eye(128, dtype=np.float32),
        "ones_k": np.ones((128, 1), ml_dtypes.bfloat16),
        "ones_m": np.ones((1, 128), ml_dtypes.bfloat16),
        "ones_mf": np.ones((1, 128), np.float32),
        "ones_kf": np.ones((128, 1), np.float32),
        "swapi": swapi,
        "bo": np.ascontiguousarray(bO[None, :]),
    }
    in_maps = []
    for c in range(NCORES):
        hs = slice(2 * c, 2 * c + 2)
        d = dict(common)
        d.update({
            "x": np.ascontiguousarray(x[c * TC:(c + 1) * TC]),
            "wq": np.ascontiguousarray(WQ[hs][:, :, perm]),
            "wk": np.ascontiguousarray(WK[hs][:, :, perm]),
            "wv": np.ascontiguousarray(WV[hs]),
            "wo": np.ascontiguousarray(WO[hs]),
            "bq": np.ascontiguousarray(bQ[hs][:, perm]),
            "bk": np.ascontiguousarray(bK[hs][:, perm]),
            "bv": np.ascontiguousarray(bV[hs]),
            "wg": np.ascontiguousarray(WG[c]),
            "wu": np.ascontiguousarray(WU[c]),
            "wd": np.ascontiguousarray(WD[c]),
            "eid": np.full((1, 1), float(c), np.float32),
        })
        in_maps.append(d)
    return in_maps


def kernel(**inputs):
    in_maps = _host_inputs(inputs)
    nc, _ = build_program(debug=DEBUG)
    res = run_bass_kernel_spmd(nc, in_maps, core_ids=list(range(NCORES)))
    outv = np.concatenate([np.asarray(r["out"]) for r in res.results], axis=0)
    return outv[None, :, :].astype(np.float32)


# version 04
# speedup vs baseline: 1.2477x; 1.2477x over previous
import numpy as np
import ml_dtypes
from contextlib import ExitStack
import concourse.bass as bass
import concourse.mybir as mybir
from concourse.bass_utils import run_bass_kernel_spmd

F32 = mybir.dt.float32
BF16 = mybir.dt.bfloat16
U32 = mybir.dt.uint32
AF = mybir.ActivationFunctionType
ALU = mybir.AluOpType
AX = mybir.AxisListType

MAGIC = 1.5 * 2.0 ** 23
NCORES = 8
T = 8192
TC = T // NCORES
D = 2048
NHL = 2
EPS = 1e-5
ISQ = float(1.0 / np.sqrt(128.0))

DEBUG = False


class S:
    def __init__(self, stack, nc, name):
        self.h = stack.enter_context(nc.semaphore(name))
        self.n = 0

    def inc(self, instr, k=1):
        instr.then_inc(self.h, k)
        self.n += k
        return self.n

    def wait(self, eng, val=None):
        eng.wait_ge(self.h, self.n if val is None else val)


def build_program(debug=False):
    nc = bass.Bass("TRN2", target_bir_lowering=False, debug=False)

    def din(name, shape, dt=F32):
        return nc.dram_tensor(name, shape, dt, kind="ExternalInput").ap()

    x_in = din("x", [TC, D])
    wq_in = din("wq", [NHL, D, 128])
    wk_in = din("wk", [NHL, D, 128])
    wv_in = din("wv", [NHL, D, 128])
    wo_in = din("wo", [NHL, 128, D])
    bq_in = din("bq", [NHL, 128])
    bk_in = din("bk", [NHL, 128])
    bv_in = din("bv", [NHL, 128])
    bo_in = din("bo", [1, D])
    rw_in = din("rw", [NCORES, D])
    wg_in = din("wg", [D, D])
    wu_in = din("wu", [D, D])
    wd_in = din("wd", [D, D])
    cos_in = din("cos", [128, T])
    sin_in = din("sin", [128, T])
    masks_in = din("masks", [4, 128, 512], BF16)
    identb_in = din("identb", [128, 128], BF16)
    identf_in = din("identf", [128, 128], F32)
    ones_k_in = din("ones_k", [128, 1], BF16)
    ones_m_in = din("ones_m", [1, 128], BF16)
    ones_mf_in = din("ones_mf", [1, 128], F32)
    ones_kf_in = din("ones_kf", [128, 1], F32)
    eid_in = din("eid", [1, 1], F32)
    swapi_in = din("swapi", [128, 128], BF16)

    out = nc.dram_tensor("out", [TC, D], F32, kind="ExternalOutput").ap()

    def dint(name, shape, dt):
        return nc.dram_tensor(name, shape, dt, kind="Internal").ap()

    ag1_src = dint("ag1_src", [D, TC], BF16)
    ag1_dst = nc.dram_tensor("ag1_dst", [NCORES, D, TC], BF16, kind="Internal", addr_space="Shared").ap()
    part_src = dint("part_src", [T, D], BF16)
    rs1_dst = dint("rs1_dst", [TC, D], BF16)
    r2_buf = dint("r2_buf", [TC, D], F32)
    ag2_src = dint("ag2_src", [D, TC], BF16)
    ag2_dst = nc.dram_tensor("ag2_dst", [NCORES, D, TC], BF16, kind="Internal", addr_space="Shared").ap()
    meta_src = dint("meta_src", [2, TC], F32)
    meta_dst = dint("meta_dst", [NCORES, 2, TC], F32)
    wqt = {m: dint(f"wqt_{m}", [4, D, 512], BF16) for m in "gud"}
    y_src = dint("y_src", [T, D], BF16)
    rs2_dst = dint("rs2_dst", [TC, D], BF16)

    dbg = {}
    if debug:
        def dout(name, shape, dt):
            dbg[name] = nc.dram_tensor("dbg_" + name, shape, dt, kind="ExternalOutput").ap()
        dout("ag1", [NCORES, D, TC], BF16)
        dout("ag1s", [D, TC], BF16)
        dout("q", [NHL, 128, T], BF16)
        dout("k", [NHL, 128, T], BF16)
        dout("v", [NHL, 128, T], BF16)
        dout("negb", [NHL, 128], F32)
        dout("z", [NHL, 128, T], BF16)
        dout("rs1", [TC, D], BF16)
        dout("r2", [TC, D], F32)
        dout("ag2", [NCORES, D, TC], BF16)
        dout("meta", [NCORES, 2, TC], F32)
        dout("wqtg", [4, D, 512], BF16)
        dout("wqtd", [4, D, 512], BF16)
        dout("winv", [128, 4], F32)
        dout("ysrc", [TC, D], BF16)
        dout("rs2", [TC, D], BF16)

    stack = ExitStack()
    with stack:
        def sbuf(st, name, shape, dt):
            return st.enter_context(nc.sbuf_tensor(name, shape, dt))

        def sem(name):
            return S(stack, nc, name)

        pb = [stack.enter_context(nc.psum_tensor(f"pb{i}", [128, 512], F32))
              for i in range(8)]

        def pbb(i):
            return pb[i][:].bitcast(BF16)

        def cmaj(ap2d, p=128):
            c = ap2d.shape[0] // p
            return ap2d.rearrange("(c p) q -> c p q", p=p).transpose([1, 0, 2])

        def s3(ap2d, c):
            return ap2d.rearrange("p (c q) -> p c q", c=c)

        identb = sbuf(stack, "identb_s", [128, 128], BF16)
        identf = sbuf(stack, "identf_s", [128, 128], F32)
        ones_k = sbuf(stack, "ones_k_s", [128, 1], BF16)
        ones_m = sbuf(stack, "ones_m_s", [1, 128], BF16)
        ones_mf = sbuf(stack, "ones_mf_s", [1, 128], F32)
        ones_kf = sbuf(stack, "ones_kf_s", [128, 1], F32)
        mask_sb = sbuf(stack, "mask_s", [128, 4 * 512], BF16)
        bqkv = sbuf(stack, "bqkv", [128, 6], F32)
        bvrow = sbuf(stack, "bvrow", [1, 2 * 128], F32)
        bo_bc = sbuf(stack, "bo_bc", [128, D], F32)
        rwt_sb = sbuf(stack, "rwt", [128, 16 * 8], F32)
        wo_sb = sbuf(stack, "wo_s", [128, NHL * D], BF16)
        negb_sb = sbuf(stack, "negb_s", [128, NHL], F32)
        scal = sbuf(stack, "scal", [1, 16], F32)
        winv = sbuf(stack, "winv_s", [128, 4], F32)
        eid_sb = sbuf(stack, "eid_s", [1, 1], F32)
        eidb = sbuf(stack, "eidb", [128, 1], F32)
        p0 = ExitStack()
        bo_row = sbuf(p0, "bo_row_s", [1, D], F32)
        rw_sb = sbuf(p0, "rw_s", [NCORES, D], F32)

        s_ld = sem("s_ld")
        s_gld = sem("s_gld")
        s_sld = sem("s_sld")
        s_mm = sem("s_mm")
        s_v = sem("s_v")
        s_sc = sem("s_sc")
        s_cc = sem("s_cc")
        s_bar = sem("s_bar")

        blk = stack.enter_context(nc.Block())

        def sec(engname):
            def deco(f):
                getattr(blk, engname)(f)
            return deco

        def vf(e, op):
            s_v.inc(op, 1)
            s_v.wait(e)

        ENGS = ("sync", "scalar", "vector", "tensor", "gpsimd")

        def barrier():
            for en in ENGS:
                @sec(en)
                def _(e):
                    e.sem_inc(s_bar.h, 1)
            s_bar.n += len(ENGS)
            tgt = s_bar.n
            for en in ENGS:
                @sec(en)
                def _(e):
                    e.wait_ge(s_bar.h, tgt)

        @sec("sync")
        def _(e):
            for dst, src in ((identb, identb_in), (identf, identf_in),
                             (ones_k, ones_k_in), (ones_m, ones_m_in),
                             (ones_mf, ones_mf_in), (ones_kf, ones_kf_in),
                             (rw_sb, rw_in), (bo_row, bo_in), (eid_sb, eid_in)):
                s_ld.inc(e.dma_start(out=dst[:], in_=src[:, :]), 16)
            s_ld.inc(e.dma_start(out=s3(mask_sb[:], 4),
                                 in_=masks_in.transpose([1, 0, 2])), 16)
            for h in range(NHL):
                for j, b in ((0, bq_in), (2, bk_in), (4, bv_in)):
                    s_ld.inc(e.dma_start(out=bqkv[:, j + h:j + h + 1],
                                         in_=b[h].unsqueeze(1)), 16)
                s_ld.inc(e.dma_start(out=bvrow[0:1, h * 128:(h + 1) * 128],
                                     in_=bv_in[h].unsqueeze(0)), 16)

        @sec("gpsimd")
        def _(e):
            for h in range(NHL):
                s_gld.inc(e.dma_start(out=wo_sb[:, h * D:(h + 1) * D], in_=wo_in[h]), 16)

        @sec("tensor")
        def _(e):
            s_ld.wait(e)
            for cc in range(16):
                op = e.transpose(pb[0][:, cc * 8:(cc + 1) * 8],
                                 rw_sb[:, cc * 128:(cc + 1) * 128], identf[0:8, 0:8])
            s_mm.inc(op, 1)
            op = e.matmul(pb[2][:, 0:1], ones_mf[:], eid_sb[:], start=True, stop=True)
            s_mm.inc(op, 1)
            vbase = s_v.n
            for mc in range(4):
                op = e.matmul(pb[1][:, :], ones_mf[:], bo_row[:, mc * 512:(mc + 1) * 512],
                              start=True, stop=True)
                s_mm.inc(op, 1)
                e.wait_ge(s_v.h, vbase + mc + 1)

        @sec("vector")
        def _(e):
            s_mm.wait(e, s_mm.n - 4)
            e.tensor_copy(rwt_sb[:], pb[0][:, 0:128])
            e.tensor_copy(eidb[:], pb[2][:, 0:1])
            for mc in range(4):
                s_mm.wait(e, s_mm.n - 3 + mc)
                op = e.tensor_copy(bo_bc[:, mc * 512:(mc + 1) * 512], pb[1][:, :])
                s_v.inc(op, 1)

        barrier()
        p0.close()

        p1 = ExitStack()
        xt = [sbuf(p1, f"xt{i}", [128, D], F32) for i in range(2)]
        xn1b = [sbuf(p1, f"xn1b{i}", [128, D], BF16) for i in range(2)]
        stage1 = [sbuf(p1, f"stage1_{i}", [128, D], BF16) for i in range(2)]
        mv_sb = [sbuf(p1, f"mv{i}", [128, 2], F32) for i in range(2)]
        ve_sb = [sbuf(p1, f"ve{i}", [128, 4], F32) for i in range(2)]

        for t in range(8):
            @sec("sync")
            def _(e, t=t):
                if t >= 2:
                    e.wait_ge(s_v.h, s_v.n)
                s_ld.inc(e.dma_start(out=xt[t % 2][:],
                                     in_=x_in[t * 128:(t + 1) * 128, :]), 16)

            @sec("vector")
            def _(e, t=t):
                s_ld.wait(e)
                if t >= 2:
                    s_mm.wait(e)
                vf(e, e.reduce_sum(ve_sb[t % 2][:, 3:4], xt[t % 2][:], axis=AX.X))
                e.tensor_scalar(mv_sb[t % 2][:, 0:1], ve_sb[t % 2][:, 3:4], 1.0 / D, None, ALU.mult)
                op = e.tensor_scalar(mv_sb[t % 2][:, 1:2], ve_sb[t % 2][:, 3:4], -1.0 / D, None, ALU.mult)
                s_v.inc(op, 1)

            @sec("scalar")
            def _(e, t=t):
                s_v.wait(e)
                op = e.activation(xn1b[t % 2][:], xt[t % 2][:], AF.Square,
                                  bias=mv_sb[t % 2][:, 1:2], accum_out=ve_sb[t % 2][:, 0:1])
                s_sc.inc(op, 1)

            @sec("vector")
            def _(e, t=t):
                s_sc.wait(e)
                op = e.tensor_scalar(ve_sb[t % 2][:, 0:1], ve_sb[t % 2][:, 0:1], 1.0 / D, EPS,
                                     ALU.mult, ALU.add)
                s_v.inc(op, 1)

            @sec("scalar")
            def _(e, t=t):
                s_v.wait(e)
                op = e.activation(ve_sb[t % 2][:, 1:2], ve_sb[t % 2][:, 0:1], AF.Sqrt)
                s_sc.inc(op, 1)

            @sec("vector")
            def _(e, t=t):
                s_sc.wait(e)
                vf(e, e.reciprocal(ve_sb[t % 2][:, 2:3], ve_sb[t % 2][:, 1:2]))
                op = e.tensor_scalar(xn1b[t % 2][:], xt[t % 2][:], mv_sb[t % 2][:, 0:1],
                                     ve_sb[t % 2][:, 2:3], ALU.subtract, ALU.mult)
                s_v.inc(op, 1)

            @sec("tensor")
            def _(e, t=t):
                s_v.wait(e)
                bank = (t % 2) * 2
                for cc in range(16):
                    op = e.transpose(
                        pbb(bank + cc // 8)[:, (cc % 8) * 128:(cc % 8 + 1) * 128],
                        xn1b[t % 2][:, cc * 128:(cc + 1) * 128], identb[:])
                s_mm.inc(op, 1)

            @sec("vector")
            def _(e, t=t):
                s_mm.wait(e)
                if t >= 2:
                    s_ld.wait(e)
                bank = (t % 2) * 2
                e.tensor_copy(stage1[t % 2][:, 0:1024], pbb(bank)[:, 0:1024])
                op = e.tensor_copy(stage1[t % 2][:, 1024:2048], pbb(bank + 1)[:, 0:1024])
                s_v.inc(op, 1)

            @sec("sync")
            def _(e, t=t):
                s_v.wait(e)
                op = e.dma_start(
                    out=cmaj(ag1_src[:, t * 128:(t + 1) * 128]),
                    in_=s3(stage1[t % 2][:], 16))
                s_ld.inc(op, 16)

        @sec("gpsimd")
        def _(e):
            e.wait_ge(s_ld.h, s_ld.n)
            op = e.collective_compute(
                "AllGather", ALU.bypass, replica_groups=[list(range(NCORES))],
                ins=[ag1_src[:, :]], outs=[ag1_dst[:, :, :]])
            s_cc.inc(op, 1)

        wst = ExitStack()
        wf = [sbuf(wst, f"wf{i}", [128, D], F32) for i in range(2)]
        wqrow = sbuf(wst, "wqrow", [128, D], BF16)
        wacc = sbuf(wst, "wacc", [128, 16], F32)
        waccs = sbuf(wst, "waccs", [128, 1], F32)
        wstage = sbuf(wst, "wstage", [128, 16 * 128], BF16)
        swb = sbuf(wst, "swb", [128, 1], F32)

        for mi, (mname, w_in) in enumerate((("g", wg_in), ("u", wu_in), ("d", wd_in))):
            @sec("gpsimd")
            def _(e, w_in=w_in):
                e.wait_ge(s_v.h, s_v.n)
                vb = s_v.n
                for rc in range(16):
                    if rc >= 2:
                        e.wait_ge(s_v.h, vb + rc - 1)
                    s_gld.inc(e.dma_start(out=wf[rc % 2][:],
                                         in_=w_in[rc * 128:(rc + 1) * 128, :]), 16)

            @sec("vector")
            def _(e):
                for rc in range(16):
                    s_gld.wait(e, s_gld.n - 16 * (15 - rc))
                    op = e.reduce_sum(wacc[:, rc:rc + 1], wf[rc % 2][:], axis=AX.X,
                                      apply_absolute_value=True)
                    if rc == 15:
                        vf(e, op)
                    else:
                        s_v.inc(op, 1)
                vf(e, e.reduce_sum(waccs[:], wacc[:], axis=AX.X))

            @sec("tensor")
            def _(e):
                s_v.wait(e)
                op = e.matmul(pb[6][0:1, 0:1], waccs[:], ones_kf[:], start=True, stop=True)
                s_mm.inc(op, 1)

            @sec("vector")
            def _(e):
                s_mm.wait(e)
                vf(e, e.tensor_scalar(scal[:, 0:1], pb[6][0:1, 0:1], 1.0 / (D * D), None, ALU.mult))
                vf(e, e.tensor_scalar(scal[:, 1:2], scal[:, 0:1], EPS, None, ALU.max))
                op = e.reciprocal(scal[:, 2:3], scal[:, 1:2])
                s_v.inc(op, 1)

            @sec("tensor")
            def _(e, mi=mi):
                s_v.wait(e)
                op = e.matmul(pb[7][:, 0:1], ones_mf[:], scal[:, 2:3], start=True, stop=True)
                s_mm.inc(op, 1)
                op = e.matmul(pb[7][:, 1:2], ones_mf[:], scal[:, 1:2], start=True, stop=True)
                s_mm.inc(op, 1)

            @sec("vector")
            def _(e, mi=mi):
                s_mm.wait(e)
                e.tensor_copy(swb[:], pb[7][:, 0:1])
                op = e.tensor_copy(winv[:, mi:mi + 1], pb[7][:, 1:2])
                s_v.inc(op, 1)

            @sec("gpsimd")
            def _(e, w_in=w_in):
                e.wait_ge(s_v.h, s_v.n)
                vb = s_v.n
                for rc in range(16):
                    if rc >= 2:
                        e.wait_ge(s_v.h, vb + 4 * (rc - 1))
                    s_gld.inc(e.dma_start(out=wf[rc % 2][:],
                                         in_=w_in[rc * 128:(rc + 1) * 128, :]), 16)

            for rc in range(16):
                @sec("vector")
                def _(e, rc=rc):
                    s_gld.wait(e, s_gld.n - 16 * (15 - rc))
                    if rc >= 1:
                        s_mm.wait(e)
                    vf(e, e.tensor_scalar(wf[rc % 2][:], wf[rc % 2][:], swb[:], None, ALU.mult))
                    vf(e, e.tensor_scalar(wf[rc % 2][:], wf[rc % 2][:], MAGIC, -MAGIC,
                                          ALU.add, ALU.add))
                    op = e.tensor_scalar(wqrow[:], wf[rc % 2][:], 1.0, -1.0,
                                         ALU.min, ALU.max)
                    s_v.inc(op, 1)

                @sec("tensor")
                def _(e, rc=rc):
                    s_v.wait(e)
                    base = 4 + (rc % 2) * 2
                    for cc in range(16):
                        op = e.transpose(
                            pbb(base + cc // 8)[:, (cc % 8) * 128:(cc % 8 + 1) * 128],
                            wqrow[:, cc * 128:(cc + 1) * 128], identb[:])
                    s_mm.inc(op, 1)

                @sec("vector")
                def _(e, rc=rc):
                    s_mm.wait(e)
                    if rc >= 1:
                        s_sld.wait(e)
                    base = 4 + (rc % 2) * 2
                    e.tensor_copy(wstage[:, 0:1024], pbb(base)[:, 0:1024])
                    op = e.tensor_copy(wstage[:, 1024:2048], pbb(base + 1)[:, 0:1024])
                    s_v.inc(op, 1)

                @sec("scalar")
                def _(e, rc=rc, wqt_m=wqt[mname]):
                    s_v.wait(e)
                    op = e.dma_start(
                        out=cmaj(wqt_m[rc // 4][:, (rc % 4) * 128:(rc % 4 + 1) * 128]),
                        in_=s3(wstage[:], 16))
                    s_sld.inc(op, 16)

        barrier()

        if debug:
            @sec("gpsimd")
            def _(e):
                s_cc.wait(e)
                s_gld.inc(e.dma_start(out=dbg["ag1"][:, :, :], in_=ag1_dst[:, :, :]), 16)
                s_gld.inc(e.dma_start(out=dbg["ag1s"][:, :], in_=ag1_src[:, :]), 16)
                s_gld.inc(e.dma_start(out=dbg["wqtg"][:, :, :], in_=wqt["g"][:, :, :]), 16)
                s_gld.inc(e.dma_start(out=dbg["wqtd"][:, :, :], in_=wqt["d"][:, :, :]), 16)
                s_gld.inc(e.dma_start(out=dbg["winv"][:, :], in_=winv[:]), 16)

        wst.close()
        p1.close()

        p2 = ExitStack()
        q_sb = sbuf(p2, "q_sb", [128, T], BF16)
        k_sb = sbuf(p2, "k_sb", [128, T], BF16)
        v_sb = sbuf(p2, "v_sb", [128, T], BF16)
        z_sb = [sbuf(p2, f"z{h}", [128, T], BF16) for h in range(NHL)]
        bvb = sbuf(p2, "bvb", [128, 128], F32)
        p_sb = [sbuf(p2, f"p{i}", [128, 512], BF16) for i in range(3)]
        rd_sb = sbuf(p2, "rd", [1, 512], F32)
        rdb_sb = sbuf(p2, "rdb", [128, 512], F32)
        p2q = ExitStack()
        cos_sb = sbuf(p2q, "cos_s", [128, T // 2], F32)
        sin_sb = sbuf(p2q, "sin_s", [128, T // 2], F32)
        swapi = sbuf(p2q, "swapi_s", [128, 128], BF16)
        tq_b = sbuf(p2q, "tq_b", [128, 512], BF16)
        wqkv_h = sbuf(p2q, "wqkv_h", [128, 3 * D], BF16)
        xn_sb = [sbuf(p2q, f"xn{i}", [128, 16 * 512], BF16) for i in range(2)]
        tmpq = sbuf(p2q, "tmpq", [128, 512], F32)
        rop = sbuf(p2q, "rop", [128, 512], F32)
        sqt = sbuf(p2q, "sqt", [128, 512], BF16)
        qn_row = sbuf(p2q, "qn_row", [1, 16], F32)
        kn_row = sbuf(p2q, "kn_row", [1, 16], F32)

        @sec("sync")
        def _(e):
            s_ld.inc(e.dma_start(out=swapi[:], in_=swapi_in[:, :]), 16)
            s_ld.inc(e.dma_start(out=cos_sb[:], in_=cos_in[:, 0:T // 2]), 16)
            s_ld.inc(e.dma_start(out=sin_sb[:], in_=sin_in[:, 0:T // 2]), 16)

        for h in range(NHL):
            @sec("gpsimd")
            def _(e, h=h):
                if h > 0:
                    e.wait_ge(s_mm.h, s_mm.n)
                for j, w in ((0, wq_in), (1, wk_in), (2, wv_in)):
                    s_gld.inc(e.dma_start(
                        out=s3(wqkv_h[:, j * D:(j + 1) * D], 16),
                        in_=cmaj(w[h])), 16)

            @sec("tensor")
            def _(e, h=h):
                s_ld.wait(e)
                op = e.matmul(pb[6][:, 0:128], ones_mf[:],
                              bvrow[0:1, h * 128:(h + 1) * 128], start=True, stop=True)
                s_mm.inc(op, 1)

            @sec("vector")
            def _(e, h=h):
                s_mm.wait(e)
                op = e.tensor_copy(bvb[:], pb[6][:, 0:128])
                s_v.inc(op, 1)

            for grp in range(16):
                r, wnd = grp // 2, grp % 2

                @sec("sync")
                def _(e, grp=grp, r=r, wnd=wnd, h=h):
                    s_cc.wait(e)
                    if grp >= 2:
                        e.wait_ge(s_mm.h, s_mm.n)
                    if (grp == 8) or (h > 0 and grp == 0):
                        half = 1 if grp == 8 else 0
                        e.wait_ge(s_v.h, s_v.n)
                        s_ld.inc(e.dma_start(out=cos_sb[:],
                                 in_=cos_in[:, half * (T // 2):(half + 1) * (T // 2)]), 16)
                        s_ld.inc(e.dma_start(out=sin_sb[:],
                                 in_=sin_in[:, half * (T // 2):(half + 1) * (T // 2)]), 16)
                    op = e.dma_start(
                        out=s3(xn_sb[grp % 2][:], 16),
                        in_=cmaj(ag1_dst[r, :, wnd * 512:(wnd + 1) * 512]))
                    s_ld.inc(op, 16)

                @sec("tensor")
                def _(e, grp=grp, h=h):
                    s_ld.wait(e)
                    s_gld.wait(e)
                    s_v.wait(e)
                    xb = xn_sb[grp % 2]
                    for cc in range(16):
                        st, sp = cc == 0, cc == 15
                        e.matmul(pb[0][:, :], wqkv_h[:, cc * 128:(cc + 1) * 128],
                                 xb[:, cc * 512:(cc + 1) * 512], start=st, stop=sp)
                        e.matmul(pb[1][:, :], wqkv_h[:, D + cc * 128:D + (cc + 1) * 128],
                                 xb[:, cc * 512:(cc + 1) * 512], start=st, stop=sp)
                        for j in range(4):
                            op = e.matmul(pb[2][:, j * 128:(j + 1) * 128],
                                          xb[:, cc * 512 + j * 128:cc * 512 + (j + 1) * 128],
                                          wqkv_h[:, 2 * D + cc * 128:2 * D + (cc + 1) * 128],
                                          start=(cc == 0 and j == 0),
                                          stop=(cc == 15 and j == 3))
                    s_mm.inc(op, 1)

                for psrc, dst_sb, brow, nrow in ((0, q_sb, 0, qn_row), (1, k_sb, 2, kn_row)):
                    @sec("vector")
                    def _(e, h=h, psrc=psrc, brow=brow):
                        s_mm.wait(e)
                        op = e.tensor_scalar(tq_b[:], pb[psrc][:, :],
                                             bqkv[:, brow + h:brow + h + 1], None, ALU.add)
                        s_v.inc(op, 1)

                    @sec("tensor")
                    def _(e):
                        s_v.wait(e)
                        op = e.matmul(pb[5][:, :], swapi[:], tq_b[:], start=True, stop=True)
                        s_mm.inc(op, 1)

                    @sec("vector")
                    def _(e, grp=grp, dst_sb=dst_sb):
                        s_mm.wait(e)
                        g0 = grp * 512
                        c0 = (grp % 8) * 512
                        e.tensor_tensor(rop[:], tq_b[:], cos_sb[:, c0:c0 + 512], ALU.mult)
                        vf(e, e.scalar_tensor_tensor(rdb_sb[:], pb[5][:, :], 0.0,
                                               sin_sb[:, c0:c0 + 512], ALU.bypass, ALU.mult))
                        vf(e, e.tensor_add(dst_sb[:, g0:g0 + 512], rop[:], rdb_sb[:]))
                        op = e.tensor_mul(sqt[:], dst_sb[:, g0:g0 + 512],
                                          dst_sb[:, g0:g0 + 512])
                        s_v.inc(op, 1)

                    @sec("tensor")
                    def _(e):
                        s_v.wait(e)
                        op = e.matmul(pb[3][0:1, :], ones_k[:], sqt[:], start=True, stop=True)
                        s_mm.inc(op, 1)

                    @sec("vector")
                    def _(e, grp=grp, nrow=nrow):
                        s_mm.wait(e)
                        op = e.reduce_max(nrow[0:1, grp:grp + 1], pb[3][0:1, :], axis=AX.X)
                        s_v.inc(op, 1)

                @sec("vector")
                def _(e, grp=grp):
                    g0 = grp * 512
                    for j in range(4):
                        op = e.tensor_add(v_sb[:, g0 + j * 128:g0 + (j + 1) * 128],
                                          pb[2][:, j * 128:(j + 1) * 128], bvb[:])
                    s_v.inc(op, 1)

            @sec("vector")
            def _(e):
                e.reduce_max(scal[:, 4:5], qn_row[:], axis=AX.X)
                vf(e, e.reduce_max(scal[:, 5:6], kn_row[:], axis=AX.X))
                op = e.tensor_tensor(scal[:, 6:7], scal[:, 4:5], scal[:, 5:6], ALU.mult)
                s_v.inc(op, 1)

            @sec("scalar")
            def _(e):
                s_v.wait(e)
                op = e.activation(scal[:, 7:8], scal[:, 6:7], AF.Sqrt)
                s_sc.inc(op, 1)

            @sec("vector")
            def _(e):
                s_sc.wait(e)
                op = e.tensor_scalar(scal[:, 8:9], scal[:, 7:8], -1.02 * ISQ, -1.0,
                                     ALU.mult, ALU.add)
                s_v.inc(op, 1)

            @sec("tensor")
            def _(e):
                s_v.wait(e)
                op = e.matmul(pb[3][:, 0:1], ones_mf[:], scal[:, 8:9], start=True, stop=True)
                s_mm.inc(op, 1)

            @sec("vector")
            def _(e, h=h):
                s_mm.wait(e)
                op = e.tensor_copy(negb_sb[:, h:h + 1], pb[3][:, 0:1])
                s_v.inc(op, 1)

            if debug:
                @sec("gpsimd")
                def _(e, h=h):
                    e.wait_ge(s_v.h, s_v.n)
                    s_gld.inc(e.dma_start(out=dbg["q"][h], in_=q_sb[:]), 16)
                    s_gld.inc(e.dma_start(out=dbg["k"][h], in_=k_sb[:]), 16)
                    s_gld.inc(e.dma_start(out=dbg["v"][h], in_=v_sb[:]), 16)
                    s_gld.inc(e.dma_start(out=dbg["negb"][h].unsqueeze(1),
                                          in_=negb_sb[:, h:h + 1]), 16)

            barrier()

            STB = (0, 1, 6)
            for qg in range(16):
                q0 = qg * 512
                nkb = 4 * (qg + 1)
                exp_base = s_sc.n
                mask_done = {}
                for step in range(nkb + 1):
                    if step < nkb:
                        kb = step
                        diag = kb >= 4 * qg

                        @sec("tensor")
                        def _(e, kb=kb, q0=q0, exp_base=exp_base):
                            if kb >= 3:
                                e.wait_ge(s_sc.h, exp_base + kb - 2)
                            op = e.matmul(pb[STB[kb % 3]][:, :],
                                          k_sb[:, kb * 128:(kb + 1) * 128],
                                          q_sb[:, q0:q0 + 512], start=True, stop=True)
                            s_mm.inc(op, 1)

                        @sec("scalar")
                        def _(e, kb=kb, h=h):
                            s_mm.wait(e)
                            op = e.activation(p_sb[kb % 3][:], pb[STB[kb % 3]][:, :], AF.Exp,
                                              bias=negb_sb[:, h:h + 1], scale=ISQ)
                            s_sc.inc(op, 1)

                        if diag:
                            @sec("vector")
                            def _(e, kb=kb, qg=qg):
                                s_sc.wait(e)
                                o = kb - 4 * qg
                                op = e.tensor_mul(p_sb[kb % 3][:], p_sb[kb % 3][:],
                                                  mask_sb[:, o * 512:(o + 1) * 512])
                                s_v.inc(op, 1)
                                mask_done[kb] = s_v.n

                    if step >= 1:
                        kb2 = step - 1
                        diag2 = kb2 >= 4 * qg

                        @sec("tensor")
                        def _(e, kb2=kb2, qg=qg, nkb=nkb, exp_base=exp_base, diag2=diag2):
                            if diag2:
                                e.wait_ge(s_v.h, mask_done[kb2])
                            else:
                                e.wait_ge(s_sc.h, exp_base + kb2 + 1)
                                if kb2 == 0:
                                    s_v.wait(e)
                            st, sp = kb2 == 0, kb2 == nkb - 1
                            e.matmul(pb[2][:, :], v_sb[:, kb2 * 128:(kb2 + 1) * 128],
                                     p_sb[kb2 % 3][:], start=st, stop=sp)
                            op = e.matmul(pb[3][0:1, :], ones_k[:], p_sb[kb2 % 3][:],
                                          start=st, stop=sp)
                            s_mm.inc(op, 1)

                @sec("vector")
                def _(e):
                    s_mm.wait(e)
                    op = e.reciprocal(rd_sb[:], pb[3][0:1, :])
                    s_v.inc(op, 1)

                @sec("tensor")
                def _(e):
                    s_v.wait(e)
                    op = e.matmul(pb[4][:, :], ones_mf[:], rd_sb[:], start=True, stop=True)
                    s_mm.inc(op, 1)

                @sec("vector")
                def _(e, h=h, q0=q0):
                    s_mm.wait(e)
                    vf(e, e.tensor_copy(rdb_sb[:], pb[4][:, :]))
                    op = e.tensor_mul(z_sb[h][:, q0:q0 + 512], pb[2][:, :], rdb_sb[:])
                    s_v.inc(op, 1)

            if debug:
                @sec("gpsimd")
                def _(e, h=h):
                    e.wait_ge(s_v.h, s_v.n)
                    s_gld.inc(e.dma_start(out=dbg["z"][h], in_=z_sb[h][:]), 16)

            barrier()

        p2q.close()
        po_sb = [sbuf(p2, f"po{i}", [128, D], BF16) for i in range(2)]
        for tt in range(64):
            @sec("tensor")
            def _(e, tt=tt):
                if tt >= 2:
                    s_v.wait(e, s_v.n - 1)
                for mc in range(4):
                    bank = (tt % 2) * 4 + mc
                    for h in range(NHL):
                        op = e.matmul(pb[bank][:, :], z_sb[h][:, tt * 128:(tt + 1) * 128],
                                      wo_sb[:, h * D + mc * 512:h * D + (mc + 1) * 512],
                                      start=(h == 0), stop=(h == 1))
                s_mm.inc(op, 1)

            @sec("vector")
            def _(e, tt=tt):
                s_mm.wait(e)
                if tt >= 2:
                    s_ld.wait(e)
                for mc in range(4):
                    op = e.tensor_copy(po_sb[tt % 2][:, mc * 512:(mc + 1) * 512],
                                       pb[(tt % 2) * 4 + mc][:, :])
                s_v.inc(op, 1)

            @sec("sync")
            def _(e, tt=tt):
                s_v.wait(e)
                s_ld.inc(e.dma_start(out=part_src[tt * 128:(tt + 1) * 128, :],
                                     in_=po_sb[tt % 2][:]), 16)

        @sec("gpsimd")
        def _(e):
            e.wait_ge(s_ld.h, s_ld.n)
            op = e.collective_compute(
                "ReduceScatter", ALU.add, replica_groups=[list(range(NCORES))],
                ins=[part_src[:, :]], outs=[rs1_dst[:, :]])
            s_cc.inc(op, 1)

        if debug:
            @sec("gpsimd")
            def _(e):
                s_cc.wait(e)
                s_gld.inc(e.dma_start(out=dbg["rs1"][:, :], in_=rs1_dst[:, :]), 16)

        p2.close()
        barrier()

        p5 = ExitStack()
        xt2 = [sbuf(p5, f"xt2_{i}", [128, D], F32) for i in range(2)]
        at2 = [sbuf(p5, f"at2_{i}", [128, D], BF16) for i in range(2)]
        r2t = [sbuf(p5, f"r2t{i}", [128, D], F32) for i in range(2)]
        xn2f = [sbuf(p5, f"xn2f{i}", [128, D], F32) for i in range(2)]
        xqf = sbuf(p5, "xqf", [128, D], F32)
        xqb = [sbuf(p5, f"xqb{i}", [128, D], BF16) for i in range(2)]
        stage2 = [sbuf(p5, f"stage2_{i}", [128, D], BF16) for i in range(2)]
        xn2fm = sbuf(p5, "xn2fm", [128, 16 * 128], F32)
        mv2 = [sbuf(p5, f"mv2{i}", [128, 2], F32) for i in range(2)]
        ve2 = [sbuf(p5, f"ve2{i}", [128, 4], F32) for i in range(2)]
        qs2 = sbuf(p5, "qs2", [128, 8], F32)
        lg_sb = sbuf(p5, "lg", [128, 8], F32)
        mx8 = sbuf(p5, "mx8", [128, 8], F32)
        mi8 = sbuf(p5, "mi8", [128, 8], U32)
        idxf = sbuf(p5, "idxf", [128, 1], F32)

        for t in range(8):
            @sec("sync")
            def _(e, t=t):
                s_cc.wait(e)
                if t >= 2:
                    e.wait_ge(s_v.h, s_v.n)
                s_ld.inc(e.dma_start(out=xt2[t % 2][:],
                                     in_=x_in[t * 128:(t + 1) * 128, :]), 16)
                s_ld.inc(e.dma_start(out=at2[t % 2][:],
                                     in_=rs1_dst[t * 128:(t + 1) * 128, :]), 16)

            @sec("vector")
            def _(e, t=t):
                s_ld.wait(e)
                if t >= 1:
                    s_sld.wait(e)
                    s_mm.wait(e)
                vf(e, e.tensor_add(r2t[t % 2][:], xt2[t % 2][:], at2[t % 2][:]))
                op = e.tensor_add(r2t[t % 2][:], r2t[t % 2][:], bo_bc[:])
                s_v.inc(op, 1)

            @sec("scalar")
            def _(e, t=t):
                s_v.wait(e)
                op = e.dma_start(out=r2_buf[t * 128:(t + 1) * 128, :], in_=r2t[t % 2][:])
                s_sld.inc(op, 16)

            @sec("vector")
            def _(e, t=t):
                vf(e, e.reduce_sum(ve2[t % 2][:, 3:4], r2t[t % 2][:], axis=AX.X))
                e.tensor_scalar(mv2[t % 2][:, 0:1], ve2[t % 2][:, 3:4], 1.0 / D, None, ALU.mult)
                op = e.tensor_scalar(mv2[t % 2][:, 1:2], ve2[t % 2][:, 3:4], -1.0 / D, None, ALU.mult)
                s_v.inc(op, 1)

            @sec("scalar")
            def _(e, t=t):
                s_v.wait(e)
                op = e.activation(xn2f[t % 2][:], r2t[t % 2][:], AF.Square,
                                  bias=mv2[t % 2][:, 1:2], accum_out=ve2[t % 2][:, 0:1])
                s_sc.inc(op, 1)

            @sec("vector")
            def _(e, t=t):
                s_sc.wait(e)
                op = e.tensor_scalar(ve2[t % 2][:, 0:1], ve2[t % 2][:, 0:1], 1.0 / D, EPS,
                                     ALU.mult, ALU.add)
                s_v.inc(op, 1)

            @sec("scalar")
            def _(e, t=t):
                s_v.wait(e)
                op = e.activation(ve2[t % 2][:, 1:2], ve2[t % 2][:, 0:1], AF.Sqrt)
                s_sc.inc(op, 1)

            @sec("vector")
            def _(e, t=t):
                s_sc.wait(e)
                vf(e, e.reciprocal(ve2[t % 2][:, 2:3], ve2[t % 2][:, 1:2]))
                vf(e, e.tensor_scalar(xn2f[t % 2][:], r2t[t % 2][:], mv2[t % 2][:, 0:1],
                                      ve2[t % 2][:, 2:3], ALU.subtract, ALU.mult))
                vf(e, e.reduce_max(qs2[:, 0:1], xn2f[t % 2][:], axis=AX.X,
                                   apply_absolute_value=True))
                vf(e, e.tensor_scalar(qs2[:, 1:2], qs2[:, 0:1], EPS, None, ALU.max))
                vf(e, e.reciprocal(qs2[:, 2:3], qs2[:, 1:2]))
                e.tensor_scalar(qs2[:, 3:4], qs2[:, 2:3], 127.0, None, ALU.mult)
                vf(e, e.tensor_scalar(qs2[:, 4:5], qs2[:, 1:2], 1.0 / 127.0, None, ALU.mult))
                vf(e, e.tensor_scalar(xqf[:], xn2f[t % 2][:], qs2[:, 3:4], None, ALU.mult))
                vf(e, e.tensor_scalar(xqf[:], xqf[:], MAGIC, -MAGIC, ALU.add, ALU.add))
                op = e.tensor_scalar(xqb[t % 2][:], xqf[:], 127.0, -128.0, ALU.min, ALU.max)
                s_v.inc(op, 1)

            @sec("scalar")
            def _(e, t=t):
                s_v.wait(e)
                op = e.dma_start(out=meta_src[0, t * 128:(t + 1) * 128].unsqueeze(1), in_=qs2[:, 4:5])
                s_sld.inc(op, 16)

            @sec("tensor")
            def _(e, t=t):
                s_v.wait(e)
                bank = 4 + (t % 2) * 2
                for cc in range(16):
                    op = e.transpose(
                        pbb(bank + cc // 8)[:, (cc % 8) * 128:(cc % 8 + 1) * 128],
                        xqb[t % 2][:, cc * 128:(cc + 1) * 128], identb[:])
                s_mm.inc(op, 1)

            @sec("vector")
            def _(e, t=t):
                s_mm.wait(e)
                if t >= 2:
                    s_ld.wait(e)
                bank = 4 + (t % 2) * 2
                e.tensor_copy(stage2[t % 2][:, 0:1024], pbb(bank)[:, 0:1024])
                op = e.tensor_copy(stage2[t % 2][:, 1024:2048], pbb(bank + 1)[:, 0:1024])
                s_v.inc(op, 1)

            @sec("sync")
            def _(e, t=t):
                s_v.wait(e)
                op = e.dma_start(
                    out=cmaj(ag2_src[:, t * 128:(t + 1) * 128]),
                    in_=s3(stage2[t % 2][:], 16))
                s_ld.inc(op, 16)

            @sec("tensor")
            def _(e, t=t):
                for cc4 in range(4):
                    for ccl in range(4):
                        cc = cc4 * 4 + ccl
                        op = e.transpose(pb[cc4][:, ccl * 128:(ccl + 1) * 128],
                                         xn2f[t % 2][:, cc * 128:(cc + 1) * 128], identf[:])
                s_mm.inc(op, 1)

            @sec("vector")
            def _(e, t=t):
                s_mm.wait(e)
                for cc4 in range(4):
                    op = e.tensor_copy(xn2fm[:, cc4 * 512:(cc4 + 1) * 512], pb[cc4][:, :])
                s_v.inc(op, 1)

            @sec("tensor")
            def _(e, t=t):
                s_v.wait(e)
                for cc in range(16):
                    op = e.matmul(pb[6][:, 0:8], xn2fm[:, cc * 128:(cc + 1) * 128],
                                  rwt_sb[:, cc * 8:(cc + 1) * 8],
                                  start=(cc == 0), stop=(cc == 15))
                s_mm.inc(op, 1)

            @sec("vector")
            def _(e, t=t):
                s_mm.wait(e)
                vf(e, e.tensor_copy(lg_sb[:], pb[6][:, 0:8]))
                vf(e, e.max(mx8[:], lg_sb[:]))
                vf(e, e.max_index(mi8[:], mx8[:], lg_sb[:]))
                op = e.tensor_copy(idxf[:], mi8[:, 0:1])
                s_v.inc(op, 1)

            @sec("scalar")
            def _(e, t=t):
                s_v.wait(e)
                op = e.dma_start(out=meta_src[1, t * 128:(t + 1) * 128].unsqueeze(1), in_=idxf[:])
                s_sld.inc(op, 16)

        @sec("gpsimd")
        def _(e):
            e.wait_ge(s_ld.h, s_ld.n)
            e.wait_ge(s_sld.h, s_sld.n)
            op = e.collective_compute(
                "AllGather", ALU.bypass, replica_groups=[list(range(NCORES))],
                ins=[meta_src[:, :]], outs=[meta_dst[:, :, :]])
            s_cc.inc(op, 1)
            op = e.collective_compute(
                "AllGather", ALU.bypass, replica_groups=[list(range(NCORES))],
                ins=[ag2_src[:, :]], outs=[ag2_dst[:, :, :]])
            s_cc.inc(op, 1)

        if debug:
            @sec("gpsimd")
            def _(e):
                s_cc.wait(e)
                s_gld.inc(e.dma_start(out=dbg["ag2"][:, :, :], in_=ag2_dst[:, :, :]), 16)
                s_gld.inc(e.dma_start(out=dbg["meta"][:, :, :], in_=meta_dst[:, :, :]), 16)
                s_gld.inc(e.dma_start(out=dbg["r2"][:, :], in_=r2_buf[:, :]), 16)

        p5.close()
        barrier()

        pf = ExitStack()
        act_sg = sbuf(pf, "act_sg", [128, 16 * 1024], BF16)
        h_sg = sbuf(pf, "h_sg", [128, 16 * 1024], F32)
        wblk = [sbuf(pf, f"wblk{i}", [128, 16 * 512], BF16) for i in range(2)]
        sxg = sbuf(pf, "sxg", [128, 1024], F32)
        sxu = sbuf(pf, "sxu", [128, 1024], F32)
        shb = sbuf(pf, "shb", [128, 1024], F32)
        hmax = sbuf(pf, "hmax", [128, 1024], F32)
        srow = sbuf(pf, "srow", [1, 1024], F32)
        irow = sbuf(pf, "irow", [1, 1024], F32)
        srow2 = sbuf(pf, "srow2", [1, 1024], F32)
        hmcl = sbuf(pf, "hmcl", [128, 8], F32)
        invsh = sbuf(pf, "invsh", [128, 8], F32)
        comb = sbuf(pf, "comb", [128, 8], F32)
        idxt = sbuf(pf, "idxt", [128, 8], F32)
        gtile = [sbuf(pf, f"gtile{i}", [128, 512], F32) for i in range(2)]
        utile = [sbuf(pf, f"utile{i}", [128, 512], F32) for i in range(2)]
        hmax2 = sbuf(pf, "hmax2", [128, 1024], F32)
        y_sb = [sbuf(pf, f"y{i}", [128, D], BF16) for i in range(1)]

        wdv = h_sg[:].bitcast(BF16)
        wblk_free = [0, 0]
        gld_tgt = [0, 0, 0, 0]
        uld_tgt = [0, 0, 0, 0]

        for sg in range(8):
            @sec("sync")
            def _(e, sg=sg):
                s_cc.wait(e)
                if sg > 0:
                    e.wait_ge(s_mm.h, s_mm.n)
                op = e.dma_start(out=s3(act_sg[:], 16),
                                 in_=cmaj(ag2_dst[sg]))
                s_ld.inc(op, 16)
                op = e.dma_start(out=srow[:], in_=meta_dst[sg, 0].unsqueeze(0))
                s_ld.inc(op, 16)
                op = e.dma_start(out=irow[:], in_=meta_dst[sg, 1].unsqueeze(0))
                s_ld.inc(op, 16)

            @sec("tensor")
            def _(e, sg=sg):
                s_ld.wait(e)
                if sg > 0:
                    e.wait_ge(s_v.h, s_v.n)
                for j in range(2):
                    op = e.matmul(pb[6][:, :], ones_mf[:], srow[0:1, j * 512:(j + 1) * 512],
                                  start=True, stop=True)
                    s_mm.inc(op, 1)
                    e.wait_ge(s_v.h, s_v.n + j + 1)

            @sec("vector")
            def _(e, sg=sg):
                base = s_mm.n - 2
                e.memset(hmax[:], 0.0)
                e.memset(hmax2[:], 0.0)
                for j in range(2):
                    s_mm.wait(e, base + j + 1)
                    e.tensor_scalar(sxg[:, j * 512:(j + 1) * 512], pb[6][:, :],
                                    winv[:, 0:1], None, ALU.mult)
                    op = e.tensor_scalar(sxu[:, j * 512:(j + 1) * 512], pb[6][:, :],
                                         winv[:, 1:2], None, ALU.mult)
                    s_v.inc(op, 1)

            @sec("scalar")
            def _(e, sg=sg):
                e.wait_ge(s_mm.h, wblk_free[0])
                op = e.dma_start(out=s3(wblk[0][:], 16), in_=cmaj(wqt["g"][0]))
                s_sld.inc(op, 16)
                gld_tgt[0] = s_sld.n
                e.wait_ge(s_mm.h, wblk_free[1])
                op = e.dma_start(out=s3(wblk[1][:], 16), in_=cmaj(wqt["u"][0]))
                s_sld.inc(op, 16)
                uld_tgt[0] = s_sld.n

            for ds in range(4):
                for j in range(2):
                    @sec("tensor")
                    def _(e, ds=ds, j=j):
                        e.wait_ge(s_sld.h, gld_tgt[ds])
                        s_v.wait(e)
                        if ds == 0 and j == 0:
                            s_ld.wait(e)
                        for hc in range(16):
                            st, sp = hc == 0, hc == 15
                            for dc in range(4):
                                op = e.matmul(pb[dc][:, :],
                                              wblk[0][:, hc * 512 + dc * 128:hc * 512 + (dc + 1) * 128],
                                              act_sg[:, hc * 1024 + j * 512:hc * 1024 + (j + 1) * 512],
                                              start=st, stop=sp)
                        s_mm.inc(op, 1)
                        if j == 1:
                            wblk_free[0] = s_mm.n

                    @sec("tensor")
                    def _(e, ds=ds, j=j):
                        e.wait_ge(s_sld.h, uld_tgt[ds])
                        for hc in range(16):
                            st, sp = hc == 0, hc == 15
                            for dc in range(4):
                                op = e.matmul(pb[4 + dc][:, :],
                                              wblk[1][:, hc * 512 + dc * 128:hc * 512 + (dc + 1) * 128],
                                              act_sg[:, hc * 1024 + j * 512:hc * 1024 + (j + 1) * 512],
                                              start=st, stop=sp)
                        s_mm.inc(op, 1)
                        if j == 1:
                            wblk_free[1] = s_mm.n

                    @sec("vector")
                    def _(e, ds=ds, j=j):
                        s_mm.wait(e)
                        hm = (hmax, hmax2)
                        for dp in range(2):
                            dcs = (2 * dp, 2 * dp + 1)
                            for i, dc in enumerate(dcs):
                                e.tensor_tensor(gtile[i][:], pb[dc][:, :],
                                                sxg[:, j * 512:(j + 1) * 512], ALU.mult)
                            for i, dc in enumerate(dcs):
                                e.tensor_tensor(utile[i][:], pb[4 + dc][:, :],
                                                sxu[:, j * 512:(j + 1) * 512], ALU.mult)
                            for i, dc in enumerate(dcs):
                                e.scalar_tensor_tensor(gtile[i][:], gtile[i][:], 0.0,
                                                       gtile[i][:], ALU.max, ALU.mult)
                            for i, dc in enumerate(dcs):
                                col = (ds * 4 + dc) * 1024 + j * 512
                                e.tensor_tensor(h_sg[:, col:col + 512], gtile[i][:],
                                                utile[i][:], ALU.mult)
                            for i, dc in enumerate(dcs):
                                col = (ds * 4 + dc) * 1024 + j * 512
                                e.tensor_mul(gtile[i][:], h_sg[:, col:col + 512],
                                             h_sg[:, col:col + 512])
                            for i, dc in enumerate(dcs):
                                op = e.tensor_max(hm[i][:, j * 512:(j + 1) * 512],
                                                  hm[i][:, j * 512:(j + 1) * 512],
                                                  gtile[i][:])
                        s_v.inc(op, 1)

                if ds < 3:
                    @sec("scalar")
                    def _(e, ds=ds, sg=sg):
                        e.wait_ge(s_mm.h, wblk_free[0])
                        op = e.dma_start(out=s3(wblk[0][:], 16), in_=cmaj(wqt["g"][ds + 1]))
                        s_sld.inc(op, 16)
                        gld_tgt[ds + 1] = s_sld.n
                        e.wait_ge(s_mm.h, wblk_free[1])
                        op = e.dma_start(out=s3(wblk[1][:], 16), in_=cmaj(wqt["u"][ds + 1]))
                        s_sld.inc(op, 16)
                        uld_tgt[ds + 1] = s_sld.n

            @sec("vector")
            def _(e):
                op = e.tensor_max(hmax[:], hmax[:], hmax2[:])
                vf(e, op)

            for w in range(8):
                @sec("tensor")
                def _(e, w=w):
                    s_v.wait(e)
                    op = e.transpose(pb[5][:, (w % 4) * 128:(w % 4 + 1) * 128],
                                     hmax[:, w * 128:(w + 1) * 128], identf[:])
                    s_mm.inc(op, 1)

                @sec("vector")
                def _(e, w=w):
                    s_mm.wait(e)
                    op = e.reduce_max(hmcl[:, w:w + 1],
                                      pb[5][:, (w % 4) * 128:(w % 4 + 1) * 128], axis=AX.X)
                    s_v.inc(op, 1)

            @sec("scalar")
            def _(e):
                s_v.wait(e)
                op = e.activation(hmcl[:], hmcl[:], AF.Sqrt)
                s_sc.inc(op, 1)

            @sec("vector")
            def _(e):
                s_sc.wait(e)
                vf(e, e.tensor_scalar(hmcl[:], hmcl[:], EPS, None, ALU.max))
                op = e.tensor_scalar(invsh[:], hmcl[:], 1.0 / 127.0, None, ALU.mult)
                s_v.inc(op, 1)

            for w in range(8):
                @sec("tensor")
                def _(e, w=w):
                    s_v.wait(e)
                    op = e.transpose(pb[6 + w // 4][0:1, (w % 4) * 128:(w % 4 + 1) * 128],
                                     hmcl[:, w:w + 1], identf[:])
                    s_mm.inc(op, 1)

            @sec("vector")
            def _(e):
                s_mm.wait(e)
                e.tensor_copy(srow2[0:1, 0:512], pb[6][0:1, 0:512])
                vf(e, e.tensor_copy(srow2[0:1, 512:1024], pb[7][0:1, 0:512]))
                vf(e, e.reciprocal(srow2[:], srow2[:]))
                op = e.tensor_scalar(srow2[:], srow2[:], 127.0, None, ALU.mult)
                s_v.inc(op, 1)

            @sec("tensor")
            def _(e):
                s_v.wait(e)
                for j in range(2):
                    op = e.matmul(pb[7][:, :], ones_mf[:], srow2[0:1, j * 512:(j + 1) * 512],
                                  start=True, stop=True)
                    s_mm.inc(op, 1)
                    e.wait_ge(s_v.h, s_v.n + j + 1)

            @sec("vector")
            def _(e):
                base = s_mm.n - 2
                for j in range(2):
                    s_mm.wait(e, base + j + 1)
                    op = e.tensor_copy(shb[:, j * 512:(j + 1) * 512], pb[7][:, :])
                    s_v.inc(op, 1)

            @sec("vector")
            def _(e):
                for ch in range(16):
                    for j in range(2):
                        col = ch * 1024 + j * 512
                        e.tensor_tensor(gtile[j][:], h_sg[:, col:col + 512],
                                        shb[:, j * 512:(j + 1) * 512], ALU.mult)
                    for j in range(2):
                        e.tensor_scalar(gtile[j][:], gtile[j][:], MAGIC, -MAGIC,
                                        ALU.add, ALU.add)
                    for j in range(2):
                        col = ch * 1024 + j * 512
                        op = e.tensor_scalar(act_sg[:, col:col + 512], gtile[j][:],
                                             127.0, -128.0, ALU.min, ALU.max)
                s_v.inc(op, 1)

            for w in range(8):
                @sec("tensor")
                def _(e, w=w):
                    if w == 0:
                        s_ld.wait(e)
                    op = e.transpose(pb[4][:, w * 4:w * 4 + 1],
                                     irow[0:1, w * 128:(w + 1) * 128], identf[0:1, 0:1])
                    s_mm.inc(op, 1)

            @sec("vector")
            def _(e):
                s_mm.wait(e)
                for w in range(8):
                    e.tensor_copy(idxt[:, w:w + 1], pb[4][:, w * 4:w * 4 + 1])
                for w in range(8):
                    e.tensor_tensor(comb[:, w:w + 1], idxt[:, w:w + 1], eidb[:],
                                    ALU.is_equal)
                for w in range(8):
                    e.tensor_tensor(comb[:, w:w + 1], comb[:, w:w + 1],
                                    invsh[:, w:w + 1], ALU.mult)
                for w in range(8):
                    op = e.tensor_scalar(comb[:, w:w + 1], comb[:, w:w + 1],
                                         winv[:, 2:3], None, ALU.mult)
                s_v.inc(op, 1)

            @sec("scalar")
            def _(e, sg=sg):
                s_v.wait(e)
                for hfg in range(4):
                    op = e.dma_start(out=s3(wdv[:, hfg * 8192:(hfg + 1) * 8192], 16),
                                     in_=cmaj(wqt["d"][hfg]))
                    s_sld.inc(op, 16)

            for tt in range(8):
                @sec("tensor")
                def _(e, sg=sg, tt=tt):
                    s_sld.wait(e)
                    s_v.wait(e)
                    if tt >= 2:
                        s_v.wait(e)
                    for hfg in range(4):
                        bank = (tt % 2) * 4 + hfg
                        for dc in range(16):
                            op = e.matmul(pb[bank][:, :],
                                          act_sg[:, dc * 1024 + tt * 128:dc * 1024 + (tt + 1) * 128],
                                          wdv[:, hfg * 8192 + dc * 512:hfg * 8192 + (dc + 1) * 512],
                                          start=(dc == 0), stop=(dc == 15))
                    s_mm.inc(op, 1)

                @sec("vector")
                def _(e, tt=tt):
                    s_mm.wait(e)
                    if tt >= 2:
                        s_ld.wait(e)
                    for hfg in range(4):
                        op = e.tensor_scalar(y_sb[0][:, hfg * 512:(hfg + 1) * 512],
                                             pb[(tt % 2) * 4 + hfg][:, :],
                                             comb[:, tt:tt + 1], None, ALU.mult)
                    s_v.inc(op, 1)

                @sec("sync")
                def _(e, sg=sg, tt=tt):
                    s_v.wait(e)
                    s_ld.inc(e.dma_start(
                        out=y_src[sg * 1024 + tt * 128:sg * 1024 + (tt + 1) * 128, :],
                        in_=y_sb[0][:]), 16)

        @sec("gpsimd")
        def _(e):
            e.wait_ge(s_ld.h, s_ld.n)
            op = e.collective_compute(
                "ReduceScatter", ALU.add, replica_groups=[list(range(NCORES))],
                ins=[y_src[:, :]], outs=[rs2_dst[:, :]])
            s_cc.inc(op, 1)

        if debug:
            @sec("gpsimd")
            def _(e):
                s_cc.wait(e)
                s_gld.inc(e.dma_start(out=dbg["ysrc"][:, :], in_=y_src[0:TC, :]), 16)
                s_gld.inc(e.dma_start(out=dbg["rs2"][:, :], in_=rs2_dst[:, :]), 16)

        pf.close()
        barrier()

        p8 = ExitStack()
        f1 = [sbuf(p8, f"f1_{i}", [128, D], F32) for i in range(2)]
        f2 = [sbuf(p8, f"f2_{i}", [128, D], BF16) for i in range(2)]
        fo = [sbuf(p8, f"fo_{i}", [128, D], F32) for i in range(2)]
        for t in range(8):
            @sec("sync")
            def _(e, t=t):
                s_cc.wait(e)
                if t >= 2:
                    e.wait_ge(s_v.h, s_v.n)
                s_ld.inc(e.dma_start(out=f1[t % 2][:],
                                     in_=r2_buf[t * 128:(t + 1) * 128, :]), 16)
                s_ld.inc(e.dma_start(out=f2[t % 2][:],
                                     in_=rs2_dst[t * 128:(t + 1) * 128, :]), 16)

            @sec("vector")
            def _(e, t=t):
                s_ld.wait(e)
                if t >= 2:
                    s_sld.wait(e)
                op = e.tensor_add(fo[t % 2][:], f1[t % 2][:], f2[t % 2][:])
                s_v.inc(op, 1)

            @sec("scalar")
            def _(e, t=t):
                s_v.wait(e)
                op = e.dma_start(out=out[t * 128:(t + 1) * 128, :], in_=fo[t % 2][:])
                s_sld.inc(op, 16)

        @sec("scalar")
        def _(e):
            e.wait_ge(s_sld.h, s_sld.n)
        @sec("sync")
        def _(e):
            e.wait_ge(s_sld.h, s_sld.n)

        p8.close()

    return nc, dbg



def _host_inputs(inputs):
    x = np.ascontiguousarray(np.asarray(inputs["L0_attn_normalized_resid_pre"], np.float32)[0])
    WQ = np.asarray(inputs["L0_attn_W_Q"], np.float32)
    WK = np.asarray(inputs["L0_attn_W_K"], np.float32)
    WV = np.asarray(inputs["L0_attn_W_V"], np.float32)
    WO = np.asarray(inputs["L0_attn_W_O"], np.float32)
    bQ = np.asarray(inputs["L0_attn_b_Q"], np.float32)
    bK = np.asarray(inputs["L0_attn_b_K"], np.float32)
    bV = np.asarray(inputs["L0_attn_b_V"], np.float32)
    bO = np.asarray(inputs["L0_attn_b_O"], np.float32)
    RW = np.asarray(inputs["L0_ffn_router_w"], np.float32)
    WG = np.asarray(inputs["L0_ffn_wg"], np.float32)
    WU = np.asarray(inputs["L0_ffn_wu"], np.float32)
    WD = np.asarray(inputs["L0_ffn_wd"], np.float32)

    perm = np.concatenate([np.arange(0, 128, 2), np.arange(1, 128, 2)])
    inv_freq = 1.0 / (10000.0 ** (np.arange(0, 128, 2, dtype=np.float32) / 128.0))
    m = np.arange(T, dtype=np.float32)
    freqs = np.outer(m, inv_freq)
    cos64 = np.cos(freqs).T.astype(np.float32)
    sin64 = np.sin(freqs).T.astype(np.float32)
    cos_t = np.ascontiguousarray(np.concatenate([cos64, cos64], axis=0))
    sin_t = np.ascontiguousarray(np.concatenate([-sin64, sin64], axis=0))
    swapi = np.ascontiguousarray(np.roll(np.eye(128, dtype=ml_dtypes.bfloat16), 64, axis=0))

    masks = np.zeros((4, 128, 512), dtype=ml_dtypes.bfloat16)
    for o in range(4):
        kk = np.arange(128)[:, None] + 128 * o
        qq = np.arange(512)[None, :]
        masks[o] = (qq >= kk).astype(ml_dtypes.bfloat16)

    common = {
        "rw": RW, "cos": cos_t, "sin": sin_t, "masks": masks,
        "identb": np.eye(128, dtype=ml_dtypes.bfloat16),
        "identf": np.eye(128, dtype=np.float32),
        "ones_k": np.ones((128, 1), ml_dtypes.bfloat16),
        "ones_m": np.ones((1, 128), ml_dtypes.bfloat16),
        "ones_mf": np.ones((1, 128), np.float32),
        "ones_kf": np.ones((128, 1), np.float32),
        "swapi": swapi,
        "bo": np.ascontiguousarray(bO[None, :]),
    }
    in_maps = []
    for c in range(NCORES):
        hs = slice(2 * c, 2 * c + 2)
        d = dict(common)
        d.update({
            "x": np.ascontiguousarray(x[c * TC:(c + 1) * TC]),
            "wq": np.ascontiguousarray(WQ[hs][:, :, perm]),
            "wk": np.ascontiguousarray(WK[hs][:, :, perm]),
            "wv": np.ascontiguousarray(WV[hs]),
            "wo": np.ascontiguousarray(WO[hs]),
            "bq": np.ascontiguousarray(bQ[hs][:, perm]),
            "bk": np.ascontiguousarray(bK[hs][:, perm]),
            "bv": np.ascontiguousarray(bV[hs]),
            "wg": np.ascontiguousarray(WG[c]),
            "wu": np.ascontiguousarray(WU[c]),
            "wd": np.ascontiguousarray(WD[c]),
            "eid": np.full((1, 1), float(c), np.float32),
        })
        in_maps.append(d)
    return in_maps


def kernel(**inputs):
    in_maps = _host_inputs(inputs)
    nc, _ = build_program(debug=DEBUG)
    res = run_bass_kernel_spmd(nc, in_maps, core_ids=list(range(NCORES)))
    outv = np.concatenate([np.asarray(r["out"]) for r in res.results], axis=0)
    return outv[None, :, :].astype(np.float32)


# version 07 (final)
# speedup vs baseline: 1.3773x; 1.1039x over previous
import numpy as np
import ml_dtypes
from contextlib import ExitStack
import concourse.bass as bass
import concourse.mybir as mybir
from concourse.bass_utils import run_bass_kernel_spmd

F32 = mybir.dt.float32
BF16 = mybir.dt.bfloat16
U32 = mybir.dt.uint32
AF = mybir.ActivationFunctionType
ALU = mybir.AluOpType
AX = mybir.AxisListType

MAGIC = 1.5 * 2.0 ** 23
NCORES = 8
T = 8192
TC = T // NCORES
D = 2048
NHL = 2
EPS = 1e-5
ISQ = float(1.0 / np.sqrt(128.0))

DEBUG = False


class S:
    def __init__(self, stack, nc, name):
        self.h = stack.enter_context(nc.semaphore(name))
        self.n = 0

    def inc(self, instr, k=1):
        instr.then_inc(self.h, k)
        self.n += k
        return self.n

    def wait(self, eng, val=None):
        eng.wait_ge(self.h, self.n if val is None else val)


def build_program(debug=False):
    nc = bass.Bass("TRN2", target_bir_lowering=False, debug=False)

    def din(name, shape, dt=F32):
        return nc.dram_tensor(name, shape, dt, kind="ExternalInput").ap()

    x_in = din("x", [TC, D])
    wq_in = din("wq", [NHL, D, 128])
    wk_in = din("wk", [NHL, D, 128])
    wv_in = din("wv", [NHL, D, 128])
    wo_in = din("wo", [NHL, 128, D])
    bq_in = din("bq", [NHL, 128])
    bk_in = din("bk", [NHL, 128])
    bv_in = din("bv", [NHL, 128])
    bo_in = din("bo", [1, D])
    rw_in = din("rw", [NCORES, D])
    wg_in = din("wg", [D, D])
    wu_in = din("wu", [D, D])
    wd_in = din("wd", [D, D])
    cos_in = din("cos", [128, T])
    sin_in = din("sin", [128, T])
    masks_in = din("masks", [4, 128, 512], BF16)
    identb_in = din("identb", [128, 128], BF16)
    identf_in = din("identf", [128, 128], F32)
    ones_k_in = din("ones_k", [128, 1], BF16)
    ones_m_in = din("ones_m", [1, 128], BF16)
    ones_mf_in = din("ones_mf", [1, 128], F32)
    ones_kf_in = din("ones_kf", [128, 1], F32)
    eid_in = din("eid", [1, 1], F32)
    swapi_in = din("swapi", [128, 128], BF16)

    out = nc.dram_tensor("out", [TC, D], F32, kind="ExternalOutput").ap()

    def dint(name, shape, dt):
        return nc.dram_tensor(name, shape, dt, kind="Internal").ap()

    ag1_src = dint("ag1_src", [D, TC], BF16)
    ag1_dst = nc.dram_tensor("ag1_dst", [NCORES, D, TC], BF16, kind="Internal", addr_space="Shared").ap()
    part_src = dint("part_src", [T, D], BF16)
    rs1_dst = dint("rs1_dst", [TC, D], BF16)
    r2_buf = dint("r2_buf", [TC, D], F32)
    ag2_src = dint("ag2_src", [D, TC], BF16)
    ag2_dst = nc.dram_tensor("ag2_dst", [NCORES, D, TC], BF16, kind="Internal", addr_space="Shared").ap()
    meta_src = dint("meta_src", [2, TC], F32)
    meta_dst = dint("meta_dst", [NCORES, 2, TC], F32)
    wqt = {m: dint(f"wqt_{m}", [4, D, 512], BF16) for m in "gud"}
    y_src = dint("y_src", [T, D], BF16)
    rs2_dst = dint("rs2_dst", [TC, D], BF16)

    dbg = {}
    if debug:
        def dout(name, shape, dt):
            dbg[name] = nc.dram_tensor("dbg_" + name, shape, dt, kind="ExternalOutput").ap()
        dout("ag1", [NCORES, D, TC], BF16)
        dout("ag1s", [D, TC], BF16)
        dout("q", [NHL, 128, T], BF16)
        dout("k", [NHL, 128, T], BF16)
        dout("v", [NHL, 128, T], BF16)
        dout("negb", [NHL, 128], F32)
        dout("z", [NHL, 128, T], BF16)
        dout("rs1", [TC, D], BF16)
        dout("r2", [TC, D], F32)
        dout("ag2", [NCORES, D, TC], BF16)
        dout("meta", [NCORES, 2, TC], F32)
        dout("wqtg", [4, D, 512], BF16)
        dout("wqtd", [4, D, 512], BF16)
        dout("winv", [128, 4], F32)
        dout("ysrc", [TC, D], BF16)
        dout("rs2", [TC, D], BF16)

    stack = ExitStack()
    with stack:
        def sbuf(st, name, shape, dt):
            return st.enter_context(nc.sbuf_tensor(name, shape, dt))

        def sem(name):
            return S(stack, nc, name)

        pq = [stack.enter_context(nc.psum_tensor(f"pq{i}", [128, 1024], F32))
              for i in range(4)]
        pb = [pq[i // 2][:, (i % 2) * 512:(i % 2 + 1) * 512] for i in range(8)]

        def pbb(i):
            return pb[i].bitcast(BF16)

        def cmaj(ap2d, p=128):
            c = ap2d.shape[0] // p
            return ap2d.rearrange("(c p) q -> c p q", p=p).transpose([1, 0, 2])

        def s3(ap2d, c):
            return ap2d.rearrange("p (c q) -> p c q", c=c)

        identb = sbuf(stack, "identb_s", [128, 128], BF16)
        identf = sbuf(stack, "identf_s", [128, 128], F32)
        ones_k = sbuf(stack, "ones_k_s", [128, 1], BF16)
        ones_m = sbuf(stack, "ones_m_s", [1, 128], BF16)
        ones_mf = sbuf(stack, "ones_mf_s", [1, 128], F32)
        ones_kf = sbuf(stack, "ones_kf_s", [128, 1], F32)
        mask_sb = sbuf(stack, "mask_s", [128, 4 * 512], BF16)
        bqkv = sbuf(stack, "bqkv", [128, 6], F32)
        bvrow = sbuf(stack, "bvrow", [1, 2 * 128], F32)
        bo_bc = sbuf(stack, "bo_bc", [128, D], F32)
        rwt_sb = sbuf(stack, "rwt", [128, 16 * 8], F32)
        wo_sb = sbuf(stack, "wo_s", [128, NHL * D], BF16)
        negb_sb = sbuf(stack, "negb_s", [128, NHL], F32)
        scal = sbuf(stack, "scal", [1, 16], F32)
        winv = sbuf(stack, "winv_s", [128, 4], F32)
        eid_sb = sbuf(stack, "eid_s", [1, 1], F32)
        eidb = sbuf(stack, "eidb", [128, 1], F32)
        p0 = ExitStack()
        bo_row = sbuf(p0, "bo_row_s", [1, D], F32)
        rw_sb = sbuf(p0, "rw_s", [NCORES, D], F32)

        s_ld = sem("s_ld")
        s_gld = sem("s_gld")
        s_sld = sem("s_sld")
        s_mm = sem("s_mm")
        s_v = sem("s_v")
        s_sc = sem("s_sc")
        s_cc = sem("s_cc")
        s_bar = sem("s_bar")

        blk = stack.enter_context(nc.Block())

        def sec(engname):
            def deco(f):
                getattr(blk, engname)(f)
            return deco

        def vf(e, op):
            s_v.inc(op, 1)
            s_v.wait(e)

        ENGS = ("sync", "scalar", "vector", "tensor", "gpsimd")

        def barrier():
            for en in ENGS:
                @sec(en)
                def _(e):
                    e.sem_inc(s_bar.h, 1)
            s_bar.n += len(ENGS)
            tgt = s_bar.n
            for en in ENGS:
                @sec(en)
                def _(e):
                    e.wait_ge(s_bar.h, tgt)

        @sec("sync")
        def _(e):
            for dst, src in ((identb, identb_in), (identf, identf_in),
                             (ones_k, ones_k_in), (ones_m, ones_m_in),
                             (ones_mf, ones_mf_in), (ones_kf, ones_kf_in),
                             (rw_sb, rw_in), (bo_row, bo_in), (eid_sb, eid_in)):
                s_ld.inc(e.dma_start(out=dst[:], in_=src[:, :]), 16)
            s_ld.inc(e.dma_start(out=s3(mask_sb[:], 4),
                                 in_=masks_in.transpose([1, 0, 2])), 16)
            for h in range(NHL):
                for j, b in ((0, bq_in), (2, bk_in), (4, bv_in)):
                    s_ld.inc(e.dma_start(out=bqkv[:, j + h:j + h + 1],
                                         in_=b[h].unsqueeze(1)), 16)
                s_ld.inc(e.dma_start(out=bvrow[0:1, h * 128:(h + 1) * 128],
                                     in_=bv_in[h].unsqueeze(0)), 16)

        @sec("gpsimd")
        def _(e):
            for h in range(NHL):
                s_gld.inc(e.dma_start(out=wo_sb[:, h * D:(h + 1) * D], in_=wo_in[h]), 16)

        @sec("tensor")
        def _(e):
            s_ld.wait(e)
            for cc in range(16):
                op = e.transpose(pb[0][:, cc * 8:(cc + 1) * 8],
                                 rw_sb[:, cc * 128:(cc + 1) * 128], identf[0:8, 0:8])
            s_mm.inc(op, 1)
            op = e.matmul(pb[2][:, 0:1], ones_mf[:], eid_sb[:], start=True, stop=True)
            s_mm.inc(op, 1)
            vbase = s_v.n
            for mc in range(4):
                op = e.matmul(pb[1], ones_mf[:], bo_row[:, mc * 512:(mc + 1) * 512],
                              start=True, stop=True)
                s_mm.inc(op, 1)
                e.wait_ge(s_v.h, vbase + mc + 1)

        @sec("vector")
        def _(e):
            s_mm.wait(e, s_mm.n - 4)
            e.tensor_copy(rwt_sb[:], pb[0][:, 0:128])
            e.tensor_copy(eidb[:], pb[2][:, 0:1])
            for mc in range(4):
                s_mm.wait(e, s_mm.n - 3 + mc)
                op = e.tensor_copy(bo_bc[:, mc * 512:(mc + 1) * 512], pb[1])
                s_v.inc(op, 1)

        barrier()
        p0.close()

        p1 = ExitStack()
        xt = [sbuf(p1, f"xt{i}", [128, D], F32) for i in range(2)]
        xn1b = [sbuf(p1, f"xn1b{i}", [128, D], BF16) for i in range(2)]
        stage1 = [sbuf(p1, f"stage1_{i}", [128, D], BF16) for i in range(2)]
        mv_sb = [sbuf(p1, f"mv{i}", [128, 2], F32) for i in range(2)]
        ve_sb = [sbuf(p1, f"ve{i}", [128, 4], F32) for i in range(2)]

        for t in range(8):
            @sec("sync")
            def _(e, t=t):
                if t >= 2:
                    e.wait_ge(s_v.h, s_v.n)
                s_ld.inc(e.dma_start(out=xt[t % 2][:],
                                     in_=x_in[t * 128:(t + 1) * 128, :]), 16)

            @sec("vector")
            def _(e, t=t):
                s_ld.wait(e)
                if t >= 2:
                    s_mm.wait(e)
                vf(e, e.reduce_sum(ve_sb[t % 2][:, 3:4], xt[t % 2][:], axis=AX.X))
                e.tensor_scalar(mv_sb[t % 2][:, 0:1], ve_sb[t % 2][:, 3:4], 1.0 / D, None, ALU.mult)
                op = e.tensor_scalar(mv_sb[t % 2][:, 1:2], ve_sb[t % 2][:, 3:4], -1.0 / D, None, ALU.mult)
                s_v.inc(op, 1)

            @sec("scalar")
            def _(e, t=t):
                s_v.wait(e)
                op = e.activation(xn1b[t % 2][:], xt[t % 2][:], AF.Square,
                                  bias=mv_sb[t % 2][:, 1:2], accum_out=ve_sb[t % 2][:, 0:1])
                s_sc.inc(op, 1)

            @sec("vector")
            def _(e, t=t):
                s_sc.wait(e)
                op = e.tensor_scalar(ve_sb[t % 2][:, 0:1], ve_sb[t % 2][:, 0:1], 1.0 / D, EPS,
                                     ALU.mult, ALU.add)
                s_v.inc(op, 1)

            @sec("scalar")
            def _(e, t=t):
                s_v.wait(e)
                op = e.activation(ve_sb[t % 2][:, 1:2], ve_sb[t % 2][:, 0:1], AF.Sqrt)
                s_sc.inc(op, 1)

            @sec("vector")
            def _(e, t=t):
                s_sc.wait(e)
                vf(e, e.reciprocal(ve_sb[t % 2][:, 2:3], ve_sb[t % 2][:, 1:2]))
                op = e.tensor_scalar(xn1b[t % 2][:], xt[t % 2][:], mv_sb[t % 2][:, 0:1],
                                     ve_sb[t % 2][:, 2:3], ALU.subtract, ALU.mult)
                s_v.inc(op, 1)

            @sec("tensor")
            def _(e, t=t):
                s_v.wait(e)
                bank = (t % 2) * 2
                for cc in range(16):
                    op = e.transpose(
                        pbb(bank + cc // 8)[:, (cc % 8) * 128:(cc % 8 + 1) * 128],
                        xn1b[t % 2][:, cc * 128:(cc + 1) * 128], identb[:])
                s_mm.inc(op, 1)

            @sec("vector")
            def _(e, t=t):
                s_mm.wait(e)
                if t >= 2:
                    s_ld.wait(e)
                bank = (t % 2) * 2
                e.tensor_copy(stage1[t % 2][:, 0:1024], pbb(bank)[:, 0:1024])
                op = e.tensor_copy(stage1[t % 2][:, 1024:2048], pbb(bank + 1)[:, 0:1024])
                s_v.inc(op, 1)

            @sec("sync")
            def _(e, t=t):
                s_v.wait(e)
                op = e.dma_start(
                    out=cmaj(ag1_src[:, t * 128:(t + 1) * 128]),
                    in_=s3(stage1[t % 2][:], 16))
                s_ld.inc(op, 16)

        @sec("gpsimd")
        def _(e):
            e.wait_ge(s_ld.h, s_ld.n)
            op = e.collective_compute(
                "AllGather", ALU.bypass, replica_groups=[list(range(NCORES))],
                ins=[ag1_src[:, :]], outs=[ag1_dst[:, :, :]])
            s_cc.inc(op, 1)

        wst = ExitStack()
        wf = [sbuf(wst, f"wf{i}", [128, D], F32) for i in range(4)]
        wqrow2 = None
        wqrow = [sbuf(wst, f"wqrow{i}", [128, D], BF16) for i in range(2)]
        wacc = sbuf(wst, "wacc", [128, 16], F32)
        waccs = sbuf(wst, "waccs", [128, 1], F32)
        wstage = sbuf(wst, "wstage", [128, 16 * 128], BF16)
        swb = sbuf(wst, "swb", [128, 1], F32)

        for mi, (mname, w_in) in enumerate((("g", wg_in), ("u", wu_in), ("d", wd_in))):
            @sec("gpsimd")
            def _(e, w_in=w_in):
                e.wait_ge(s_v.h, s_v.n)
                vb = s_v.n
                for rc in range(16):
                    if rc >= 4:
                        e.wait_ge(s_v.h, vb + rc - 3)
                    s_gld.inc(e.dma_start(out=wf[rc % 4][:],
                                         in_=w_in[rc * 128:(rc + 1) * 128, :]), 16)

            @sec("vector")
            def _(e):
                for rc in range(16):
                    s_gld.wait(e, s_gld.n - 16 * (15 - rc))
                    op = e.reduce_sum(wacc[:, rc:rc + 1], wf[rc % 4][:], axis=AX.X,
                                      apply_absolute_value=True)
                    if rc == 15:
                        vf(e, op)
                    else:
                        s_v.inc(op, 1)
                vf(e, e.reduce_sum(waccs[:], wacc[:], axis=AX.X))

            @sec("tensor")
            def _(e):
                s_v.wait(e)
                op = e.matmul(pb[6][0:1, 0:1], waccs[:], ones_kf[:], start=True, stop=True)
                s_mm.inc(op, 1)

            @sec("vector")
            def _(e):
                s_mm.wait(e)
                vf(e, e.tensor_scalar(scal[:, 0:1], pb[6][0:1, 0:1], 1.0 / (D * D), None, ALU.mult))
                vf(e, e.tensor_scalar(scal[:, 1:2], scal[:, 0:1], EPS, None, ALU.max))
                op = e.reciprocal(scal[:, 2:3], scal[:, 1:2])
                s_v.inc(op, 1)

            @sec("tensor")
            def _(e, mi=mi):
                s_v.wait(e)
                op = e.matmul(pb[7][:, 0:1], ones_mf[:], scal[:, 2:3], start=True, stop=True)
                s_mm.inc(op, 1)
                op = e.matmul(pb[7][:, 1:2], ones_mf[:], scal[:, 1:2], start=True, stop=True)
                s_mm.inc(op, 1)

            @sec("vector")
            def _(e, mi=mi):
                s_mm.wait(e)
                e.tensor_copy(swb[:], pb[7][:, 0:1])
                op = e.tensor_copy(winv[:, mi:mi + 1], pb[7][:, 1:2])
                s_v.inc(op, 1)

            @sec("gpsimd")
            def _(e, w_in=w_in):
                e.wait_ge(s_v.h, s_v.n)
                vb = s_v.n
                for rc in range(16):
                    if rc >= 4:
                        e.wait_ge(s_v.h, vb + 4 * (rc - 3))
                    s_gld.inc(e.dma_start(out=wf[rc % 4][:],
                                         in_=w_in[rc * 128:(rc + 1) * 128, :]), 16)

            trp_done = {}
            for rc in range(16):
                @sec("vector")
                def _(e, rc=rc):
                    s_gld.wait(e, s_gld.n - 16 * (15 - rc))
                    if rc >= 2:
                        e.wait_ge(s_mm.h, trp_done[rc - 2])
                    vf(e, e.tensor_scalar(wf[rc % 4][:], wf[rc % 4][:], swb[:], None, ALU.mult))
                    vf(e, e.tensor_scalar(wf[rc % 4][:], wf[rc % 4][:], MAGIC, -MAGIC,
                                          ALU.add, ALU.add))
                    op = e.tensor_scalar(wqrow[rc % 2][:], wf[rc % 4][:], 1.0, -1.0,
                                         ALU.min, ALU.max)
                    s_v.inc(op, 1)

                @sec("tensor")
                def _(e, rc=rc):
                    s_v.wait(e)
                    base = 4 + (rc % 2) * 2
                    for cc in range(16):
                        op = e.transpose(
                            pbb(base + cc // 8)[:, (cc % 8) * 128:(cc % 8 + 1) * 128],
                            wqrow[rc % 2][:, cc * 128:(cc + 1) * 128], identb[:])
                    s_mm.inc(op, 1)
                    trp_done[rc] = s_mm.n

                @sec("vector")
                def _(e, rc=rc):
                    s_mm.wait(e)
                    if rc >= 1:
                        s_sld.wait(e)
                    base = 4 + (rc % 2) * 2
                    e.tensor_copy(wstage[:, 0:1024], pbb(base)[:, 0:1024])
                    op = e.tensor_copy(wstage[:, 1024:2048], pbb(base + 1)[:, 0:1024])
                    s_v.inc(op, 1)

                @sec("scalar")
                def _(e, rc=rc, wqt_m=wqt[mname]):
                    s_v.wait(e)
                    op = e.dma_start(
                        out=cmaj(wqt_m[rc // 4][:, (rc % 4) * 128:(rc % 4 + 1) * 128]),
                        in_=s3(wstage[:], 16))
                    s_sld.inc(op, 16)

        barrier()

        if debug:
            @sec("gpsimd")
            def _(e):
                s_cc.wait(e)
                s_gld.inc(e.dma_start(out=dbg["ag1"][:, :, :], in_=ag1_dst[:, :, :]), 16)
                s_gld.inc(e.dma_start(out=dbg["ag1s"][:, :], in_=ag1_src[:, :]), 16)
                s_gld.inc(e.dma_start(out=dbg["wqtg"][:, :, :], in_=wqt["g"][:, :, :]), 16)
                s_gld.inc(e.dma_start(out=dbg["wqtd"][:, :, :], in_=wqt["d"][:, :, :]), 16)
                s_gld.inc(e.dma_start(out=dbg["winv"][:, :], in_=winv[:]), 16)

        wst.close()
        p1.close()

        p2 = ExitStack()
        q_sb = sbuf(p2, "q_sb", [128, T], BF16)
        k_sb = sbuf(p2, "k_sb", [128, T], BF16)
        v_sb = sbuf(p2, "v_sb", [128, T], BF16)
        z_sb = [sbuf(p2, f"z{h}", [128, T], BF16) for h in range(NHL)]
        bvb = sbuf(p2, "bvb", [128, 128], F32)
        p_sb = [sbuf(p2, f"p{i}", [128, 1024], BF16) for i in range(2)]
        rd_sb = sbuf(p2, "rd", [1, 512], F32)
        rdb_sb = sbuf(p2, "rdb", [128, 512], F32)
        p2q = ExitStack()
        cos_sb = sbuf(p2q, "cos_s", [128, T // 2], F32)
        sin_sb = sbuf(p2q, "sin_s", [128, T // 2], F32)
        swapi = sbuf(p2q, "swapi_s", [128, 128], BF16)
        tq_b = sbuf(p2q, "tq_b", [128, 512], BF16)
        wqkv_h = sbuf(p2q, "wqkv_h", [128, 3 * D], BF16)
        xn_sb = [sbuf(p2q, f"xn{i}", [128, 16 * 512], BF16) for i in range(2)]
        tmpq = sbuf(p2q, "tmpq", [128, 512], F32)
        rop = sbuf(p2q, "rop", [128, 512], F32)
        sqt = sbuf(p2q, "sqt", [128, 512], BF16)
        qn_row = sbuf(p2q, "qn_row", [1, 16], F32)
        kn_row = sbuf(p2q, "kn_row", [1, 16], F32)

        @sec("sync")
        def _(e):
            s_ld.inc(e.dma_start(out=swapi[:], in_=swapi_in[:, :]), 16)
            s_ld.inc(e.dma_start(out=cos_sb[:], in_=cos_in[:, 0:T // 2]), 16)
            s_ld.inc(e.dma_start(out=sin_sb[:], in_=sin_in[:, 0:T // 2]), 16)

        for h in range(NHL):
            @sec("gpsimd")
            def _(e, h=h):
                if h > 0:
                    e.wait_ge(s_mm.h, s_mm.n)
                for j, w in ((0, wq_in), (1, wk_in), (2, wv_in)):
                    s_gld.inc(e.dma_start(
                        out=s3(wqkv_h[:, j * D:(j + 1) * D], 16),
                        in_=cmaj(w[h])), 16)

            @sec("tensor")
            def _(e, h=h):
                s_ld.wait(e)
                op = e.matmul(pb[6][:, 0:128], ones_mf[:],
                              bvrow[0:1, h * 128:(h + 1) * 128], start=True, stop=True)
                s_mm.inc(op, 1)

            @sec("vector")
            def _(e, h=h):
                s_mm.wait(e)
                op = e.tensor_copy(bvb[:], pb[6][:, 0:128])
                s_v.inc(op, 1)

            for grp in range(16):
                r, wnd = grp // 2, grp % 2

                @sec("sync")
                def _(e, grp=grp, r=r, wnd=wnd, h=h):
                    s_cc.wait(e)
                    if grp >= 2:
                        e.wait_ge(s_mm.h, s_mm.n)
                    if (grp == 8) or (h > 0 and grp == 0):
                        half = 1 if grp == 8 else 0
                        e.wait_ge(s_v.h, s_v.n)
                        s_ld.inc(e.dma_start(out=cos_sb[:],
                                 in_=cos_in[:, half * (T // 2):(half + 1) * (T // 2)]), 16)
                        s_ld.inc(e.dma_start(out=sin_sb[:],
                                 in_=sin_in[:, half * (T // 2):(half + 1) * (T // 2)]), 16)
                    op = e.dma_start(
                        out=s3(xn_sb[grp % 2][:], 16),
                        in_=cmaj(ag1_dst[r, :, wnd * 512:(wnd + 1) * 512]))
                    s_ld.inc(op, 16)

                @sec("tensor")
                def _(e, grp=grp, h=h):
                    s_ld.wait(e)
                    s_gld.wait(e)
                    s_v.wait(e)
                    xb = xn_sb[grp % 2]
                    for cc in range(16):
                        st, sp = cc == 0, cc == 15
                        e.matmul(pb[0][:, :], wqkv_h[:, cc * 128:(cc + 1) * 128],
                                 xb[:, cc * 512:(cc + 1) * 512], start=st, stop=sp)
                        e.matmul(pb[1], wqkv_h[:, D + cc * 128:D + (cc + 1) * 128],
                                 xb[:, cc * 512:(cc + 1) * 512], start=st, stop=sp)
                        for j in range(4):
                            op = e.matmul(pb[2][:, j * 128:(j + 1) * 128],
                                          xb[:, cc * 512 + j * 128:cc * 512 + (j + 1) * 128],
                                          wqkv_h[:, 2 * D + cc * 128:2 * D + (cc + 1) * 128],
                                          start=(cc == 0 and j == 0),
                                          stop=(cc == 15 and j == 3))
                    s_mm.inc(op, 1)

                for psrc, dst_sb, brow, nrow in ((0, q_sb, 0, qn_row), (1, k_sb, 2, kn_row)):
                    @sec("vector")
                    def _(e, h=h, psrc=psrc, brow=brow):
                        s_mm.wait(e)
                        op = e.tensor_scalar(tq_b[:], pb[psrc][:, :],
                                             bqkv[:, brow + h:brow + h + 1], None, ALU.add)
                        s_v.inc(op, 1)

                    @sec("tensor")
                    def _(e):
                        s_v.wait(e)
                        op = e.matmul(pb[5][:, :], swapi[:], tq_b[:], start=True, stop=True)
                        s_mm.inc(op, 1)

                    @sec("vector")
                    def _(e, grp=grp, dst_sb=dst_sb):
                        s_mm.wait(e)
                        g0 = grp * 512
                        c0 = (grp % 8) * 512
                        e.tensor_tensor(rop[:], tq_b[:], cos_sb[:, c0:c0 + 512], ALU.mult)
                        vf(e, e.scalar_tensor_tensor(rdb_sb[:], pb[5][:, :], 0.0,
                                               sin_sb[:, c0:c0 + 512], ALU.bypass, ALU.mult))
                        vf(e, e.tensor_add(dst_sb[:, g0:g0 + 512], rop[:], rdb_sb[:]))
                        op = e.tensor_mul(sqt[:], dst_sb[:, g0:g0 + 512],
                                          dst_sb[:, g0:g0 + 512])
                        s_v.inc(op, 1)

                    @sec("tensor")
                    def _(e):
                        s_v.wait(e)
                        op = e.matmul(pb[3][0:1, :], ones_k[:], sqt[:], start=True, stop=True)
                        s_mm.inc(op, 1)

                    @sec("vector")
                    def _(e, grp=grp, nrow=nrow):
                        s_mm.wait(e)
                        op = e.reduce_max(nrow[0:1, grp:grp + 1], pb[3][0:1, :], axis=AX.X)
                        s_v.inc(op, 1)

                @sec("vector")
                def _(e, grp=grp):
                    g0 = grp * 512
                    for j in range(4):
                        op = e.tensor_add(v_sb[:, g0 + j * 128:g0 + (j + 1) * 128],
                                          pb[2][:, j * 128:(j + 1) * 128], bvb[:])
                    s_v.inc(op, 1)

            @sec("vector")
            def _(e):
                e.reduce_max(scal[:, 4:5], qn_row[:], axis=AX.X)
                vf(e, e.reduce_max(scal[:, 5:6], kn_row[:], axis=AX.X))
                op = e.tensor_tensor(scal[:, 6:7], scal[:, 4:5], scal[:, 5:6], ALU.mult)
                s_v.inc(op, 1)

            @sec("scalar")
            def _(e):
                s_v.wait(e)
                op = e.activation(scal[:, 7:8], scal[:, 6:7], AF.Sqrt)
                s_sc.inc(op, 1)

            @sec("vector")
            def _(e):
                s_sc.wait(e)
                op = e.tensor_scalar(scal[:, 8:9], scal[:, 7:8], -1.02 * ISQ, -1.0,
                                     ALU.mult, ALU.add)
                s_v.inc(op, 1)

            @sec("tensor")
            def _(e):
                s_v.wait(e)
                op = e.matmul(pb[3][:, 0:1], ones_mf[:], scal[:, 8:9], start=True, stop=True)
                s_mm.inc(op, 1)

            @sec("vector")
            def _(e, h=h):
                s_mm.wait(e)
                op = e.tensor_copy(negb_sb[:, h:h + 1], pb[3][:, 0:1])
                s_v.inc(op, 1)

            if debug:
                @sec("gpsimd")
                def _(e, h=h):
                    e.wait_ge(s_v.h, s_v.n)
                    s_gld.inc(e.dma_start(out=dbg["q"][h], in_=q_sb[:]), 16)
                    s_gld.inc(e.dma_start(out=dbg["k"][h], in_=k_sb[:]), 16)
                    s_gld.inc(e.dma_start(out=dbg["v"][h], in_=v_sb[:]), 16)
                    s_gld.inc(e.dma_start(out=dbg["negb"][h].unsqueeze(1),
                                          in_=negb_sb[:, h:h + 1]), 16)

            barrier()

            for qg in range(16):
                q0 = qg * 512
                nkb = 4 * (qg + 1)
                npr = nkb // 2
                exp_base = s_sc.n
                mask_done = {}
                for step in range(npr + 1):
                    if step < npr:
                        g2 = step
                        A = pq[0] if g2 % 2 == 0 else pq[3]

                        @sec("tensor")
                        def _(e, g2=g2, q0=q0, exp_base=exp_base, A=A):
                            if g2 >= 2:
                                e.wait_ge(s_sc.h, exp_base + g2 - 1)
                            kb = 2 * g2
                            e.matmul(A[:, 0:512], k_sb[:, kb * 128:(kb + 1) * 128],
                                     q_sb[:, q0:q0 + 512], start=True, stop=True)
                            op = e.matmul(A[:, 512:1024],
                                          k_sb[:, (kb + 1) * 128:(kb + 2) * 128],
                                          q_sb[:, q0:q0 + 512], start=True, stop=True)
                            s_mm.inc(op, 1)

                        @sec("scalar")
                        def _(e, g2=g2, h=h, A=A):
                            s_mm.wait(e)
                            op = e.activation(p_sb[g2 % 2][:], A[:, 0:1024], AF.Exp,
                                              bias=negb_sb[:, h:h + 1], scale=ISQ)
                            s_sc.inc(op, 1)

                        for half in range(2):
                            kb = 2 * g2 + half
                            if kb >= 4 * qg:
                                @sec("vector")
                                def _(e, g2=g2, kb=kb, qg=qg, half=half):
                                    s_sc.wait(e)
                                    o = kb - 4 * qg
                                    op = e.tensor_mul(
                                        p_sb[g2 % 2][:, half * 512:(half + 1) * 512],
                                        p_sb[g2 % 2][:, half * 512:(half + 1) * 512],
                                        mask_sb[:, o * 512:(o + 1) * 512])
                                    s_v.inc(op, 1)
                                    mask_done[g2] = s_v.n

                    if step >= 1:
                        g2p = step - 1

                        @sec("tensor")
                        def _(e, g2p=g2p, qg=qg, nkb=nkb, exp_base=exp_base):
                            if g2p in mask_done:
                                e.wait_ge(s_v.h, mask_done[g2p])
                            else:
                                e.wait_ge(s_sc.h, exp_base + g2p + 1)
                            if g2p == 0:
                                s_v.wait(e)
                            for half in range(2):
                                kb = 2 * g2p + half
                                st, sp = kb == 0, kb == nkb - 1
                                e.matmul(pb[2], v_sb[:, kb * 128:(kb + 1) * 128],
                                         p_sb[g2p % 2][:, half * 512:(half + 1) * 512],
                                         start=st, stop=sp)
                                op = e.matmul(pb[3][0:1, :], ones_k[:],
                                              p_sb[g2p % 2][:, half * 512:(half + 1) * 512],
                                              start=st, stop=sp)
                            s_mm.inc(op, 1)

                @sec("vector")
                def _(e):
                    s_mm.wait(e)
                    op = e.reciprocal(rd_sb[:], pb[3][0:1, :])
                    s_v.inc(op, 1)

                @sec("tensor")
                def _(e):
                    s_v.wait(e)
                    op = e.matmul(pb[4], ones_mf[:], rd_sb[:], start=True, stop=True)
                    s_mm.inc(op, 1)

                @sec("vector")
                def _(e, h=h, q0=q0):
                    s_mm.wait(e)
                    vf(e, e.tensor_copy(rdb_sb[:], pb[4]))
                    op = e.tensor_mul(z_sb[h][:, q0:q0 + 512], pb[2], rdb_sb[:])
                    s_v.inc(op, 1)

            if debug:
                @sec("gpsimd")
                def _(e, h=h):
                    e.wait_ge(s_v.h, s_v.n)
                    s_gld.inc(e.dma_start(out=dbg["z"][h], in_=z_sb[h][:]), 16)

            barrier()

        p2q.close()
        po_sb = [sbuf(p2, f"po{i}", [128, D], BF16) for i in range(2)]
        for tt in range(64):
            @sec("tensor")
            def _(e, tt=tt):
                if tt >= 2:
                    s_v.wait(e, s_v.n - 1)
                for mc in range(4):
                    bank = (tt % 2) * 4 + mc
                    for h in range(NHL):
                        op = e.matmul(pb[bank], z_sb[h][:, tt * 128:(tt + 1) * 128],
                                      wo_sb[:, h * D + mc * 512:h * D + (mc + 1) * 512],
                                      start=(h == 0), stop=(h == 1))
                s_mm.inc(op, 1)

            @sec("vector")
            def _(e, tt=tt):
                s_mm.wait(e)
                if tt >= 2:
                    s_ld.wait(e)
                for mc in range(4):
                    op = e.tensor_copy(po_sb[tt % 2][:, mc * 512:(mc + 1) * 512],
                                       pb[(tt % 2) * 4 + mc][:, :])
                s_v.inc(op, 1)

            @sec("sync")
            def _(e, tt=tt):
                s_v.wait(e)
                s_ld.inc(e.dma_start(out=part_src[tt * 128:(tt + 1) * 128, :],
                                     in_=po_sb[tt % 2][:]), 16)

        @sec("gpsimd")
        def _(e):
            e.wait_ge(s_ld.h, s_ld.n)
            op = e.collective_compute(
                "ReduceScatter", ALU.add, replica_groups=[list(range(NCORES))],
                ins=[part_src[:, :]], outs=[rs1_dst[:, :]])
            s_cc.inc(op, 1)

        if debug:
            @sec("gpsimd")
            def _(e):
                s_cc.wait(e)
                s_gld.inc(e.dma_start(out=dbg["rs1"][:, :], in_=rs1_dst[:, :]), 16)

        p2.close()
        barrier()

        p5 = ExitStack()
        xt2 = [sbuf(p5, f"xt2_{i}", [128, D], F32) for i in range(2)]
        at2 = [sbuf(p5, f"at2_{i}", [128, D], BF16) for i in range(2)]
        r2t = [sbuf(p5, f"r2t{i}", [128, D], F32) for i in range(2)]
        xn2f = [sbuf(p5, f"xn2f{i}", [128, D], F32) for i in range(2)]
        xqf = sbuf(p5, "xqf", [128, D], F32)
        xqb = [sbuf(p5, f"xqb{i}", [128, D], BF16) for i in range(2)]
        stage2 = [sbuf(p5, f"stage2_{i}", [128, D], BF16) for i in range(2)]
        xn2fm = sbuf(p5, "xn2fm", [128, 16 * 128], F32)
        mv2 = [sbuf(p5, f"mv2{i}", [128, 2], F32) for i in range(2)]
        ve2 = [sbuf(p5, f"ve2{i}", [128, 4], F32) for i in range(2)]
        qs2 = sbuf(p5, "qs2", [128, 8], F32)
        lg_sb = sbuf(p5, "lg", [128, 8], F32)
        mx8 = sbuf(p5, "mx8", [128, 8], F32)
        mi8 = sbuf(p5, "mi8", [128, 8], U32)
        idxf = sbuf(p5, "idxf", [128, 1], F32)

        for t in range(8):
            @sec("sync")
            def _(e, t=t):
                s_cc.wait(e)
                if t >= 2:
                    e.wait_ge(s_v.h, s_v.n)
                s_ld.inc(e.dma_start(out=xt2[t % 2][:],
                                     in_=x_in[t * 128:(t + 1) * 128, :]), 16)
                s_ld.inc(e.dma_start(out=at2[t % 2][:],
                                     in_=rs1_dst[t * 128:(t + 1) * 128, :]), 16)

            @sec("vector")
            def _(e, t=t):
                s_ld.wait(e)
                if t >= 1:
                    s_sld.wait(e)
                    s_mm.wait(e)
                vf(e, e.tensor_add(r2t[t % 2][:], xt2[t % 2][:], at2[t % 2][:]))
                op = e.tensor_add(r2t[t % 2][:], r2t[t % 2][:], bo_bc[:])
                s_v.inc(op, 1)

            @sec("scalar")
            def _(e, t=t):
                s_v.wait(e)
                op = e.dma_start(out=r2_buf[t * 128:(t + 1) * 128, :], in_=r2t[t % 2][:])
                s_sld.inc(op, 16)

            @sec("vector")
            def _(e, t=t):
                vf(e, e.reduce_sum(ve2[t % 2][:, 3:4], r2t[t % 2][:], axis=AX.X))
                e.tensor_scalar(mv2[t % 2][:, 0:1], ve2[t % 2][:, 3:4], 1.0 / D, None, ALU.mult)
                op = e.tensor_scalar(mv2[t % 2][:, 1:2], ve2[t % 2][:, 3:4], -1.0 / D, None, ALU.mult)
                s_v.inc(op, 1)

            @sec("scalar")
            def _(e, t=t):
                s_v.wait(e)
                op = e.activation(xn2f[t % 2][:], r2t[t % 2][:], AF.Square,
                                  bias=mv2[t % 2][:, 1:2], accum_out=ve2[t % 2][:, 0:1])
                s_sc.inc(op, 1)

            @sec("vector")
            def _(e, t=t):
                s_sc.wait(e)
                op = e.tensor_scalar(ve2[t % 2][:, 0:1], ve2[t % 2][:, 0:1], 1.0 / D, EPS,
                                     ALU.mult, ALU.add)
                s_v.inc(op, 1)

            @sec("scalar")
            def _(e, t=t):
                s_v.wait(e)
                op = e.activation(ve2[t % 2][:, 1:2], ve2[t % 2][:, 0:1], AF.Sqrt)
                s_sc.inc(op, 1)

            @sec("vector")
            def _(e, t=t):
                s_sc.wait(e)
                vf(e, e.reciprocal(ve2[t % 2][:, 2:3], ve2[t % 2][:, 1:2]))
                vf(e, e.tensor_scalar(xn2f[t % 2][:], r2t[t % 2][:], mv2[t % 2][:, 0:1],
                                      ve2[t % 2][:, 2:3], ALU.subtract, ALU.mult))
                vf(e, e.reduce_max(qs2[:, 0:1], xn2f[t % 2][:], axis=AX.X,
                                   apply_absolute_value=True))
                vf(e, e.tensor_scalar(qs2[:, 1:2], qs2[:, 0:1], EPS, None, ALU.max))
                vf(e, e.reciprocal(qs2[:, 2:3], qs2[:, 1:2]))
                e.tensor_scalar(qs2[:, 3:4], qs2[:, 2:3], 127.0, None, ALU.mult)
                vf(e, e.tensor_scalar(qs2[:, 4:5], qs2[:, 1:2], 1.0 / 127.0, None, ALU.mult))
                vf(e, e.tensor_scalar(xqf[:], xn2f[t % 2][:], qs2[:, 3:4], None, ALU.mult))
                vf(e, e.tensor_scalar(xqf[:], xqf[:], MAGIC, -MAGIC, ALU.add, ALU.add))
                op = e.tensor_scalar(xqb[t % 2][:], xqf[:], 127.0, -128.0, ALU.min, ALU.max)
                s_v.inc(op, 1)

            @sec("scalar")
            def _(e, t=t):
                s_v.wait(e)
                op = e.dma_start(out=meta_src[0, t * 128:(t + 1) * 128].unsqueeze(1), in_=qs2[:, 4:5])
                s_sld.inc(op, 16)

            @sec("tensor")
            def _(e, t=t):
                s_v.wait(e)
                bank = 4 + (t % 2) * 2
                for cc in range(16):
                    op = e.transpose(
                        pbb(bank + cc // 8)[:, (cc % 8) * 128:(cc % 8 + 1) * 128],
                        xqb[t % 2][:, cc * 128:(cc + 1) * 128], identb[:])
                s_mm.inc(op, 1)

            @sec("vector")
            def _(e, t=t):
                s_mm.wait(e)
                if t >= 2:
                    s_ld.wait(e)
                bank = 4 + (t % 2) * 2
                e.tensor_copy(stage2[t % 2][:, 0:1024], pbb(bank)[:, 0:1024])
                op = e.tensor_copy(stage2[t % 2][:, 1024:2048], pbb(bank + 1)[:, 0:1024])
                s_v.inc(op, 1)

            @sec("sync")
            def _(e, t=t):
                s_v.wait(e)
                op = e.dma_start(
                    out=cmaj(ag2_src[:, t * 128:(t + 1) * 128]),
                    in_=s3(stage2[t % 2][:], 16))
                s_ld.inc(op, 16)

            @sec("tensor")
            def _(e, t=t):
                for cc4 in range(4):
                    for ccl in range(4):
                        cc = cc4 * 4 + ccl
                        op = e.transpose(pb[cc4][:, ccl * 128:(ccl + 1) * 128],
                                         xn2f[t % 2][:, cc * 128:(cc + 1) * 128], identf[:])
                s_mm.inc(op, 1)

            @sec("vector")
            def _(e, t=t):
                s_mm.wait(e)
                for cc4 in range(4):
                    op = e.tensor_copy(xn2fm[:, cc4 * 512:(cc4 + 1) * 512], pb[cc4][:, :])
                s_v.inc(op, 1)

            @sec("tensor")
            def _(e, t=t):
                s_v.wait(e)
                for cc in range(16):
                    op = e.matmul(pb[6][:, 0:8], xn2fm[:, cc * 128:(cc + 1) * 128],
                                  rwt_sb[:, cc * 8:(cc + 1) * 8],
                                  start=(cc == 0), stop=(cc == 15))
                s_mm.inc(op, 1)

            @sec("vector")
            def _(e, t=t):
                s_mm.wait(e)
                vf(e, e.tensor_copy(lg_sb[:], pb[6][:, 0:8]))
                vf(e, e.max(mx8[:], lg_sb[:]))
                vf(e, e.max_index(mi8[:], mx8[:], lg_sb[:]))
                op = e.tensor_copy(idxf[:], mi8[:, 0:1])
                s_v.inc(op, 1)

            @sec("scalar")
            def _(e, t=t):
                s_v.wait(e)
                op = e.dma_start(out=meta_src[1, t * 128:(t + 1) * 128].unsqueeze(1), in_=idxf[:])
                s_sld.inc(op, 16)

        @sec("gpsimd")
        def _(e):
            e.wait_ge(s_ld.h, s_ld.n)
            e.wait_ge(s_sld.h, s_sld.n)
            op = e.collective_compute(
                "AllGather", ALU.bypass, replica_groups=[list(range(NCORES))],
                ins=[meta_src[:, :]], outs=[meta_dst[:, :, :]])
            s_cc.inc(op, 1)
            op = e.collective_compute(
                "AllGather", ALU.bypass, replica_groups=[list(range(NCORES))],
                ins=[ag2_src[:, :]], outs=[ag2_dst[:, :, :]])
            s_cc.inc(op, 1)

        if debug:
            @sec("gpsimd")
            def _(e):
                s_cc.wait(e)
                s_gld.inc(e.dma_start(out=dbg["ag2"][:, :, :], in_=ag2_dst[:, :, :]), 16)
                s_gld.inc(e.dma_start(out=dbg["meta"][:, :, :], in_=meta_dst[:, :, :]), 16)
                s_gld.inc(e.dma_start(out=dbg["r2"][:, :], in_=r2_buf[:, :]), 16)

        p5.close()
        barrier()

        pf = ExitStack()
        act_sg = sbuf(pf, "act_sg", [128, 16 * 1024], BF16)
        h_sg = sbuf(pf, "h_sg", [128, 16 * 1024], F32)
        wblk = [sbuf(pf, f"wblk{i}", [128, 16 * 512], BF16) for i in range(2)]
        sxg = sbuf(pf, "sxg", [128, 1024], F32)
        sxu = sbuf(pf, "sxu", [128, 1024], F32)
        shb = sbuf(pf, "shb", [128, 1024], F32)
        hmax = sbuf(pf, "hmax", [128, 1024], F32)
        srow = sbuf(pf, "srow", [1, 1024], F32)
        irow = sbuf(pf, "irow", [1, 1024], F32)
        srow2 = sbuf(pf, "srow2", [1, 1024], F32)
        hmcl = sbuf(pf, "hmcl", [128, 8], F32)
        invsh = sbuf(pf, "invsh", [128, 8], F32)
        comb = sbuf(pf, "comb", [128, 8], F32)
        idxt = sbuf(pf, "idxt", [128, 8], F32)
        gtile = [sbuf(pf, f"gtile{i}", [128, 512], F32) for i in range(2)]
        utile = [sbuf(pf, f"utile{i}", [128, 512], F32) for i in range(2)]
        hmax2 = sbuf(pf, "hmax2", [128, 1024], F32)
        y_sb = [sbuf(pf, f"y{i}", [128, D], BF16) for i in range(1)]

        ybv = h_sg[:].bitcast(BF16)
        wblk_free = [0, 0]
        wd_tgt = [0, 0, 0, 0]
        ymm_done = [0, 0, 0, 0]
        gld_tgt = [0, 0, 0, 0]
        uld_tgt = [0, 0, 0, 0]

        for sg in range(8):
            @sec("sync")
            def _(e, sg=sg):
                s_cc.wait(e)
                if sg > 0:
                    e.wait_ge(s_mm.h, s_mm.n)
                op = e.dma_start(out=s3(act_sg[:], 16),
                                 in_=cmaj(ag2_dst[sg]))
                s_ld.inc(op, 16)
                op = e.dma_start(out=srow[:], in_=meta_dst[sg, 0].unsqueeze(0))
                s_ld.inc(op, 16)
                op = e.dma_start(out=irow[:], in_=meta_dst[sg, 1].unsqueeze(0))
                s_ld.inc(op, 16)

            @sec("tensor")
            def _(e, sg=sg):
                s_ld.wait(e)
                if sg > 0:
                    e.wait_ge(s_v.h, s_v.n)
                for j in range(2):
                    op = e.matmul(pb[6], ones_mf[:], srow[0:1, j * 512:(j + 1) * 512],
                                  start=True, stop=True)
                    s_mm.inc(op, 1)
                    e.wait_ge(s_v.h, s_v.n + j + 1)

            @sec("vector")
            def _(e, sg=sg):
                if sg > 0:
                    s_ld.wait(e)
                base = s_mm.n - 2
                e.memset(hmax[:], 0.0)
                e.memset(hmax2[:], 0.0)
                for j in range(2):
                    s_mm.wait(e, base + j + 1)
                    e.tensor_scalar(sxg[:, j * 512:(j + 1) * 512], pb[6],
                                    winv[:, 0:1], None, ALU.mult)
                    op = e.tensor_scalar(sxu[:, j * 512:(j + 1) * 512], pb[6],
                                         winv[:, 1:2], None, ALU.mult)
                    s_v.inc(op, 1)

            @sec("scalar")
            def _(e, sg=sg):
                e.wait_ge(s_mm.h, wblk_free[0])
                op = e.dma_start(out=s3(wblk[0][:], 16), in_=cmaj(wqt["g"][0]))
                s_sld.inc(op, 16)
                gld_tgt[0] = s_sld.n
                e.wait_ge(s_mm.h, wblk_free[1])
                op = e.dma_start(out=s3(wblk[1][:], 16), in_=cmaj(wqt["u"][0]))
                s_sld.inc(op, 16)
                uld_tgt[0] = s_sld.n

            for ds in range(4):
                for j in range(2):
                    @sec("tensor")
                    def _(e, ds=ds, j=j):
                        e.wait_ge(s_sld.h, gld_tgt[ds])
                        s_v.wait(e)
                        if ds == 0 and j == 0:
                            s_ld.wait(e)
                        for hc in range(16):
                            st, sp = hc == 0, hc == 15
                            for dc in range(4):
                                op = e.matmul(pb[dc],
                                              wblk[0][:, hc * 512 + dc * 128:hc * 512 + (dc + 1) * 128],
                                              act_sg[:, hc * 1024 + j * 512:hc * 1024 + (j + 1) * 512],
                                              start=st, stop=sp)
                        s_mm.inc(op, 1)
                        if j == 1:
                            wblk_free[0] = s_mm.n

                    @sec("tensor")
                    def _(e, ds=ds, j=j):
                        e.wait_ge(s_sld.h, uld_tgt[ds])
                        for hc in range(16):
                            st, sp = hc == 0, hc == 15
                            for dc in range(4):
                                op = e.matmul(pb[4 + dc],
                                              wblk[1][:, hc * 512 + dc * 128:hc * 512 + (dc + 1) * 128],
                                              act_sg[:, hc * 1024 + j * 512:hc * 1024 + (j + 1) * 512],
                                              start=st, stop=sp)
                        s_mm.inc(op, 1)
                        if j == 1:
                            wblk_free[1] = s_mm.n

                    @sec("vector")
                    def _(e, ds=ds, j=j):
                        s_mm.wait(e)
                        hm = (hmax, hmax2)
                        for dp in range(2):
                            dcs = (2 * dp, 2 * dp + 1)
                            for i, dc in enumerate(dcs):
                                e.tensor_tensor(gtile[i][:], pb[dc],
                                                sxg[:, j * 512:(j + 1) * 512], ALU.mult)
                            for i, dc in enumerate(dcs):
                                e.tensor_tensor(utile[i][:], pb[4 + dc],
                                                sxu[:, j * 512:(j + 1) * 512], ALU.mult)
                            for i, dc in enumerate(dcs):
                                e.scalar_tensor_tensor(gtile[i][:], gtile[i][:], 0.0,
                                                       gtile[i][:], ALU.max, ALU.mult)
                            for i, dc in enumerate(dcs):
                                col = (ds * 4 + dc) * 1024 + j * 512
                                e.tensor_tensor(h_sg[:, col:col + 512], gtile[i][:],
                                                utile[i][:], ALU.mult)
                            for i, dc in enumerate(dcs):
                                col = (ds * 4 + dc) * 1024 + j * 512
                                e.tensor_mul(gtile[i][:], h_sg[:, col:col + 512],
                                             h_sg[:, col:col + 512])
                            for i, dc in enumerate(dcs):
                                op = e.tensor_max(hm[i][:, j * 512:(j + 1) * 512],
                                                  hm[i][:, j * 512:(j + 1) * 512],
                                                  gtile[i][:])
                        s_v.inc(op, 1)

                if ds < 3:
                    @sec("scalar")
                    def _(e, ds=ds, sg=sg):
                        e.wait_ge(s_mm.h, wblk_free[0])
                        op = e.dma_start(out=s3(wblk[0][:], 16), in_=cmaj(wqt["g"][ds + 1]))
                        s_sld.inc(op, 16)
                        gld_tgt[ds + 1] = s_sld.n
                        e.wait_ge(s_mm.h, wblk_free[1])
                        op = e.dma_start(out=s3(wblk[1][:], 16), in_=cmaj(wqt["u"][ds + 1]))
                        s_sld.inc(op, 16)
                        uld_tgt[ds + 1] = s_sld.n

            @sec("vector")
            def _(e):
                e.wait_ge(s_v.h, s_v.n)
                op = e.tensor_max(hmax[:], hmax[:], hmax2[:])
                vf(e, op)

            @sec("tensor")
            def _(e):
                s_v.wait(e)
                for w in range(8):
                    bank = pb[5] if w < 4 else pb[4]
                    op = e.transpose(bank[:, (w % 4) * 128:(w % 4 + 1) * 128],
                                     hmax[:, w * 128:(w + 1) * 128], identf[:])
                s_mm.inc(op, 1)

            @sec("vector")
            def _(e):
                s_mm.wait(e)
                for w in range(8):
                    bank = pb[5] if w < 4 else pb[4]
                    op = e.reduce_max(hmcl[:, w:w + 1],
                                      bank[:, (w % 4) * 128:(w % 4 + 1) * 128], axis=AX.X)
                s_v.inc(op, 1)

            @sec("scalar")
            def _(e):
                s_v.wait(e)
                op = e.activation(hmcl[:], hmcl[:], AF.Sqrt)
                s_sc.inc(op, 1)

            @sec("vector")
            def _(e):
                s_sc.wait(e)
                vf(e, e.tensor_scalar(hmcl[:], hmcl[:], EPS, None, ALU.max))
                op = e.tensor_scalar(invsh[:], hmcl[:], 1.0 / 127.0, None, ALU.mult)
                s_v.inc(op, 1)

            for w in range(8):
                @sec("tensor")
                def _(e, w=w):
                    s_v.wait(e)
                    op = e.transpose(pb[6 + w // 4][0:1, (w % 4) * 128:(w % 4 + 1) * 128],
                                     hmcl[:, w:w + 1], identf[:])
                    s_mm.inc(op, 1)

            @sec("vector")
            def _(e):
                s_mm.wait(e)
                e.tensor_copy(srow2[0:1, 0:512], pb[6][0:1, 0:512])
                vf(e, e.tensor_copy(srow2[0:1, 512:1024], pb[7][0:1, 0:512]))
                vf(e, e.reciprocal(srow2[:], srow2[:]))
                op = e.tensor_scalar(srow2[:], srow2[:], 127.0, None, ALU.mult)
                s_v.inc(op, 1)

            @sec("tensor")
            def _(e):
                s_v.wait(e)
                for j in range(2):
                    op = e.matmul(pb[7], ones_mf[:], srow2[0:1, j * 512:(j + 1) * 512],
                                  start=True, stop=True)
                    s_mm.inc(op, 1)
                    e.wait_ge(s_v.h, s_v.n + j + 1)

            @sec("vector")
            def _(e):
                base = s_mm.n - 2
                for j in range(2):
                    s_mm.wait(e, base + j + 1)
                    op = e.tensor_copy(shb[:, j * 512:(j + 1) * 512], pb[7])
                    s_v.inc(op, 1)

            @sec("vector")
            def _(e):
                for ch in range(16):
                    for j in range(2):
                        col = ch * 1024 + j * 512
                        e.tensor_tensor(gtile[j][:], h_sg[:, col:col + 512],
                                        shb[:, j * 512:(j + 1) * 512], ALU.mult)
                    for j in range(2):
                        e.tensor_scalar(gtile[j][:], gtile[j][:], MAGIC, -MAGIC,
                                        ALU.add, ALU.add)
                    for j in range(2):
                        col = ch * 1024 + j * 512
                        op = e.tensor_scalar(act_sg[:, col:col + 512], gtile[j][:],
                                             127.0, -128.0, ALU.min, ALU.max)
                s_v.inc(op, 1)

            @sec("tensor")
            def _(e):
                s_v.wait(e)
                s_ld.wait(e)
                for w in range(8):
                    op = e.transpose(pb[4][:, w * 4:w * 4 + 1],
                                     irow[0:1, w * 128:(w + 1) * 128], identf[0:1, 0:1])
                s_mm.inc(op, 1)

            @sec("vector")
            def _(e):
                s_mm.wait(e)
                for w in range(8):
                    e.tensor_copy(idxt[:, w:w + 1], pb[4][:, w * 4:w * 4 + 1])
                for w in range(8):
                    e.tensor_tensor(comb[:, w:w + 1], idxt[:, w:w + 1], eidb[:],
                                    ALU.is_equal)
                for w in range(8):
                    e.tensor_tensor(comb[:, w:w + 1], comb[:, w:w + 1],
                                    invsh[:, w:w + 1], ALU.mult)
                for w in range(8):
                    op = e.tensor_scalar(comb[:, w:w + 1], comb[:, w:w + 1],
                                         winv[:, 2:3], None, ALU.mult)
                s_v.inc(op, 1)

            @sec("scalar")
            def _(e, sg=sg):
                for hfg in range(2):
                    e.wait_ge(s_mm.h, wblk_free[hfg])
                    op = e.dma_start(out=s3(wblk[hfg][:], 16), in_=cmaj(wqt["d"][hfg]))
                    s_sld.inc(op, 16)
                    wd_tgt[hfg] = s_sld.n

            for hfg in range(4):
                @sec("tensor")
                def _(e, hfg=hfg):
                    e.wait_ge(s_sld.h, wd_tgt[hfg])
                    s_v.wait(e)
                    for tt in range(8):
                        for dc in range(16):
                            op = e.matmul(pb[tt],
                                          act_sg[:, dc * 1024 + tt * 128:dc * 1024 + (tt + 1) * 128],
                                          wblk[hfg % 2][:, dc * 512:(dc + 1) * 512],
                                          start=(dc == 0), stop=(dc == 15))
                    s_mm.inc(op, 1)
                    ymm_done[hfg] = s_mm.n

                if hfg < 2:
                    @sec("scalar")
                    def _(e, hfg=hfg):
                        e.wait_ge(s_mm.h, ymm_done[hfg])
                        op = e.dma_start(out=s3(wblk[hfg][:], 16), in_=cmaj(wqt["d"][hfg + 2]))
                        s_sld.inc(op, 16)
                        wd_tgt[hfg + 2] = s_sld.n

                @sec("vector")
                def _(e, hfg=hfg):
                    s_mm.wait(e)
                    for tt in range(8):
                        op = e.tensor_scalar(
                            ybv[:, tt * 2048 + hfg * 512:tt * 2048 + (hfg + 1) * 512],
                            pb[tt], comb[:, tt:tt + 1], None, ALU.mult)
                    s_v.inc(op, 1)

            @sec("sync")
            def _(e, sg=sg):
                s_v.wait(e)
                for tt in range(8):
                    s_ld.inc(e.dma_start(
                        out=y_src[sg * 1024 + tt * 128:sg * 1024 + (tt + 1) * 128, :],
                        in_=ybv[:, tt * 2048:(tt + 1) * 2048]), 16)

            wblk_free[0] = ymm_done[2]
            wblk_free[1] = ymm_done[3]

        @sec("gpsimd")
        def _(e):
            e.wait_ge(s_ld.h, s_ld.n)
            op = e.collective_compute(
                "ReduceScatter", ALU.add, replica_groups=[list(range(NCORES))],
                ins=[y_src[:, :]], outs=[rs2_dst[:, :]])
            s_cc.inc(op, 1)

        if debug:
            @sec("gpsimd")
            def _(e):
                s_cc.wait(e)
                s_gld.inc(e.dma_start(out=dbg["ysrc"][:, :], in_=y_src[0:TC, :]), 16)
                s_gld.inc(e.dma_start(out=dbg["rs2"][:, :], in_=rs2_dst[:, :]), 16)

        pf.close()
        barrier()

        p8 = ExitStack()
        f1 = [sbuf(p8, f"f1_{i}", [128, D], F32) for i in range(2)]
        f2 = [sbuf(p8, f"f2_{i}", [128, D], BF16) for i in range(2)]
        fo = [sbuf(p8, f"fo_{i}", [128, D], F32) for i in range(2)]
        for t in range(8):
            @sec("sync")
            def _(e, t=t):
                s_cc.wait(e)
                if t >= 2:
                    e.wait_ge(s_v.h, s_v.n)
                s_ld.inc(e.dma_start(out=f1[t % 2][:],
                                     in_=r2_buf[t * 128:(t + 1) * 128, :]), 16)
                s_ld.inc(e.dma_start(out=f2[t % 2][:],
                                     in_=rs2_dst[t * 128:(t + 1) * 128, :]), 16)

            @sec("vector")
            def _(e, t=t):
                s_ld.wait(e)
                if t >= 2:
                    s_sld.wait(e)
                op = e.tensor_add(fo[t % 2][:], f1[t % 2][:], f2[t % 2][:])
                s_v.inc(op, 1)

            @sec("scalar")
            def _(e, t=t):
                s_v.wait(e)
                op = e.dma_start(out=out[t * 128:(t + 1) * 128, :], in_=fo[t % 2][:])
                s_sld.inc(op, 16)

        @sec("scalar")
        def _(e):
            e.wait_ge(s_sld.h, s_sld.n)
        @sec("sync")
        def _(e):
            e.wait_ge(s_sld.h, s_sld.n)

        p8.close()

    return nc, dbg



def _host_inputs(inputs):
    x = np.ascontiguousarray(np.asarray(inputs["L0_attn_normalized_resid_pre"], np.float32)[0])
    WQ = np.asarray(inputs["L0_attn_W_Q"], np.float32)
    WK = np.asarray(inputs["L0_attn_W_K"], np.float32)
    WV = np.asarray(inputs["L0_attn_W_V"], np.float32)
    WO = np.asarray(inputs["L0_attn_W_O"], np.float32)
    bQ = np.asarray(inputs["L0_attn_b_Q"], np.float32)
    bK = np.asarray(inputs["L0_attn_b_K"], np.float32)
    bV = np.asarray(inputs["L0_attn_b_V"], np.float32)
    bO = np.asarray(inputs["L0_attn_b_O"], np.float32)
    RW = np.asarray(inputs["L0_ffn_router_w"], np.float32)
    WG = np.asarray(inputs["L0_ffn_wg"], np.float32)
    WU = np.asarray(inputs["L0_ffn_wu"], np.float32)
    WD = np.asarray(inputs["L0_ffn_wd"], np.float32)

    perm = np.concatenate([np.arange(0, 128, 2), np.arange(1, 128, 2)])
    inv_freq = 1.0 / (10000.0 ** (np.arange(0, 128, 2, dtype=np.float32) / 128.0))
    m = np.arange(T, dtype=np.float32)
    freqs = np.outer(m, inv_freq)
    cos64 = np.cos(freqs).T.astype(np.float32)
    sin64 = np.sin(freqs).T.astype(np.float32)
    cos_t = np.ascontiguousarray(np.concatenate([cos64, cos64], axis=0))
    sin_t = np.ascontiguousarray(np.concatenate([-sin64, sin64], axis=0))
    swapi = np.ascontiguousarray(np.roll(np.eye(128, dtype=ml_dtypes.bfloat16), 64, axis=0))

    masks = np.zeros((4, 128, 512), dtype=ml_dtypes.bfloat16)
    for o in range(4):
        kk = np.arange(128)[:, None] + 128 * o
        qq = np.arange(512)[None, :]
        masks[o] = (qq >= kk).astype(ml_dtypes.bfloat16)

    common = {
        "rw": RW, "cos": cos_t, "sin": sin_t, "masks": masks,
        "identb": np.eye(128, dtype=ml_dtypes.bfloat16),
        "identf": np.eye(128, dtype=np.float32),
        "ones_k": np.ones((128, 1), ml_dtypes.bfloat16),
        "ones_m": np.ones((1, 128), ml_dtypes.bfloat16),
        "ones_mf": np.ones((1, 128), np.float32),
        "ones_kf": np.ones((128, 1), np.float32),
        "swapi": swapi,
        "bo": np.ascontiguousarray(bO[None, :]),
    }
    in_maps = []
    for c in range(NCORES):
        hs = slice(2 * c, 2 * c + 2)
        d = dict(common)
        d.update({
            "x": np.ascontiguousarray(x[c * TC:(c + 1) * TC]),
            "wq": np.ascontiguousarray(WQ[hs][:, :, perm]),
            "wk": np.ascontiguousarray(WK[hs][:, :, perm]),
            "wv": np.ascontiguousarray(WV[hs]),
            "wo": np.ascontiguousarray(WO[hs]),
            "bq": np.ascontiguousarray(bQ[hs][:, perm]),
            "bk": np.ascontiguousarray(bK[hs][:, perm]),
            "bv": np.ascontiguousarray(bV[hs]),
            "wg": np.ascontiguousarray(WG[c]),
            "wu": np.ascontiguousarray(WU[c]),
            "wd": np.ascontiguousarray(WD[c]),
            "eid": np.full((1, 1), float(c), np.float32),
        })
        in_maps.append(d)
    return in_maps


def kernel(**inputs):
    in_maps = _host_inputs(inputs)
    nc, _ = build_program(debug=DEBUG)
    res = run_bass_kernel_spmd(nc, in_maps, core_ids=list(range(NCORES)))
    outv = np.concatenate([np.asarray(r["out"]) for r in res.results], axis=0)
    return outv[None, :, :].astype(np.float32)


# version 09
# speedup vs baseline: 1.4251x; 1.0347x over previous
import numpy as np
import ml_dtypes
from contextlib import ExitStack
import concourse.bass as bass
import concourse.mybir as mybir
from concourse.bass_utils import run_bass_kernel_spmd

F32 = mybir.dt.float32
BF16 = mybir.dt.bfloat16
U32 = mybir.dt.uint32
AF = mybir.ActivationFunctionType
ALU = mybir.AluOpType
AX = mybir.AxisListType

MAGIC = 1.5 * 2.0 ** 23
NCORES = 8
T = 8192
TC = T // NCORES
D = 2048
NHL = 2
EPS = 1e-5
ISQ = float(1.0 / np.sqrt(128.0))

DEBUG = False


class S:
    def __init__(self, stack, nc, name):
        self.h = stack.enter_context(nc.semaphore(name))
        self.n = 0

    def inc(self, instr, k=1):
        instr.then_inc(self.h, k)
        self.n += k
        return self.n

    def wait(self, eng, val=None):
        eng.wait_ge(self.h, self.n if val is None else val)


def build_program(debug=False):
    nc = bass.Bass("TRN2", target_bir_lowering=False, debug=False)

    def din(name, shape, dt=F32):
        return nc.dram_tensor(name, shape, dt, kind="ExternalInput").ap()

    x_in = din("x", [TC, D])
    wq_in = din("wq", [NHL, D, 128])
    wk_in = din("wk", [NHL, D, 128])
    wv_in = din("wv", [NHL, D, 128])
    wo_in = din("wo", [NHL, 128, D])
    bq_in = din("bq", [NHL, 128])
    bk_in = din("bk", [NHL, 128])
    bv_in = din("bv", [NHL, 128])
    bo_in = din("bo", [1, D])
    rw_in = din("rw", [NCORES, D])
    wg_in = din("wg", [D, D])
    wu_in = din("wu", [D, D])
    wd_in = din("wd", [D, D])
    cos_in = din("cos", [128, T])
    sin_in = din("sin", [128, T])
    masks_in = din("masks", [4, 128, 512], BF16)
    identb_in = din("identb", [128, 128], BF16)
    identf_in = din("identf", [128, 128], F32)
    ones_k_in = din("ones_k", [128, 1], BF16)
    ones_m_in = din("ones_m", [1, 128], BF16)
    ones_mf_in = din("ones_mf", [1, 128], F32)
    ones_kf_in = din("ones_kf", [128, 1], F32)
    eid_in = din("eid", [1, 1], F32)
    magic_in = din("magic", [128, 2], F32)
    swapi_in = din("swapi", [128, 128], BF16)

    out = nc.dram_tensor("out", [TC, D], F32, kind="ExternalOutput").ap()

    def dint(name, shape, dt):
        return nc.dram_tensor(name, shape, dt, kind="Internal").ap()

    ag1_src = dint("ag1_src", [D, TC], BF16)
    ag1_dst = nc.dram_tensor("ag1_dst", [NCORES, D, TC], BF16, kind="Internal", addr_space="Shared").ap()
    part_src = dint("part_src", [T, D], BF16)
    rs1_dst = dint("rs1_dst", [TC, D], BF16)
    r2_buf = dint("r2_buf", [TC, D], F32)
    ag2_src = dint("ag2_src", [D, TC], BF16)
    ag2_dst = nc.dram_tensor("ag2_dst", [NCORES, D, TC], BF16, kind="Internal", addr_space="Shared").ap()
    meta_src = dint("meta_src", [2, TC], F32)
    meta_dst = dint("meta_dst", [NCORES, 2, TC], F32)
    wqt = {m: dint(f"wqt_{m}", [4, D, 512], BF16) for m in "gud"}
    y_src = dint("y_src", [T, D], BF16)
    rs2_dst = dint("rs2_dst", [TC, D], BF16)

    dbg = {}
    if debug:
        def dout(name, shape, dt):
            dbg[name] = nc.dram_tensor("dbg_" + name, shape, dt, kind="ExternalOutput").ap()
        dout("ag1", [NCORES, D, TC], BF16)
        dout("ag1s", [D, TC], BF16)
        dout("q", [NHL, 128, T], BF16)
        dout("k", [NHL, 128, T], BF16)
        dout("v", [NHL, 128, T], BF16)
        dout("negb", [NHL, 128], F32)
        dout("z", [NHL, 128, T], BF16)
        dout("rs1", [TC, D], BF16)
        dout("r2", [TC, D], F32)
        dout("ag2", [NCORES, D, TC], BF16)
        dout("meta", [NCORES, 2, TC], F32)
        dout("wqtg", [4, D, 512], BF16)
        dout("wqtd", [4, D, 512], BF16)
        dout("winv", [128, 4], F32)
        dout("ysrc", [TC, D], BF16)
        dout("rs2", [TC, D], BF16)

    stack = ExitStack()
    with stack:
        def sbuf(st, name, shape, dt):
            return st.enter_context(nc.sbuf_tensor(name, shape, dt))

        def sem(name):
            return S(stack, nc, name)

        pq = [stack.enter_context(nc.psum_tensor(f"pq{i}", [128, 1024], F32))
              for i in range(4)]
        pb = [pq[i // 2][:, (i % 2) * 512:(i % 2 + 1) * 512] for i in range(8)]

        def pbb(i):
            return pb[i].bitcast(BF16)

        def cmaj(ap2d, p=128):
            c = ap2d.shape[0] // p
            return ap2d.rearrange("(c p) q -> c p q", p=p).transpose([1, 0, 2])

        def s3(ap2d, c):
            return ap2d.rearrange("p (c q) -> p c q", c=c)

        identb = sbuf(stack, "identb_s", [128, 128], BF16)
        identf = sbuf(stack, "identf_s", [128, 128], F32)
        ones_k = sbuf(stack, "ones_k_s", [128, 1], BF16)
        ones_m = sbuf(stack, "ones_m_s", [1, 128], BF16)
        ones_mf = sbuf(stack, "ones_mf_s", [1, 128], F32)
        ones_kf = sbuf(stack, "ones_kf_s", [128, 1], F32)
        mask_sb = sbuf(stack, "mask_s", [128, 4 * 512], BF16)
        bqkv = sbuf(stack, "bqkv", [128, 6], F32)
        bvrow = sbuf(stack, "bvrow", [1, 2 * 128], F32)
        bo_bc = sbuf(stack, "bo_bc", [128, D], F32)
        rwt_sb = sbuf(stack, "rwt", [128, 16 * 8], F32)
        wo_sb = sbuf(stack, "wo_s", [128, NHL * D], BF16)
        negb_sb = sbuf(stack, "negb_s", [128, NHL], F32)
        scal = sbuf(stack, "scal", [1, 16], F32)
        winv = sbuf(stack, "winv_s", [128, 4], F32)
        eid_sb = sbuf(stack, "eid_s", [1, 1], F32)
        eidb = sbuf(stack, "eidb", [128, 1], F32)
        magic_sb = sbuf(stack, "magic_s", [128, 2], F32)
        p0 = ExitStack()
        bo_row = sbuf(p0, "bo_row_s", [1, D], F32)
        rw_sb = sbuf(p0, "rw_s", [NCORES, D], F32)

        s_ld = sem("s_ld")
        s_gld = sem("s_gld")
        s_sld = sem("s_sld")
        s_mm = sem("s_mm")
        s_v = sem("s_v")
        s_sc = sem("s_sc")
        s_cc = sem("s_cc")
        s_bar = sem("s_bar")

        blk = stack.enter_context(nc.Block())

        def sec(engname):
            def deco(f):
                getattr(blk, engname)(f)
            return deco

        def vf(e, op):
            s_v.inc(op, 1)
            s_v.wait(e)

        ENGS = ("sync", "scalar", "vector", "tensor", "gpsimd")

        def barrier():
            for en in ENGS:
                @sec(en)
                def _(e):
                    e.sem_inc(s_bar.h, 1)
            s_bar.n += len(ENGS)
            tgt = s_bar.n
            for en in ENGS:
                @sec(en)
                def _(e):
                    e.wait_ge(s_bar.h, tgt)

        @sec("sync")
        def _(e):
            for dst, src in ((magic_sb, magic_in), (identb, identb_in), (identf, identf_in),
                             (ones_k, ones_k_in), (ones_m, ones_m_in),
                             (ones_mf, ones_mf_in), (ones_kf, ones_kf_in),
                             (rw_sb, rw_in), (bo_row, bo_in), (eid_sb, eid_in)):
                s_ld.inc(e.dma_start(out=dst[:], in_=src[:, :]), 16)
            s_ld.inc(e.dma_start(out=s3(mask_sb[:], 4),
                                 in_=masks_in.transpose([1, 0, 2])), 16)
            for h in range(NHL):
                for j, b in ((0, bq_in), (2, bk_in), (4, bv_in)):
                    s_ld.inc(e.dma_start(out=bqkv[:, j + h:j + h + 1],
                                         in_=b[h].unsqueeze(1)), 16)
                s_ld.inc(e.dma_start(out=bvrow[0:1, h * 128:(h + 1) * 128],
                                     in_=bv_in[h].unsqueeze(0)), 16)

        @sec("gpsimd")
        def _(e):
            for h in range(NHL):
                s_gld.inc(e.dma_start(out=wo_sb[:, h * D:(h + 1) * D], in_=wo_in[h]), 16)

        @sec("tensor")
        def _(e):
            s_ld.wait(e)
            for cc in range(16):
                op = e.transpose(pb[0][:, cc * 8:(cc + 1) * 8],
                                 rw_sb[:, cc * 128:(cc + 1) * 128], identf[0:8, 0:8])
            s_mm.inc(op, 1)
            op = e.matmul(pb[2][:, 0:1], ones_mf[:], eid_sb[:], start=True, stop=True)
            s_mm.inc(op, 1)
            vbase = s_v.n
            for mc in range(4):
                op = e.matmul(pb[1], ones_mf[:], bo_row[:, mc * 512:(mc + 1) * 512],
                              start=True, stop=True)
                s_mm.inc(op, 1)
                e.wait_ge(s_v.h, vbase + mc + 1)

        @sec("vector")
        def _(e):
            s_mm.wait(e, s_mm.n - 4)
            e.tensor_copy(rwt_sb[:], pb[0][:, 0:128])
            e.tensor_copy(eidb[:], pb[2][:, 0:1])
            for mc in range(4):
                s_mm.wait(e, s_mm.n - 3 + mc)
                op = e.tensor_copy(bo_bc[:, mc * 512:(mc + 1) * 512], pb[1])
                s_v.inc(op, 1)

        barrier()
        p0.close()

        p1 = ExitStack()
        xt = [sbuf(p1, f"xt{i}", [128, D], F32) for i in range(2)]
        xn1b = [sbuf(p1, f"xn1b{i}", [128, D], BF16) for i in range(2)]
        stage1 = [sbuf(p1, f"stage1_{i}", [128, D], BF16) for i in range(2)]
        mv_sb = [sbuf(p1, f"mv{i}", [128, 2], F32) for i in range(2)]
        ve_sb = [sbuf(p1, f"ve{i}", [128, 4], F32) for i in range(2)]

        for t in range(8):
            @sec("sync")
            def _(e, t=t):
                if t >= 2:
                    e.wait_ge(s_v.h, s_v.n)
                s_ld.inc(e.dma_start(out=xt[t % 2][:],
                                     in_=x_in[t * 128:(t + 1) * 128, :]), 16)

            @sec("vector")
            def _(e, t=t):
                s_ld.wait(e)
                if t >= 2:
                    s_mm.wait(e)
                vf(e, e.reduce_sum(ve_sb[t % 2][:, 3:4], xt[t % 2][:], axis=AX.X))
                e.tensor_scalar(mv_sb[t % 2][:, 0:1], ve_sb[t % 2][:, 3:4], 1.0 / D, None, ALU.mult)
                op = e.tensor_scalar(mv_sb[t % 2][:, 1:2], ve_sb[t % 2][:, 3:4], -1.0 / D, None, ALU.mult)
                s_v.inc(op, 1)

            @sec("scalar")
            def _(e, t=t):
                s_v.wait(e)
                op = e.activation(xn1b[t % 2][:], xt[t % 2][:], AF.Square,
                                  bias=mv_sb[t % 2][:, 1:2], accum_out=ve_sb[t % 2][:, 0:1])
                s_sc.inc(op, 1)

            @sec("vector")
            def _(e, t=t):
                s_sc.wait(e)
                op = e.tensor_scalar(ve_sb[t % 2][:, 0:1], ve_sb[t % 2][:, 0:1], 1.0 / D, EPS,
                                     ALU.mult, ALU.add)
                s_v.inc(op, 1)

            @sec("scalar")
            def _(e, t=t):
                s_v.wait(e)
                op = e.activation(ve_sb[t % 2][:, 1:2], ve_sb[t % 2][:, 0:1], AF.Sqrt)
                s_sc.inc(op, 1)

            @sec("vector")
            def _(e, t=t):
                s_sc.wait(e)
                vf(e, e.reciprocal(ve_sb[t % 2][:, 2:3], ve_sb[t % 2][:, 1:2]))
                op = e.tensor_scalar(xn1b[t % 2][:], xt[t % 2][:], mv_sb[t % 2][:, 0:1],
                                     ve_sb[t % 2][:, 2:3], ALU.subtract, ALU.mult)
                s_v.inc(op, 1)

            @sec("tensor")
            def _(e, t=t):
                s_v.wait(e)
                bank = (t % 2) * 2
                for cc in range(16):
                    op = e.transpose(
                        pbb(bank + cc // 8)[:, (cc % 8) * 128:(cc % 8 + 1) * 128],
                        xn1b[t % 2][:, cc * 128:(cc + 1) * 128], identb[:])
                s_mm.inc(op, 1)

            @sec("vector")
            def _(e, t=t):
                s_mm.wait(e)
                if t >= 2:
                    s_ld.wait(e)
                bank = (t % 2) * 2
                e.tensor_copy(stage1[t % 2][:, 0:1024], pbb(bank)[:, 0:1024])
                op = e.tensor_copy(stage1[t % 2][:, 1024:2048], pbb(bank + 1)[:, 0:1024])
                s_v.inc(op, 1)

            @sec("sync")
            def _(e, t=t):
                s_v.wait(e)
                op = e.dma_start(
                    out=cmaj(ag1_src[:, t * 128:(t + 1) * 128]),
                    in_=s3(stage1[t % 2][:], 16))
                s_ld.inc(op, 16)

        @sec("gpsimd")
        def _(e):
            e.wait_ge(s_ld.h, s_ld.n)
            op = e.collective_compute(
                "AllGather", ALU.bypass, replica_groups=[list(range(NCORES))],
                ins=[ag1_src[:, :]], outs=[ag1_dst[:, :, :]])
            s_cc.inc(op, 1)

        wst = ExitStack()
        wf = [sbuf(wst, f"wf{i}", [128, D], F32) for i in range(4)]
        wqrow2 = None
        wqrow = [sbuf(wst, f"wqrow{i}", [128, D], BF16) for i in range(2)]
        wacc = sbuf(wst, "wacc", [128, 16], F32)
        waccs = sbuf(wst, "waccs", [128, 1], F32)
        wstage = sbuf(wst, "wstage", [128, 16 * 128], BF16)
        swb = sbuf(wst, "swb", [128, 1], F32)

        for mi, (mname, w_in) in enumerate((("g", wg_in), ("u", wu_in), ("d", wd_in))):
            @sec("gpsimd")
            def _(e, w_in=w_in):
                e.wait_ge(s_v.h, s_v.n)
                scb = s_sc.n
                for rc in range(16):
                    if rc >= 4:
                        e.wait_ge(s_sc.h, scb + rc - 3)
                    s_gld.inc(e.dma_start(out=wf[rc % 4][:],
                                         in_=w_in[rc * 128:(rc + 1) * 128, :]), 16)

            @sec("scalar")
            def _(e):
                for rc in range(16):
                    s_gld.wait(e, s_gld.n - 16 * (15 - rc))
                    op = e.activation(wqrow[rc % 2][:], wf[rc % 4][:], AF.Abs,
                                      accum_out=wacc[:, rc:rc + 1])
                    s_sc.inc(op, 1)

            @sec("vector")
            def _(e):
                s_sc.wait(e)
                vf(e, e.reduce_sum(waccs[:], wacc[:], axis=AX.X))

            @sec("tensor")
            def _(e):
                s_v.wait(e)
                op = e.matmul(pb[6][0:1, 0:1], waccs[:], ones_kf[:], start=True, stop=True)
                s_mm.inc(op, 1)

            @sec("vector")
            def _(e):
                s_mm.wait(e)
                vf(e, e.tensor_scalar(scal[:, 0:1], pb[6][0:1, 0:1], 1.0 / (D * D), None, ALU.mult))
                vf(e, e.tensor_scalar(scal[:, 1:2], scal[:, 0:1], EPS, None, ALU.max))
                op = e.reciprocal(scal[:, 2:3], scal[:, 1:2])
                s_v.inc(op, 1)

            @sec("tensor")
            def _(e, mi=mi):
                s_v.wait(e)
                op = e.matmul(pb[7][:, 0:1], ones_mf[:], scal[:, 2:3], start=True, stop=True)
                s_mm.inc(op, 1)
                op = e.matmul(pb[7][:, 1:2], ones_mf[:], scal[:, 1:2], start=True, stop=True)
                s_mm.inc(op, 1)

            @sec("vector")
            def _(e, mi=mi):
                s_mm.wait(e)
                e.tensor_copy(swb[:], pb[7][:, 0:1])
                op = e.tensor_copy(winv[:, mi:mi + 1], pb[7][:, 1:2])
                s_v.inc(op, 1)

            @sec("gpsimd")
            def _(e, w_in=w_in):
                e.wait_ge(s_v.h, s_v.n)
                vb = s_v.n
                for rc in range(16):
                    if rc >= 4:
                        e.wait_ge(s_v.h, vb + 4 * (rc - 3))
                    s_gld.inc(e.dma_start(out=wf[rc % 4][:],
                                         in_=w_in[rc * 128:(rc + 1) * 128, :]), 16)

            trp_done = {}
            for rc in range(16):
                @sec("scalar")
                def _(e, rc=rc):
                    s_gld.wait(e, s_gld.n - 16 * (15 - rc))
                    e.activation(wf[rc % 4][:], wf[rc % 4][:], AF.Identity,
                                 scale=swb[:], bias=magic_sb[:, 0:1])
                    op = e.activation(wf[rc % 4][:], wf[rc % 4][:], AF.Identity,
                                     bias=magic_sb[:, 1:2])
                    s_sc.inc(op, 1)

                @sec("vector")
                def _(e, rc=rc):
                    s_sc.wait(e)
                    if rc >= 2:
                        e.wait_ge(s_mm.h, trp_done[rc - 2])
                    op = e.tensor_scalar(wqrow[rc % 2][:], wf[rc % 4][:], 1.0, -1.0,
                                         ALU.min, ALU.max)
                    s_v.inc(op, 3)

                @sec("tensor")
                def _(e, rc=rc):
                    s_v.wait(e)
                    base = 4 + (rc % 2) * 2
                    for cc in range(16):
                        op = e.transpose(
                            pbb(base + cc // 8)[:, (cc % 8) * 128:(cc % 8 + 1) * 128],
                            wqrow[rc % 2][:, cc * 128:(cc + 1) * 128], identb[:])
                    s_mm.inc(op, 1)
                    trp_done[rc] = s_mm.n

                @sec("vector")
                def _(e, rc=rc):
                    s_mm.wait(e)
                    if rc >= 1:
                        s_sld.wait(e)
                    base = 4 + (rc % 2) * 2
                    e.tensor_copy(wstage[:, 0:1024], pbb(base)[:, 0:1024])
                    op = e.tensor_copy(wstage[:, 1024:2048], pbb(base + 1)[:, 0:1024])
                    s_v.inc(op, 1)

                @sec("scalar")
                def _(e, rc=rc, wqt_m=wqt[mname]):
                    s_v.wait(e)
                    op = e.dma_start(
                        out=cmaj(wqt_m[rc // 4][:, (rc % 4) * 128:(rc % 4 + 1) * 128]),
                        in_=s3(wstage[:], 16))
                    s_sld.inc(op, 16)

        barrier()

        if debug:
            @sec("gpsimd")
            def _(e):
                s_cc.wait(e)
                s_gld.inc(e.dma_start(out=dbg["ag1"][:, :, :], in_=ag1_dst[:, :, :]), 16)
                s_gld.inc(e.dma_start(out=dbg["ag1s"][:, :], in_=ag1_src[:, :]), 16)
                s_gld.inc(e.dma_start(out=dbg["wqtg"][:, :, :], in_=wqt["g"][:, :, :]), 16)
                s_gld.inc(e.dma_start(out=dbg["wqtd"][:, :, :], in_=wqt["d"][:, :, :]), 16)
                s_gld.inc(e.dma_start(out=dbg["winv"][:, :], in_=winv[:]), 16)

        wst.close()
        p1.close()

        p2 = ExitStack()
        q_sb = sbuf(p2, "q_sb", [128, T], BF16)
        k_sb = sbuf(p2, "k_sb", [128, T], BF16)
        v_sb = sbuf(p2, "v_sb", [128, T], BF16)
        z_sb = [sbuf(p2, f"z{h}", [128, T], BF16) for h in range(NHL)]
        bvb = sbuf(p2, "bvb", [128, 128], F32)
        p_sb = [sbuf(p2, f"p{i}", [128, 1024], BF16) for i in range(2)]
        rd_sb = sbuf(p2, "rd", [1, 512], F32)
        rdb_sb = sbuf(p2, "rdb", [128, 512], F32)
        p2q = ExitStack()
        cos_sb = sbuf(p2q, "cos_s", [128, T // 2], F32)
        sin_sb = sbuf(p2q, "sin_s", [128, T // 2], F32)
        swapi = sbuf(p2q, "swapi_s", [128, 128], BF16)
        tq_b = sbuf(p2q, "tq_b", [128, 512], BF16)
        wqkv_h = sbuf(p2q, "wqkv_h", [128, 3 * D], BF16)
        xn_sb = [sbuf(p2q, f"xn{i}", [128, 16 * 512], BF16) for i in range(2)]
        tmpq = sbuf(p2q, "tmpq", [128, 512], F32)
        rop = sbuf(p2q, "rop", [128, 512], F32)
        sqt = sbuf(p2q, "sqt", [128, 512], BF16)
        qn_row = sbuf(p2q, "qn_row", [1, 16], F32)
        kn_row = sbuf(p2q, "kn_row", [1, 16], F32)

        @sec("sync")
        def _(e):
            s_ld.inc(e.dma_start(out=swapi[:], in_=swapi_in[:, :]), 16)
            s_ld.inc(e.dma_start(out=cos_sb[:], in_=cos_in[:, 0:T // 2]), 16)
            s_ld.inc(e.dma_start(out=sin_sb[:], in_=sin_in[:, 0:T // 2]), 16)

        for h in range(NHL):
            @sec("gpsimd")
            def _(e, h=h):
                if h > 0:
                    e.wait_ge(s_mm.h, s_mm.n)
                for j, w in ((0, wq_in), (1, wk_in), (2, wv_in)):
                    s_gld.inc(e.dma_start(
                        out=s3(wqkv_h[:, j * D:(j + 1) * D], 16),
                        in_=cmaj(w[h])), 16)

            @sec("tensor")
            def _(e, h=h):
                s_ld.wait(e)
                op = e.matmul(pb[6][:, 0:128], ones_mf[:],
                              bvrow[0:1, h * 128:(h + 1) * 128], start=True, stop=True)
                s_mm.inc(op, 1)

            @sec("vector")
            def _(e, h=h):
                s_mm.wait(e)
                op = e.tensor_copy(bvb[:], pb[6][:, 0:128])
                s_v.inc(op, 1)

            for grp in range(16):
                r, wnd = grp // 2, grp % 2

                @sec("sync")
                def _(e, grp=grp, r=r, wnd=wnd, h=h):
                    s_cc.wait(e)
                    if grp >= 2:
                        e.wait_ge(s_mm.h, s_mm.n)
                    if (grp == 8) or (h > 0 and grp == 0):
                        half = 1 if grp == 8 else 0
                        e.wait_ge(s_v.h, s_v.n)
                        s_ld.inc(e.dma_start(out=cos_sb[:],
                                 in_=cos_in[:, half * (T // 2):(half + 1) * (T // 2)]), 16)
                        s_ld.inc(e.dma_start(out=sin_sb[:],
                                 in_=sin_in[:, half * (T // 2):(half + 1) * (T // 2)]), 16)
                    op = e.dma_start(
                        out=s3(xn_sb[grp % 2][:], 16),
                        in_=cmaj(ag1_dst[r, :, wnd * 512:(wnd + 1) * 512]))
                    s_ld.inc(op, 16)

                @sec("tensor")
                def _(e, grp=grp, h=h):
                    s_ld.wait(e)
                    s_gld.wait(e)
                    s_v.wait(e)
                    xb = xn_sb[grp % 2]
                    for cc in range(16):
                        st, sp = cc == 0, cc == 15
                        e.matmul(pb[0][:, :], wqkv_h[:, cc * 128:(cc + 1) * 128],
                                 xb[:, cc * 512:(cc + 1) * 512], start=st, stop=sp)
                        e.matmul(pb[1], wqkv_h[:, D + cc * 128:D + (cc + 1) * 128],
                                 xb[:, cc * 512:(cc + 1) * 512], start=st, stop=sp)
                        for j in range(4):
                            op = e.matmul(pb[2][:, j * 128:(j + 1) * 128],
                                          xb[:, cc * 512 + j * 128:cc * 512 + (j + 1) * 128],
                                          wqkv_h[:, 2 * D + cc * 128:2 * D + (cc + 1) * 128],
                                          start=(cc == 0 and j == 0),
                                          stop=(cc == 15 and j == 3))
                    s_mm.inc(op, 1)

                for psrc, dst_sb, brow, nrow in ((0, q_sb, 0, qn_row), (1, k_sb, 2, kn_row)):
                    @sec("vector")
                    def _(e, h=h, psrc=psrc, brow=brow):
                        s_mm.wait(e)
                        op = e.tensor_scalar(tq_b[:], pb[psrc][:, :],
                                             bqkv[:, brow + h:brow + h + 1], None, ALU.add)
                        s_v.inc(op, 1)

                    @sec("tensor")
                    def _(e):
                        s_v.wait(e)
                        op = e.matmul(pb[5][:, :], swapi[:], tq_b[:], start=True, stop=True)
                        s_mm.inc(op, 1)

                    @sec("vector")
                    def _(e, grp=grp, dst_sb=dst_sb):
                        s_mm.wait(e)
                        g0 = grp * 512
                        c0 = (grp % 8) * 512
                        e.tensor_tensor(rop[:], tq_b[:], cos_sb[:, c0:c0 + 512], ALU.mult)
                        vf(e, e.scalar_tensor_tensor(rdb_sb[:], pb[5][:, :], 0.0,
                                               sin_sb[:, c0:c0 + 512], ALU.bypass, ALU.mult))
                        vf(e, e.tensor_add(dst_sb[:, g0:g0 + 512], rop[:], rdb_sb[:]))
                        op = e.tensor_mul(sqt[:], dst_sb[:, g0:g0 + 512],
                                          dst_sb[:, g0:g0 + 512])
                        s_v.inc(op, 1)

                    @sec("tensor")
                    def _(e):
                        s_v.wait(e)
                        op = e.matmul(pb[3][0:1, :], ones_k[:], sqt[:], start=True, stop=True)
                        s_mm.inc(op, 1)

                    @sec("vector")
                    def _(e, grp=grp, nrow=nrow):
                        s_mm.wait(e)
                        op = e.reduce_max(nrow[0:1, grp:grp + 1], pb[3][0:1, :], axis=AX.X)
                        s_v.inc(op, 1)

                @sec("vector")
                def _(e, grp=grp):
                    g0 = grp * 512
                    for j in range(4):
                        op = e.tensor_add(v_sb[:, g0 + j * 128:g0 + (j + 1) * 128],
                                          pb[2][:, j * 128:(j + 1) * 128], bvb[:])
                    s_v.inc(op, 1)

            @sec("vector")
            def _(e):
                e.reduce_max(scal[:, 4:5], qn_row[:], axis=AX.X)
                vf(e, e.reduce_max(scal[:, 5:6], kn_row[:], axis=AX.X))
                op = e.tensor_tensor(scal[:, 6:7], scal[:, 4:5], scal[:, 5:6], ALU.mult)
                s_v.inc(op, 1)

            @sec("scalar")
            def _(e):
                s_v.wait(e)
                op = e.activation(scal[:, 7:8], scal[:, 6:7], AF.Sqrt)
                s_sc.inc(op, 1)

            @sec("vector")
            def _(e):
                s_sc.wait(e)
                op = e.tensor_scalar(scal[:, 8:9], scal[:, 7:8], -1.02 * ISQ, -1.0,
                                     ALU.mult, ALU.add)
                s_v.inc(op, 1)

            @sec("tensor")
            def _(e):
                s_v.wait(e)
                op = e.matmul(pb[3][:, 0:1], ones_mf[:], scal[:, 8:9], start=True, stop=True)
                s_mm.inc(op, 1)

            @sec("vector")
            def _(e, h=h):
                s_mm.wait(e)
                op = e.tensor_copy(negb_sb[:, h:h + 1], pb[3][:, 0:1])
                s_v.inc(op, 1)

            if debug:
                @sec("gpsimd")
                def _(e, h=h):
                    e.wait_ge(s_v.h, s_v.n)
                    s_gld.inc(e.dma_start(out=dbg["q"][h], in_=q_sb[:]), 16)
                    s_gld.inc(e.dma_start(out=dbg["k"][h], in_=k_sb[:]), 16)
                    s_gld.inc(e.dma_start(out=dbg["v"][h], in_=v_sb[:]), 16)
                    s_gld.inc(e.dma_start(out=dbg["negb"][h].unsqueeze(1),
                                          in_=negb_sb[:, h:h + 1]), 16)

            barrier()

            for qg in range(16):
                q0 = qg * 512
                nkb = 4 * (qg + 1)
                npr = nkb // 2
                exp_base = s_sc.n
                mask_done = {}
                for step in range(npr + 1):
                    if step < npr:
                        g2 = step
                        A = pq[0] if g2 % 2 == 0 else pq[3]

                        @sec("tensor")
                        def _(e, g2=g2, q0=q0, exp_base=exp_base, A=A):
                            if g2 >= 2:
                                e.wait_ge(s_sc.h, exp_base + g2 - 1)
                            kb = 2 * g2
                            e.matmul(A[:, 0:512], k_sb[:, kb * 128:(kb + 1) * 128],
                                     q_sb[:, q0:q0 + 512], start=True, stop=True)
                            op = e.matmul(A[:, 512:1024],
                                          k_sb[:, (kb + 1) * 128:(kb + 2) * 128],
                                          q_sb[:, q0:q0 + 512], start=True, stop=True)
                            s_mm.inc(op, 1)

                        @sec("scalar")
                        def _(e, g2=g2, h=h, A=A):
                            s_mm.wait(e)
                            op = e.activation(p_sb[g2 % 2][:], A[:, 0:1024], AF.Exp,
                                              bias=negb_sb[:, h:h + 1], scale=ISQ)
                            s_sc.inc(op, 1)

                        for half in range(2):
                            kb = 2 * g2 + half
                            if kb >= 4 * qg:
                                @sec("vector")
                                def _(e, g2=g2, kb=kb, qg=qg, half=half):
                                    s_sc.wait(e)
                                    o = kb - 4 * qg
                                    op = e.tensor_mul(
                                        p_sb[g2 % 2][:, half * 512:(half + 1) * 512],
                                        p_sb[g2 % 2][:, half * 512:(half + 1) * 512],
                                        mask_sb[:, o * 512:(o + 1) * 512])
                                    s_v.inc(op, 1)
                                    mask_done[g2] = s_v.n

                    if step >= 1:
                        g2p = step - 1

                        @sec("tensor")
                        def _(e, g2p=g2p, qg=qg, nkb=nkb, exp_base=exp_base):
                            if g2p in mask_done:
                                e.wait_ge(s_v.h, mask_done[g2p])
                            else:
                                e.wait_ge(s_sc.h, exp_base + g2p + 1)
                            if g2p == 0:
                                s_v.wait(e)
                            for half in range(2):
                                kb = 2 * g2p + half
                                st, sp = kb == 0, kb == nkb - 1
                                e.matmul(pb[2], v_sb[:, kb * 128:(kb + 1) * 128],
                                         p_sb[g2p % 2][:, half * 512:(half + 1) * 512],
                                         start=st, stop=sp)
                                op = e.matmul(pb[3][0:1, :], ones_k[:],
                                              p_sb[g2p % 2][:, half * 512:(half + 1) * 512],
                                              start=st, stop=sp)
                            s_mm.inc(op, 1)

                @sec("vector")
                def _(e):
                    s_mm.wait(e)
                    op = e.reciprocal(rd_sb[:], pb[3][0:1, :])
                    s_v.inc(op, 1)

                @sec("tensor")
                def _(e):
                    s_v.wait(e)
                    op = e.matmul(pb[4], ones_mf[:], rd_sb[:], start=True, stop=True)
                    s_mm.inc(op, 1)

                @sec("vector")
                def _(e, h=h, q0=q0):
                    s_mm.wait(e)
                    vf(e, e.tensor_copy(rdb_sb[:], pb[4]))
                    op = e.tensor_mul(z_sb[h][:, q0:q0 + 512], pb[2], rdb_sb[:])
                    s_v.inc(op, 1)

            if debug:
                @sec("gpsimd")
                def _(e, h=h):
                    e.wait_ge(s_v.h, s_v.n)
                    s_gld.inc(e.dma_start(out=dbg["z"][h], in_=z_sb[h][:]), 16)

            barrier()

        p2q.close()
        po_sb = [sbuf(p2, f"po{i}", [128, D], BF16) for i in range(2)]
        for tt in range(64):
            @sec("tensor")
            def _(e, tt=tt):
                if tt >= 2:
                    s_v.wait(e, s_v.n - 1)
                for mc in range(4):
                    bank = (tt % 2) * 4 + mc
                    for h in range(NHL):
                        op = e.matmul(pb[bank], z_sb[h][:, tt * 128:(tt + 1) * 128],
                                      wo_sb[:, h * D + mc * 512:h * D + (mc + 1) * 512],
                                      start=(h == 0), stop=(h == 1))
                s_mm.inc(op, 1)

            @sec("vector")
            def _(e, tt=tt):
                s_mm.wait(e)
                if tt >= 2:
                    s_ld.wait(e)
                for mc in range(4):
                    op = e.tensor_copy(po_sb[tt % 2][:, mc * 512:(mc + 1) * 512],
                                       pb[(tt % 2) * 4 + mc][:, :])
                s_v.inc(op, 1)

            @sec("sync")
            def _(e, tt=tt):
                s_v.wait(e)
                s_ld.inc(e.dma_start(out=part_src[tt * 128:(tt + 1) * 128, :],
                                     in_=po_sb[tt % 2][:]), 16)

        @sec("gpsimd")
        def _(e):
            e.wait_ge(s_ld.h, s_ld.n)
            op = e.collective_compute(
                "ReduceScatter", ALU.add, replica_groups=[list(range(NCORES))],
                ins=[part_src[:, :]], outs=[rs1_dst[:, :]])
            s_cc.inc(op, 1)

        if debug:
            @sec("gpsimd")
            def _(e):
                s_cc.wait(e)
                s_gld.inc(e.dma_start(out=dbg["rs1"][:, :], in_=rs1_dst[:, :]), 16)

        p2.close()
        barrier()

        p5 = ExitStack()
        xt2 = [sbuf(p5, f"xt2_{i}", [128, D], F32) for i in range(2)]
        at2 = [sbuf(p5, f"at2_{i}", [128, D], BF16) for i in range(2)]
        r2t = [sbuf(p5, f"r2t{i}", [128, D], F32) for i in range(2)]
        xn2f = [sbuf(p5, f"xn2f{i}", [128, D], F32) for i in range(2)]
        xqf = sbuf(p5, "xqf", [128, D], F32)
        xqb = [sbuf(p5, f"xqb{i}", [128, D], BF16) for i in range(2)]
        stage2 = [sbuf(p5, f"stage2_{i}", [128, D], BF16) for i in range(2)]
        xn2fm = sbuf(p5, "xn2fm", [128, 16 * 128], F32)
        mv2 = [sbuf(p5, f"mv2{i}", [128, 2], F32) for i in range(2)]
        ve2 = [sbuf(p5, f"ve2{i}", [128, 4], F32) for i in range(2)]
        qs2 = sbuf(p5, "qs2", [128, 8], F32)
        lg_sb = sbuf(p5, "lg", [128, 8], F32)
        mx8 = sbuf(p5, "mx8", [128, 8], F32)
        mi8 = sbuf(p5, "mi8", [128, 8], U32)
        idxf = sbuf(p5, "idxf", [128, 1], F32)

        for t in range(8):
            @sec("sync")
            def _(e, t=t):
                s_cc.wait(e)
                if t >= 2:
                    e.wait_ge(s_v.h, s_v.n)
                s_ld.inc(e.dma_start(out=xt2[t % 2][:],
                                     in_=x_in[t * 128:(t + 1) * 128, :]), 16)
                s_ld.inc(e.dma_start(out=at2[t % 2][:],
                                     in_=rs1_dst[t * 128:(t + 1) * 128, :]), 16)

            @sec("vector")
            def _(e, t=t):
                s_ld.wait(e)
                if t >= 1:
                    s_sld.wait(e)
                    s_mm.wait(e)
                vf(e, e.tensor_add(r2t[t % 2][:], xt2[t % 2][:], at2[t % 2][:]))
                op = e.tensor_add(r2t[t % 2][:], r2t[t % 2][:], bo_bc[:])
                s_v.inc(op, 1)

            @sec("scalar")
            def _(e, t=t):
                s_v.wait(e)
                op = e.dma_start(out=r2_buf[t * 128:(t + 1) * 128, :], in_=r2t[t % 2][:])
                s_sld.inc(op, 16)

            @sec("vector")
            def _(e, t=t):
                vf(e, e.reduce_sum(ve2[t % 2][:, 3:4], r2t[t % 2][:], axis=AX.X))
                e.tensor_scalar(mv2[t % 2][:, 0:1], ve2[t % 2][:, 3:4], 1.0 / D, None, ALU.mult)
                op = e.tensor_scalar(mv2[t % 2][:, 1:2], ve2[t % 2][:, 3:4], -1.0 / D, None, ALU.mult)
                s_v.inc(op, 1)

            @sec("scalar")
            def _(e, t=t):
                s_v.wait(e)
                op = e.activation(xn2f[t % 2][:], r2t[t % 2][:], AF.Square,
                                  bias=mv2[t % 2][:, 1:2], accum_out=ve2[t % 2][:, 0:1])
                s_sc.inc(op, 1)

            @sec("vector")
            def _(e, t=t):
                s_sc.wait(e)
                op = e.tensor_scalar(ve2[t % 2][:, 0:1], ve2[t % 2][:, 0:1], 1.0 / D, EPS,
                                     ALU.mult, ALU.add)
                s_v.inc(op, 1)

            @sec("scalar")
            def _(e, t=t):
                s_v.wait(e)
                op = e.activation(ve2[t % 2][:, 1:2], ve2[t % 2][:, 0:1], AF.Sqrt)
                s_sc.inc(op, 1)

            @sec("vector")
            def _(e, t=t):
                s_sc.wait(e)
                vf(e, e.reciprocal(ve2[t % 2][:, 2:3], ve2[t % 2][:, 1:2]))
                vf(e, e.tensor_scalar(xn2f[t % 2][:], r2t[t % 2][:], mv2[t % 2][:, 0:1],
                                      ve2[t % 2][:, 2:3], ALU.subtract, ALU.mult))
                vf(e, e.reduce_max(qs2[:, 0:1], xn2f[t % 2][:], axis=AX.X,
                                   apply_absolute_value=True))
                vf(e, e.tensor_scalar(qs2[:, 1:2], qs2[:, 0:1], EPS, None, ALU.max))
                vf(e, e.reciprocal(qs2[:, 2:3], qs2[:, 1:2]))
                e.tensor_scalar(qs2[:, 3:4], qs2[:, 2:3], 127.0, None, ALU.mult)
                vf(e, e.tensor_scalar(qs2[:, 4:5], qs2[:, 1:2], 1.0 / 127.0, None, ALU.mult))
                vf(e, e.tensor_scalar(xqf[:], xn2f[t % 2][:], qs2[:, 3:4], None, ALU.mult))
                vf(e, e.tensor_scalar(xqf[:], xqf[:], MAGIC, -MAGIC, ALU.add, ALU.add))
                op = e.tensor_scalar(xqb[t % 2][:], xqf[:], 127.0, -128.0, ALU.min, ALU.max)
                s_v.inc(op, 1)

            @sec("scalar")
            def _(e, t=t):
                s_v.wait(e)
                op = e.dma_start(out=meta_src[0, t * 128:(t + 1) * 128].unsqueeze(1), in_=qs2[:, 4:5])
                s_sld.inc(op, 16)

            @sec("tensor")
            def _(e, t=t):
                s_v.wait(e)
                bank = 4 + (t % 2) * 2
                for cc in range(16):
                    op = e.transpose(
                        pbb(bank + cc // 8)[:, (cc % 8) * 128:(cc % 8 + 1) * 128],
                        xqb[t % 2][:, cc * 128:(cc + 1) * 128], identb[:])
                s_mm.inc(op, 1)

            @sec("vector")
            def _(e, t=t):
                s_mm.wait(e)
                if t >= 2:
                    s_ld.wait(e)
                bank = 4 + (t % 2) * 2
                e.tensor_copy(stage2[t % 2][:, 0:1024], pbb(bank)[:, 0:1024])
                op = e.tensor_copy(stage2[t % 2][:, 1024:2048], pbb(bank + 1)[:, 0:1024])
                s_v.inc(op, 1)

            @sec("sync")
            def _(e, t=t):
                s_v.wait(e)
                op = e.dma_start(
                    out=cmaj(ag2_src[:, t * 128:(t + 1) * 128]),
                    in_=s3(stage2[t % 2][:], 16))
                s_ld.inc(op, 16)

            @sec("tensor")
            def _(e, t=t):
                for cc4 in range(4):
                    for ccl in range(4):
                        cc = cc4 * 4 + ccl
                        op = e.transpose(pb[cc4][:, ccl * 128:(ccl + 1) * 128],
                                         xn2f[t % 2][:, cc * 128:(cc + 1) * 128], identf[:])
                s_mm.inc(op, 1)

            @sec("vector")
            def _(e, t=t):
                s_mm.wait(e)
                for cc4 in range(4):
                    op = e.tensor_copy(xn2fm[:, cc4 * 512:(cc4 + 1) * 512], pb[cc4][:, :])
                s_v.inc(op, 1)

            @sec("tensor")
            def _(e, t=t):
                s_v.wait(e)
                for cc in range(16):
                    op = e.matmul(pb[6][:, 0:8], xn2fm[:, cc * 128:(cc + 1) * 128],
                                  rwt_sb[:, cc * 8:(cc + 1) * 8],
                                  start=(cc == 0), stop=(cc == 15))
                s_mm.inc(op, 1)

            @sec("vector")
            def _(e, t=t):
                s_mm.wait(e)
                vf(e, e.tensor_copy(lg_sb[:], pb[6][:, 0:8]))
                vf(e, e.max(mx8[:], lg_sb[:]))
                vf(e, e.max_index(mi8[:], mx8[:], lg_sb[:]))
                op = e.tensor_copy(idxf[:], mi8[:, 0:1])
                s_v.inc(op, 1)

            @sec("scalar")
            def _(e, t=t):
                s_v.wait(e)
                op = e.dma_start(out=meta_src[1, t * 128:(t + 1) * 128].unsqueeze(1), in_=idxf[:])
                s_sld.inc(op, 16)

        @sec("gpsimd")
        def _(e):
            e.wait_ge(s_ld.h, s_ld.n)
            e.wait_ge(s_sld.h, s_sld.n)
            op = e.collective_compute(
                "AllGather", ALU.bypass, replica_groups=[list(range(NCORES))],
                ins=[meta_src[:, :]], outs=[meta_dst[:, :, :]])
            s_cc.inc(op, 1)
            op = e.collective_compute(
                "AllGather", ALU.bypass, replica_groups=[list(range(NCORES))],
                ins=[ag2_src[:, :]], outs=[ag2_dst[:, :, :]])
            s_cc.inc(op, 1)

        if debug:
            @sec("gpsimd")
            def _(e):
                s_cc.wait(e)
                s_gld.inc(e.dma_start(out=dbg["ag2"][:, :, :], in_=ag2_dst[:, :, :]), 16)
                s_gld.inc(e.dma_start(out=dbg["meta"][:, :, :], in_=meta_dst[:, :, :]), 16)
                s_gld.inc(e.dma_start(out=dbg["r2"][:, :], in_=r2_buf[:, :]), 16)

        p5.close()
        barrier()

        pf = ExitStack()
        act_sg = sbuf(pf, "act_sg", [128, 16 * 1024], BF16)
        h_sg = sbuf(pf, "h_sg", [128, 16 * 1024], F32)
        wblk = [sbuf(pf, f"wblk{i}", [128, 16 * 512], BF16) for i in range(2)]
        sxg = sbuf(pf, "sxg", [128, 1024], F32)
        sxu = sbuf(pf, "sxu", [128, 1024], F32)
        shb = sbuf(pf, "shb", [128, 1024], F32)
        hmax = sbuf(pf, "hmax", [128, 1024], F32)
        srow = sbuf(pf, "srow", [1, 1024], F32)
        irow = sbuf(pf, "irow", [1, 1024], F32)
        srow2 = sbuf(pf, "srow2", [1, 1024], F32)
        hmcl = sbuf(pf, "hmcl", [128, 8], F32)
        invsh = sbuf(pf, "invsh", [128, 8], F32)
        comb = sbuf(pf, "comb", [128, 8], F32)
        idxt = sbuf(pf, "idxt", [128, 8], F32)
        gtile = [sbuf(pf, f"gtile{i}", [128, 512], F32) for i in range(2)]
        utile = [sbuf(pf, f"utile{i}", [128, 512], F32) for i in range(2)]
        hmax2 = sbuf(pf, "hmax2", [128, 1024], F32)
        y_sb = [sbuf(pf, f"y{i}", [128, D], BF16) for i in range(1)]

        ybv = h_sg[:].bitcast(BF16)
        wblk_free = [0, 0]
        gbank_free = [0, 0]
        wd_tgt = [0, 0, 0, 0]
        ymm_done = [0, 0, 0, 0]
        gld_tgt = [0, 0, 0, 0]
        uld_tgt = [0, 0, 0, 0]

        for sg in range(8):
            @sec("sync")
            def _(e, sg=sg):
                s_cc.wait(e)
                if sg > 0:
                    e.wait_ge(s_mm.h, s_mm.n)
                op = e.dma_start(out=s3(act_sg[:], 16),
                                 in_=cmaj(ag2_dst[sg]))
                s_ld.inc(op, 16)
                op = e.dma_start(out=srow[:], in_=meta_dst[sg, 0].unsqueeze(0))
                s_ld.inc(op, 16)
                op = e.dma_start(out=irow[:], in_=meta_dst[sg, 1].unsqueeze(0))
                s_ld.inc(op, 16)

            @sec("tensor")
            def _(e, sg=sg):
                s_ld.wait(e)
                if sg > 0:
                    e.wait_ge(s_v.h, s_v.n)
                for j in range(2):
                    op = e.matmul(pb[6], ones_mf[:], srow[0:1, j * 512:(j + 1) * 512],
                                  start=True, stop=True)
                    s_mm.inc(op, 1)
                    e.wait_ge(s_v.h, s_v.n + j + 1)

            @sec("vector")
            def _(e, sg=sg):
                if sg > 0:
                    s_ld.wait(e)
                base = s_mm.n - 2
                e.memset(hmax[:], 0.0)
                e.memset(hmax2[:], 0.0)
                for j in range(2):
                    s_mm.wait(e, base + j + 1)
                    e.tensor_scalar(sxg[:, j * 512:(j + 1) * 512], pb[6],
                                    winv[:, 0:1], None, ALU.mult)
                    op = e.tensor_scalar(sxu[:, j * 512:(j + 1) * 512], pb[6],
                                         winv[:, 1:2], None, ALU.mult)
                    s_v.inc(op, 1)

            @sec("scalar")
            def _(e, sg=sg):
                e.wait_ge(s_mm.h, wblk_free[0])
                op = e.dma_start(out=s3(wblk[0][:], 16), in_=cmaj(wqt["g"][0]))
                s_sld.inc(op, 16)
                gld_tgt[0] = s_sld.n
                e.wait_ge(s_mm.h, wblk_free[1])
                op = e.dma_start(out=s3(wblk[1][:], 16), in_=cmaj(wqt["u"][0]))
                s_sld.inc(op, 16)
                uld_tgt[0] = s_sld.n

            for ds in range(4):
                for j in range(2):
                    @sec("tensor")
                    def _(e, ds=ds, j=j):
                        e.wait_ge(s_sld.h, gld_tgt[ds])
                        if gbank_free[0]:
                            e.wait_ge(s_v.h, gbank_free[0])
                        if ds == 0 and j == 0:
                            s_v.wait(e)
                            s_ld.wait(e)
                        for hc in range(16):
                            st, sp = hc == 0, hc == 15
                            for dc in range(4):
                                op = e.matmul(pb[dc],
                                              wblk[0][:, hc * 512 + dc * 128:hc * 512 + (dc + 1) * 128],
                                              act_sg[:, hc * 1024 + j * 512:hc * 1024 + (j + 1) * 512],
                                              start=st, stop=sp)
                        s_mm.inc(op, 1)
                        if j == 1:
                            wblk_free[0] = s_mm.n

                    @sec("tensor")
                    def _(e, ds=ds, j=j):
                        e.wait_ge(s_sld.h, uld_tgt[ds])
                        if gbank_free[1]:
                            e.wait_ge(s_v.h, gbank_free[1])
                        for hc in range(16):
                            st, sp = hc == 0, hc == 15
                            for dc in range(4):
                                op = e.matmul(pb[4 + dc],
                                              wblk[1][:, hc * 512 + dc * 128:hc * 512 + (dc + 1) * 128],
                                              act_sg[:, hc * 1024 + j * 512:hc * 1024 + (j + 1) * 512],
                                              start=st, stop=sp)
                        s_mm.inc(op, 1)
                        if j == 1:
                            wblk_free[1] = s_mm.n

                    @sec("vector")
                    def _(e, ds=ds, j=j):
                        s_mm.wait(e)
                        hm = (hmax, hmax2)
                        gref = [None, None]
                        for dp in range(2):
                            dcs = (2 * dp, 2 * dp + 1)
                            for i, dc in enumerate(dcs):
                                op = e.tensor_tensor(gtile[i][:], pb[dc],
                                                     sxg[:, j * 512:(j + 1) * 512], ALU.mult)
                            if dp == 1:
                                s_v.inc(op, 1)
                                gbank_free[0] = s_v.n
                            for i, dc in enumerate(dcs):
                                op = e.tensor_tensor(utile[i][:], pb[4 + dc],
                                                     sxu[:, j * 512:(j + 1) * 512], ALU.mult)
                            if dp == 1:
                                s_v.inc(op, 1)
                                gbank_free[1] = s_v.n
                            for i, dc in enumerate(dcs):
                                e.scalar_tensor_tensor(gtile[i][:], gtile[i][:], 0.0,
                                                       gtile[i][:], ALU.max, ALU.mult)
                            for i, dc in enumerate(dcs):
                                col = (ds * 4 + dc) * 1024 + j * 512
                                e.tensor_tensor(h_sg[:, col:col + 512], gtile[i][:],
                                                utile[i][:], ALU.mult)
                            for i, dc in enumerate(dcs):
                                col = (ds * 4 + dc) * 1024 + j * 512
                                e.tensor_mul(gtile[i][:], h_sg[:, col:col + 512],
                                             h_sg[:, col:col + 512])
                            for i, dc in enumerate(dcs):
                                op = e.tensor_max(hm[i][:, j * 512:(j + 1) * 512],
                                                  hm[i][:, j * 512:(j + 1) * 512],
                                                  gtile[i][:])
                        s_v.inc(op, 1)

                if ds < 3:
                    @sec("scalar")
                    def _(e, ds=ds, sg=sg):
                        e.wait_ge(s_mm.h, wblk_free[0])
                        op = e.dma_start(out=s3(wblk[0][:], 16), in_=cmaj(wqt["g"][ds + 1]))
                        s_sld.inc(op, 16)
                        gld_tgt[ds + 1] = s_sld.n
                        e.wait_ge(s_mm.h, wblk_free[1])
                        op = e.dma_start(out=s3(wblk[1][:], 16), in_=cmaj(wqt["u"][ds + 1]))
                        s_sld.inc(op, 16)
                        uld_tgt[ds + 1] = s_sld.n

            @sec("vector")
            def _(e):
                e.wait_ge(s_v.h, s_v.n)
                op = e.tensor_max(hmax[:], hmax[:], hmax2[:])
                vf(e, op)

            @sec("tensor")
            def _(e):
                s_v.wait(e)
                for w in range(8):
                    bank = pb[5] if w < 4 else pb[4]
                    op = e.transpose(bank[:, (w % 4) * 128:(w % 4 + 1) * 128],
                                     hmax[:, w * 128:(w + 1) * 128], identf[:])
                s_mm.inc(op, 1)

            @sec("vector")
            def _(e):
                s_mm.wait(e)
                for w in range(8):
                    bank = pb[5] if w < 4 else pb[4]
                    op = e.reduce_max(hmcl[:, w:w + 1],
                                      bank[:, (w % 4) * 128:(w % 4 + 1) * 128], axis=AX.X)
                s_v.inc(op, 1)

            @sec("scalar")
            def _(e):
                s_v.wait(e)
                op = e.activation(hmcl[:], hmcl[:], AF.Sqrt)
                s_sc.inc(op, 1)

            @sec("vector")
            def _(e):
                s_sc.wait(e)
                vf(e, e.tensor_scalar(hmcl[:], hmcl[:], EPS, None, ALU.max))
                op = e.tensor_scalar(invsh[:], hmcl[:], 1.0 / 127.0, None, ALU.mult)
                s_v.inc(op, 1)

            for w in range(8):
                @sec("tensor")
                def _(e, w=w):
                    s_v.wait(e)
                    op = e.transpose(pb[6 + w // 4][0:1, (w % 4) * 128:(w % 4 + 1) * 128],
                                     hmcl[:, w:w + 1], identf[:])
                    s_mm.inc(op, 1)

            @sec("vector")
            def _(e):
                s_mm.wait(e)
                e.tensor_copy(srow2[0:1, 0:512], pb[6][0:1, 0:512])
                vf(e, e.tensor_copy(srow2[0:1, 512:1024], pb[7][0:1, 0:512]))
                vf(e, e.reciprocal(srow2[:], srow2[:]))
                op = e.tensor_scalar(srow2[:], srow2[:], 127.0, None, ALU.mult)
                s_v.inc(op, 1)

            @sec("tensor")
            def _(e):
                s_v.wait(e)
                for j in range(2):
                    op = e.matmul(pb[7], ones_mf[:], srow2[0:1, j * 512:(j + 1) * 512],
                                  start=True, stop=True)
                    s_mm.inc(op, 1)
                    e.wait_ge(s_v.h, s_v.n + j + 1)

            @sec("vector")
            def _(e):
                base = s_mm.n - 2
                for j in range(2):
                    s_mm.wait(e, base + j + 1)
                    op = e.tensor_copy(shb[:, j * 512:(j + 1) * 512], pb[7])
                    s_v.inc(op, 1)

            @sec("vector")
            def _(e):
                for ch in range(16):
                    for j in range(2):
                        col = ch * 1024 + j * 512
                        e.tensor_tensor(gtile[j][:], h_sg[:, col:col + 512],
                                        shb[:, j * 512:(j + 1) * 512], ALU.mult)
                    for j in range(2):
                        e.tensor_scalar(gtile[j][:], gtile[j][:], MAGIC, -MAGIC,
                                        ALU.add, ALU.add)
                    for j in range(2):
                        col = ch * 1024 + j * 512
                        op = e.tensor_scalar(act_sg[:, col:col + 512], gtile[j][:],
                                             127.0, -128.0, ALU.min, ALU.max)
                s_v.inc(op, 1)

            @sec("tensor")
            def _(e):
                s_v.wait(e)
                s_ld.wait(e)
                for w in range(8):
                    op = e.transpose(pb[4][:, w * 4:w * 4 + 1],
                                     irow[0:1, w * 128:(w + 1) * 128], identf[0:1, 0:1])
                s_mm.inc(op, 1)

            @sec("vector")
            def _(e):
                s_mm.wait(e)
                for w in range(8):
                    e.tensor_copy(idxt[:, w:w + 1], pb[4][:, w * 4:w * 4 + 1])
                for w in range(8):
                    e.tensor_tensor(comb[:, w:w + 1], idxt[:, w:w + 1], eidb[:],
                                    ALU.is_equal)
                for w in range(8):
                    e.tensor_tensor(comb[:, w:w + 1], comb[:, w:w + 1],
                                    invsh[:, w:w + 1], ALU.mult)
                for w in range(8):
                    op = e.tensor_scalar(comb[:, w:w + 1], comb[:, w:w + 1],
                                         winv[:, 2:3], None, ALU.mult)
                s_v.inc(op, 1)

            @sec("scalar")
            def _(e, sg=sg):
                for hfg in range(2):
                    e.wait_ge(s_mm.h, wblk_free[hfg])
                    op = e.dma_start(out=s3(wblk[hfg][:], 16), in_=cmaj(wqt["d"][hfg]))
                    s_sld.inc(op, 16)
                    wd_tgt[hfg] = s_sld.n

            for hfg in range(4):
                @sec("tensor")
                def _(e, hfg=hfg):
                    e.wait_ge(s_sld.h, wd_tgt[hfg])
                    s_v.wait(e)
                    for tt in range(8):
                        for dc in range(16):
                            op = e.matmul(pb[tt],
                                          act_sg[:, dc * 1024 + tt * 128:dc * 1024 + (tt + 1) * 128],
                                          wblk[hfg % 2][:, dc * 512:(dc + 1) * 512],
                                          start=(dc == 0), stop=(dc == 15))
                    s_mm.inc(op, 1)
                    ymm_done[hfg] = s_mm.n

                if hfg < 2:
                    @sec("scalar")
                    def _(e, hfg=hfg):
                        e.wait_ge(s_mm.h, ymm_done[hfg])
                        op = e.dma_start(out=s3(wblk[hfg][:], 16), in_=cmaj(wqt["d"][hfg + 2]))
                        s_sld.inc(op, 16)
                        wd_tgt[hfg + 2] = s_sld.n

                @sec("vector")
                def _(e, hfg=hfg):
                    s_mm.wait(e)
                    for tt in range(8):
                        op = e.tensor_scalar(
                            ybv[:, tt * 2048 + hfg * 512:tt * 2048 + (hfg + 1) * 512],
                            pb[tt], comb[:, tt:tt + 1], None, ALU.mult)
                    s_v.inc(op, 1)

            @sec("sync")
            def _(e, sg=sg):
                s_v.wait(e)
                for tt in range(8):
                    s_ld.inc(e.dma_start(
                        out=y_src[sg * 1024 + tt * 128:sg * 1024 + (tt + 1) * 128, :],
                        in_=ybv[:, tt * 2048:(tt + 1) * 2048]), 16)

            wblk_free[0] = ymm_done[2]
            wblk_free[1] = ymm_done[3]

        @sec("gpsimd")
        def _(e):
            e.wait_ge(s_ld.h, s_ld.n)
            op = e.collective_compute(
                "ReduceScatter", ALU.add, replica_groups=[list(range(NCORES))],
                ins=[y_src[:, :]], outs=[rs2_dst[:, :]])
            s_cc.inc(op, 1)

        if debug:
            @sec("gpsimd")
            def _(e):
                s_cc.wait(e)
                s_gld.inc(e.dma_start(out=dbg["ysrc"][:, :], in_=y_src[0:TC, :]), 16)
                s_gld.inc(e.dma_start(out=dbg["rs2"][:, :], in_=rs2_dst[:, :]), 16)

        pf.close()
        barrier()

        p8 = ExitStack()
        f1 = [sbuf(p8, f"f1_{i}", [128, D], F32) for i in range(2)]
        f2 = [sbuf(p8, f"f2_{i}", [128, D], BF16) for i in range(2)]
        fo = [sbuf(p8, f"fo_{i}", [128, D], F32) for i in range(2)]
        for t in range(8):
            @sec("sync")
            def _(e, t=t):
                s_cc.wait(e)
                if t >= 2:
                    e.wait_ge(s_v.h, s_v.n)
                s_ld.inc(e.dma_start(out=f1[t % 2][:],
                                     in_=r2_buf[t * 128:(t + 1) * 128, :]), 16)
                s_ld.inc(e.dma_start(out=f2[t % 2][:],
                                     in_=rs2_dst[t * 128:(t + 1) * 128, :]), 16)

            @sec("vector")
            def _(e, t=t):
                s_ld.wait(e)
                if t >= 2:
                    s_sld.wait(e)
                op = e.tensor_add(fo[t % 2][:], f1[t % 2][:], f2[t % 2][:])
                s_v.inc(op, 1)

            @sec("scalar")
            def _(e, t=t):
                s_v.wait(e)
                op = e.dma_start(out=out[t * 128:(t + 1) * 128, :], in_=fo[t % 2][:])
                s_sld.inc(op, 16)

        @sec("scalar")
        def _(e):
            e.wait_ge(s_sld.h, s_sld.n)
        @sec("sync")
        def _(e):
            e.wait_ge(s_sld.h, s_sld.n)

        p8.close()

    return nc, dbg



def _host_inputs(inputs):
    x = np.ascontiguousarray(np.asarray(inputs["L0_attn_normalized_resid_pre"], np.float32)[0])
    WQ = np.asarray(inputs["L0_attn_W_Q"], np.float32)
    WK = np.asarray(inputs["L0_attn_W_K"], np.float32)
    WV = np.asarray(inputs["L0_attn_W_V"], np.float32)
    WO = np.asarray(inputs["L0_attn_W_O"], np.float32)
    bQ = np.asarray(inputs["L0_attn_b_Q"], np.float32)
    bK = np.asarray(inputs["L0_attn_b_K"], np.float32)
    bV = np.asarray(inputs["L0_attn_b_V"], np.float32)
    bO = np.asarray(inputs["L0_attn_b_O"], np.float32)
    RW = np.asarray(inputs["L0_ffn_router_w"], np.float32)
    WG = np.asarray(inputs["L0_ffn_wg"], np.float32)
    WU = np.asarray(inputs["L0_ffn_wu"], np.float32)
    WD = np.asarray(inputs["L0_ffn_wd"], np.float32)

    perm = np.concatenate([np.arange(0, 128, 2), np.arange(1, 128, 2)])
    inv_freq = 1.0 / (10000.0 ** (np.arange(0, 128, 2, dtype=np.float32) / 128.0))
    m = np.arange(T, dtype=np.float32)
    freqs = np.outer(m, inv_freq)
    cos64 = np.cos(freqs).T.astype(np.float32)
    sin64 = np.sin(freqs).T.astype(np.float32)
    cos_t = np.ascontiguousarray(np.concatenate([cos64, cos64], axis=0))
    sin_t = np.ascontiguousarray(np.concatenate([-sin64, sin64], axis=0))
    swapi = np.ascontiguousarray(np.roll(np.eye(128, dtype=ml_dtypes.bfloat16), 64, axis=0))

    masks = np.zeros((4, 128, 512), dtype=ml_dtypes.bfloat16)
    for o in range(4):
        kk = np.arange(128)[:, None] + 128 * o
        qq = np.arange(512)[None, :]
        masks[o] = (qq >= kk).astype(ml_dtypes.bfloat16)

    common = {
        "rw": RW, "cos": cos_t, "sin": sin_t, "masks": masks,
        "identb": np.eye(128, dtype=ml_dtypes.bfloat16),
        "identf": np.eye(128, dtype=np.float32),
        "ones_k": np.ones((128, 1), ml_dtypes.bfloat16),
        "ones_m": np.ones((1, 128), ml_dtypes.bfloat16),
        "ones_mf": np.ones((1, 128), np.float32),
        "ones_kf": np.ones((128, 1), np.float32),
        "swapi": swapi,
        "magic": np.ascontiguousarray(np.stack([np.full(128, MAGIC, np.float32),
                                                np.full(128, -MAGIC, np.float32)], axis=1)),
        "bo": np.ascontiguousarray(bO[None, :]),
    }
    in_maps = []
    for c in range(NCORES):
        hs = slice(2 * c, 2 * c + 2)
        d = dict(common)
        d.update({
            "x": np.ascontiguousarray(x[c * TC:(c + 1) * TC]),
            "wq": np.ascontiguousarray(WQ[hs][:, :, perm]),
            "wk": np.ascontiguousarray(WK[hs][:, :, perm]),
            "wv": np.ascontiguousarray(WV[hs]),
            "wo": np.ascontiguousarray(WO[hs]),
            "bq": np.ascontiguousarray(bQ[hs][:, perm]),
            "bk": np.ascontiguousarray(bK[hs][:, perm]),
            "bv": np.ascontiguousarray(bV[hs]),
            "wg": np.ascontiguousarray(WG[c]),
            "wu": np.ascontiguousarray(WU[c]),
            "wd": np.ascontiguousarray(WD[c]),
            "eid": np.full((1, 1), float(c), np.float32),
        })
        in_maps.append(d)
    return in_maps


def kernel(**inputs):
    in_maps = _host_inputs(inputs)
    nc, _ = build_program(debug=DEBUG)
    res = run_bass_kernel_spmd(nc, in_maps, core_ids=list(range(NCORES)))
    outv = np.concatenate([np.asarray(r["out"]) for r in res.results], axis=0)
    return outv[None, :, :].astype(np.float32)


# version 10
# speedup vs baseline: 1.4276x; 1.0018x over previous
import numpy as np
import ml_dtypes
from contextlib import ExitStack
import concourse.bass as bass
import concourse.mybir as mybir
from concourse.bass_utils import run_bass_kernel_spmd

F32 = mybir.dt.float32
BF16 = mybir.dt.bfloat16
U32 = mybir.dt.uint32
AF = mybir.ActivationFunctionType
ALU = mybir.AluOpType
AX = mybir.AxisListType

MAGIC = 1.5 * 2.0 ** 23
NCORES = 8
T = 8192
TC = T // NCORES
D = 2048
NHL = 2
EPS = 1e-5
ISQ = float(1.0 / np.sqrt(128.0))

DEBUG = False


class S:
    def __init__(self, stack, nc, name):
        self.h = stack.enter_context(nc.semaphore(name))
        self.n = 0

    def inc(self, instr, k=1):
        instr.then_inc(self.h, k)
        self.n += k
        return self.n

    def wait(self, eng, val=None):
        eng.wait_ge(self.h, self.n if val is None else val)


def build_program(debug=False):
    nc = bass.Bass("TRN2", target_bir_lowering=False, debug=False)

    def din(name, shape, dt=F32):
        return nc.dram_tensor(name, shape, dt, kind="ExternalInput").ap()

    x_in = din("x", [TC, D])
    wq_in = din("wq", [NHL, D, 128])
    wk_in = din("wk", [NHL, D, 128])
    wv_in = din("wv", [NHL, D, 128])
    wo_in = din("wo", [NHL, 128, D])
    bq_in = din("bq", [NHL, 128])
    bk_in = din("bk", [NHL, 128])
    bv_in = din("bv", [NHL, 128])
    bo_in = din("bo", [1, D])
    rw_in = din("rw", [NCORES, D])
    wg_in = din("wg", [D, D])
    wu_in = din("wu", [D, D])
    wd_in = din("wd", [D, D])
    cos_in = din("cos", [128, T])
    sin_in = din("sin", [128, T])
    masks_in = din("masks", [4, 128, 512], BF16)
    identb_in = din("identb", [128, 128], BF16)
    identf_in = din("identf", [128, 128], F32)
    ones_k_in = din("ones_k", [128, 1], BF16)
    ones_m_in = din("ones_m", [1, 128], BF16)
    ones_mf_in = din("ones_mf", [1, 128], F32)
    ones_kf_in = din("ones_kf", [128, 1], F32)
    eid_in = din("eid", [1, 1], F32)
    magic_in = din("magic", [128, 2], F32)
    swapi_in = din("swapi", [128, 128], BF16)

    out = nc.dram_tensor("out", [TC, D], F32, kind="ExternalOutput").ap()

    def dint(name, shape, dt):
        return nc.dram_tensor(name, shape, dt, kind="Internal").ap()

    ag1_src = dint("ag1_src", [D, TC], BF16)
    ag1_dst = nc.dram_tensor("ag1_dst", [NCORES, D, TC], BF16, kind="Internal", addr_space="Shared").ap()
    part_src = dint("part_src", [T, D], BF16)
    rs1_dst = dint("rs1_dst", [TC, D], BF16)
    r2_buf = dint("r2_buf", [TC, D], F32)
    ag2_src = dint("ag2_src", [D, TC], BF16)
    ag2_dst = nc.dram_tensor("ag2_dst", [NCORES, D, TC], BF16, kind="Internal", addr_space="Shared").ap()
    meta_src = dint("meta_src", [2, TC], F32)
    meta_dst = dint("meta_dst", [NCORES, 2, TC], F32)
    wqt = {m: dint(f"wqt_{m}", [4, D, 512], BF16) for m in "gud"}
    y_src = dint("y_src", [T, D], BF16)
    rs2_dst = dint("rs2_dst", [TC, D], BF16)

    dbg = {}
    if debug:
        def dout(name, shape, dt):
            dbg[name] = nc.dram_tensor("dbg_" + name, shape, dt, kind="ExternalOutput").ap()
        dout("ag1", [NCORES, D, TC], BF16)
        dout("ag1s", [D, TC], BF16)
        dout("q", [NHL, 128, T], BF16)
        dout("k", [NHL, 128, T], BF16)
        dout("v", [NHL, 128, T], BF16)
        dout("negb", [NHL, 128], F32)
        dout("z", [NHL, 128, T], BF16)
        dout("rs1", [TC, D], BF16)
        dout("r2", [TC, D], F32)
        dout("ag2", [NCORES, D, TC], BF16)
        dout("meta", [NCORES, 2, TC], F32)
        dout("wqtg", [4, D, 512], BF16)
        dout("wqtd", [4, D, 512], BF16)
        dout("winv", [128, 4], F32)
        dout("ysrc", [TC, D], BF16)
        dout("rs2", [TC, D], BF16)

    stack = ExitStack()
    with stack:
        def sbuf(st, name, shape, dt):
            return st.enter_context(nc.sbuf_tensor(name, shape, dt))

        def sem(name):
            return S(stack, nc, name)

        pq = [stack.enter_context(nc.psum_tensor(f"pq{i}", [128, 1024], F32))
              for i in range(4)]
        pb = [pq[i // 2][:, (i % 2) * 512:(i % 2 + 1) * 512] for i in range(8)]

        def pbb(i):
            return pb[i].bitcast(BF16)

        def cmaj(ap2d, p=128):
            c = ap2d.shape[0] // p
            return ap2d.rearrange("(c p) q -> c p q", p=p).transpose([1, 0, 2])

        def s3(ap2d, c):
            return ap2d.rearrange("p (c q) -> p c q", c=c)

        identb = sbuf(stack, "identb_s", [128, 128], BF16)
        identf = sbuf(stack, "identf_s", [128, 128], F32)
        ones_k = sbuf(stack, "ones_k_s", [128, 1], BF16)
        ones_m = sbuf(stack, "ones_m_s", [1, 128], BF16)
        ones_mf = sbuf(stack, "ones_mf_s", [1, 128], F32)
        ones_kf = sbuf(stack, "ones_kf_s", [128, 1], F32)
        mask_sb = sbuf(stack, "mask_s", [128, 4 * 512], BF16)
        bqkv = sbuf(stack, "bqkv", [128, 6], F32)
        bvrow = sbuf(stack, "bvrow", [1, 2 * 128], F32)
        bo_bc = sbuf(stack, "bo_bc", [128, D], F32)
        rwt_sb = sbuf(stack, "rwt", [128, 16 * 8], F32)
        wo_sb = sbuf(stack, "wo_s", [128, NHL * D], BF16)
        negb_sb = sbuf(stack, "negb_s", [128, NHL], F32)
        scal = sbuf(stack, "scal", [1, 16], F32)
        winv = sbuf(stack, "winv_s", [128, 4], F32)
        eid_sb = sbuf(stack, "eid_s", [1, 1], F32)
        eidb = sbuf(stack, "eidb", [128, 1], F32)
        magic_sb = sbuf(stack, "magic_s", [128, 2], F32)
        p0 = ExitStack()
        bo_row = sbuf(p0, "bo_row_s", [1, D], F32)
        rw_sb = sbuf(p0, "rw_s", [NCORES, D], F32)

        s_ld = sem("s_ld")
        s_gld = sem("s_gld")
        s_sld = sem("s_sld")
        s_mm = sem("s_mm")
        s_v = sem("s_v")
        s_sc = sem("s_sc")
        s_cc = sem("s_cc")
        s_bar = sem("s_bar")

        blk = stack.enter_context(nc.Block())

        def sec(engname):
            def deco(f):
                getattr(blk, engname)(f)
            return deco

        def vf(e, op):
            s_v.inc(op, 1)
            s_v.wait(e)

        ENGS = ("sync", "scalar", "vector", "tensor", "gpsimd")

        def barrier():
            for en in ENGS:
                @sec(en)
                def _(e):
                    e.sem_inc(s_bar.h, 1)
            s_bar.n += len(ENGS)
            tgt = s_bar.n
            for en in ENGS:
                @sec(en)
                def _(e):
                    e.wait_ge(s_bar.h, tgt)

        @sec("sync")
        def _(e):
            for dst, src in ((magic_sb, magic_in), (identb, identb_in), (identf, identf_in),
                             (ones_k, ones_k_in), (ones_m, ones_m_in),
                             (ones_mf, ones_mf_in), (ones_kf, ones_kf_in),
                             (rw_sb, rw_in), (bo_row, bo_in), (eid_sb, eid_in)):
                s_ld.inc(e.dma_start(out=dst[:], in_=src[:, :]), 16)
            s_ld.inc(e.dma_start(out=s3(mask_sb[:], 4),
                                 in_=masks_in.transpose([1, 0, 2])), 16)
            for h in range(NHL):
                for j, b in ((0, bq_in), (2, bk_in), (4, bv_in)):
                    s_ld.inc(e.dma_start(out=bqkv[:, j + h:j + h + 1],
                                         in_=b[h].unsqueeze(1)), 16)
                s_ld.inc(e.dma_start(out=bvrow[0:1, h * 128:(h + 1) * 128],
                                     in_=bv_in[h].unsqueeze(0)), 16)

        @sec("gpsimd")
        def _(e):
            for h in range(NHL):
                s_gld.inc(e.dma_start(out=wo_sb[:, h * D:(h + 1) * D], in_=wo_in[h]), 16)

        @sec("tensor")
        def _(e):
            s_ld.wait(e)
            for cc in range(16):
                op = e.transpose(pb[0][:, cc * 8:(cc + 1) * 8],
                                 rw_sb[:, cc * 128:(cc + 1) * 128], identf[0:8, 0:8])
            s_mm.inc(op, 1)
            op = e.matmul(pb[2][:, 0:1], ones_mf[:], eid_sb[:], start=True, stop=True)
            s_mm.inc(op, 1)
            vbase = s_v.n
            for mc in range(4):
                op = e.matmul(pb[1], ones_mf[:], bo_row[:, mc * 512:(mc + 1) * 512],
                              start=True, stop=True)
                s_mm.inc(op, 1)
                e.wait_ge(s_v.h, vbase + mc + 1)

        @sec("vector")
        def _(e):
            s_mm.wait(e, s_mm.n - 4)
            e.tensor_copy(rwt_sb[:], pb[0][:, 0:128])
            e.tensor_copy(eidb[:], pb[2][:, 0:1])
            for mc in range(4):
                s_mm.wait(e, s_mm.n - 3 + mc)
                op = e.tensor_copy(bo_bc[:, mc * 512:(mc + 1) * 512], pb[1])
                s_v.inc(op, 1)

        barrier()
        p0.close()

        p1 = ExitStack()
        xt = [sbuf(p1, f"xt{i}", [128, D], F32) for i in range(2)]
        xn1b = [sbuf(p1, f"xn1b{i}", [128, D], BF16) for i in range(2)]
        stage1 = [sbuf(p1, f"stage1_{i}", [128, D], BF16) for i in range(2)]
        mv_sb = [sbuf(p1, f"mv{i}", [128, 2], F32) for i in range(2)]
        ve_sb = [sbuf(p1, f"ve{i}", [128, 4], F32) for i in range(2)]

        for t in range(8):
            @sec("sync")
            def _(e, t=t):
                if t >= 2:
                    e.wait_ge(s_v.h, s_v.n)
                s_ld.inc(e.dma_start(out=xt[t % 2][:],
                                     in_=x_in[t * 128:(t + 1) * 128, :]), 16)

            @sec("vector")
            def _(e, t=t):
                s_ld.wait(e)
                if t >= 2:
                    s_mm.wait(e)
                vf(e, e.reduce_sum(ve_sb[t % 2][:, 3:4], xt[t % 2][:], axis=AX.X))
                e.tensor_scalar(mv_sb[t % 2][:, 0:1], ve_sb[t % 2][:, 3:4], 1.0 / D, None, ALU.mult)
                op = e.tensor_scalar(mv_sb[t % 2][:, 1:2], ve_sb[t % 2][:, 3:4], -1.0 / D, None, ALU.mult)
                s_v.inc(op, 1)

            @sec("scalar")
            def _(e, t=t):
                s_v.wait(e)
                op = e.activation(xn1b[t % 2][:], xt[t % 2][:], AF.Square,
                                  bias=mv_sb[t % 2][:, 1:2], accum_out=ve_sb[t % 2][:, 0:1])
                s_sc.inc(op, 1)

            @sec("vector")
            def _(e, t=t):
                s_sc.wait(e)
                op = e.tensor_scalar(ve_sb[t % 2][:, 0:1], ve_sb[t % 2][:, 0:1], 1.0 / D, EPS,
                                     ALU.mult, ALU.add)
                s_v.inc(op, 1)

            @sec("scalar")
            def _(e, t=t):
                s_v.wait(e)
                op = e.activation(ve_sb[t % 2][:, 1:2], ve_sb[t % 2][:, 0:1], AF.Sqrt)
                s_sc.inc(op, 1)

            @sec("vector")
            def _(e, t=t):
                s_sc.wait(e)
                vf(e, e.reciprocal(ve_sb[t % 2][:, 2:3], ve_sb[t % 2][:, 1:2]))
                op = e.tensor_scalar(xn1b[t % 2][:], xt[t % 2][:], mv_sb[t % 2][:, 0:1],
                                     ve_sb[t % 2][:, 2:3], ALU.subtract, ALU.mult)
                s_v.inc(op, 1)

            @sec("tensor")
            def _(e, t=t):
                s_v.wait(e)
                bank = (t % 2) * 2
                for cc in range(16):
                    op = e.transpose(
                        pbb(bank + cc // 8)[:, (cc % 8) * 128:(cc % 8 + 1) * 128],
                        xn1b[t % 2][:, cc * 128:(cc + 1) * 128], identb[:])
                s_mm.inc(op, 1)

            @sec("vector")
            def _(e, t=t):
                s_mm.wait(e)
                if t >= 2:
                    s_ld.wait(e)
                bank = (t % 2) * 2
                e.tensor_copy(stage1[t % 2][:, 0:1024], pbb(bank)[:, 0:1024])
                op = e.tensor_copy(stage1[t % 2][:, 1024:2048], pbb(bank + 1)[:, 0:1024])
                s_v.inc(op, 1)

            @sec("sync")
            def _(e, t=t):
                s_v.wait(e)
                op = e.dma_start(
                    out=cmaj(ag1_src[:, t * 128:(t + 1) * 128]),
                    in_=s3(stage1[t % 2][:], 16))
                s_ld.inc(op, 16)

        @sec("gpsimd")
        def _(e):
            e.wait_ge(s_ld.h, s_ld.n)
            op = e.collective_compute(
                "AllGather", ALU.bypass, replica_groups=[list(range(NCORES))],
                ins=[ag1_src[:, :]], outs=[ag1_dst[:, :, :]])
            s_cc.inc(op, 1)

        wst = ExitStack()
        wf = [sbuf(wst, f"wf{i}", [128, D], F32) for i in range(4)]
        wqrow2 = None
        wqrow = [sbuf(wst, f"wqrow{i}", [128, D], BF16) for i in range(2)]
        wacc = sbuf(wst, "wacc", [128, 16], F32)
        waccs = sbuf(wst, "waccs", [128, 1], F32)
        wstage = sbuf(wst, "wstage", [128, 16 * 128], BF16)
        swb = sbuf(wst, "swb", [128, 1], F32)

        for mi, (mname, w_in) in enumerate((("g", wg_in), ("u", wu_in), ("d", wd_in))):
            @sec("gpsimd")
            def _(e, w_in=w_in):
                e.wait_ge(s_v.h, s_v.n)
                scb = s_sc.n
                for rc in range(16):
                    if rc >= 4:
                        e.wait_ge(s_sc.h, scb + rc - 3)
                    s_gld.inc(e.dma_start(out=wf[rc % 4][:],
                                         in_=w_in[rc * 128:(rc + 1) * 128, :]), 16)

            @sec("scalar")
            def _(e):
                s_mm.wait(e)
                for rc in range(16):
                    s_gld.wait(e, s_gld.n - 16 * (15 - rc))
                    op = e.activation(wqrow[rc % 2][:], wf[rc % 4][:], AF.Abs,
                                      accum_out=wacc[:, rc:rc + 1])
                    s_sc.inc(op, 1)

            @sec("vector")
            def _(e):
                s_sc.wait(e)
                vf(e, e.reduce_sum(waccs[:], wacc[:], axis=AX.X))

            @sec("tensor")
            def _(e):
                s_v.wait(e)
                op = e.matmul(pb[6][0:1, 0:1], waccs[:], ones_kf[:], start=True, stop=True)
                s_mm.inc(op, 1)

            @sec("vector")
            def _(e):
                s_mm.wait(e)
                vf(e, e.tensor_scalar(scal[:, 0:1], pb[6][0:1, 0:1], 1.0 / (D * D), None, ALU.mult))
                vf(e, e.tensor_scalar(scal[:, 1:2], scal[:, 0:1], EPS, None, ALU.max))
                op = e.reciprocal(scal[:, 2:3], scal[:, 1:2])
                s_v.inc(op, 1)

            @sec("tensor")
            def _(e, mi=mi):
                s_v.wait(e)
                op = e.matmul(pb[7][:, 0:1], ones_mf[:], scal[:, 2:3], start=True, stop=True)
                s_mm.inc(op, 1)
                op = e.matmul(pb[7][:, 1:2], ones_mf[:], scal[:, 1:2], start=True, stop=True)
                s_mm.inc(op, 1)

            @sec("vector")
            def _(e, mi=mi):
                s_mm.wait(e)
                e.tensor_copy(swb[:], pb[7][:, 0:1])
                op = e.tensor_copy(winv[:, mi:mi + 1], pb[7][:, 1:2])
                s_v.inc(op, 1)

            @sec("gpsimd")
            def _(e, w_in=w_in):
                e.wait_ge(s_sc.h, s_sc.n)
                vb = s_v.n
                for rc in range(16):
                    if rc >= 4:
                        e.wait_ge(s_v.h, vb + 4 * (rc - 3))
                    s_gld.inc(e.dma_start(out=wf[rc % 4][:],
                                         in_=w_in[rc * 128:(rc + 1) * 128, :]), 16)

            trp_done = {}
            for rc in range(16):
                @sec("scalar")
                def _(e, rc=rc):
                    s_gld.wait(e, s_gld.n - 16 * (15 - rc))
                    e.activation(wf[rc % 4][:], wf[rc % 4][:], AF.Identity,
                                 scale=swb[:], bias=magic_sb[:, 0:1])
                    op = e.activation(wf[rc % 4][:], wf[rc % 4][:], AF.Identity,
                                     bias=magic_sb[:, 1:2])
                    s_sc.inc(op, 1)

                @sec("vector")
                def _(e, rc=rc):
                    s_sc.wait(e)
                    if rc >= 2:
                        e.wait_ge(s_mm.h, trp_done[rc - 2])
                    op = e.tensor_scalar(wqrow[rc % 2][:], wf[rc % 4][:], 1.0, -1.0,
                                         ALU.min, ALU.max)
                    s_v.inc(op, 3)

                @sec("tensor")
                def _(e, rc=rc):
                    s_v.wait(e)
                    base = 4 + (rc % 2) * 2
                    for cc in range(16):
                        op = e.transpose(
                            pbb(base + cc // 8)[:, (cc % 8) * 128:(cc % 8 + 1) * 128],
                            wqrow[rc % 2][:, cc * 128:(cc + 1) * 128], identb[:])
                    s_mm.inc(op, 1)
                    trp_done[rc] = s_mm.n

                @sec("vector")
                def _(e, rc=rc):
                    s_mm.wait(e)
                    if rc >= 1:
                        s_sld.wait(e)
                    base = 4 + (rc % 2) * 2
                    e.tensor_copy(wstage[:, 0:1024], pbb(base)[:, 0:1024])
                    op = e.tensor_copy(wstage[:, 1024:2048], pbb(base + 1)[:, 0:1024])
                    s_v.inc(op, 1)

                @sec("scalar")
                def _(e, rc=rc, wqt_m=wqt[mname]):
                    s_v.wait(e)
                    op = e.dma_start(
                        out=cmaj(wqt_m[rc // 4][:, (rc % 4) * 128:(rc % 4 + 1) * 128]),
                        in_=s3(wstage[:], 16))
                    s_sld.inc(op, 16)

        barrier()

        if debug:
            @sec("gpsimd")
            def _(e):
                s_cc.wait(e)
                s_gld.inc(e.dma_start(out=dbg["ag1"][:, :, :], in_=ag1_dst[:, :, :]), 16)
                s_gld.inc(e.dma_start(out=dbg["ag1s"][:, :], in_=ag1_src[:, :]), 16)
                s_gld.inc(e.dma_start(out=dbg["wqtg"][:, :, :], in_=wqt["g"][:, :, :]), 16)
                s_gld.inc(e.dma_start(out=dbg["wqtd"][:, :, :], in_=wqt["d"][:, :, :]), 16)
                s_gld.inc(e.dma_start(out=dbg["winv"][:, :], in_=winv[:]), 16)

        wst.close()
        p1.close()

        p2 = ExitStack()
        q_sb = sbuf(p2, "q_sb", [128, T], BF16)
        k_sb = sbuf(p2, "k_sb", [128, T], BF16)
        v_sb = sbuf(p2, "v_sb", [128, T], BF16)
        z_sb = [sbuf(p2, f"z{h}", [128, T], BF16) for h in range(NHL)]
        bvb = sbuf(p2, "bvb", [128, 128], F32)
        p_sb = [sbuf(p2, f"p{i}", [128, 1024], BF16) for i in range(2)]
        rd_sb = sbuf(p2, "rd", [1, 512], F32)
        rdb_sb = sbuf(p2, "rdb", [128, 512], F32)
        p2q = ExitStack()
        cos_sb = sbuf(p2q, "cos_s", [128, T // 2], F32)
        sin_sb = sbuf(p2q, "sin_s", [128, T // 2], F32)
        swapi = sbuf(p2q, "swapi_s", [128, 128], BF16)
        tq_b = sbuf(p2q, "tq_b", [128, 512], BF16)
        wqkv_h = sbuf(p2q, "wqkv_h", [128, 3 * D], BF16)
        xn_sb = [sbuf(p2q, f"xn{i}", [128, 16 * 512], BF16) for i in range(2)]
        tmpq = sbuf(p2q, "tmpq", [128, 512], F32)
        rop = sbuf(p2q, "rop", [128, 512], F32)
        sqt = sbuf(p2q, "sqt", [128, 512], BF16)
        qn_row = sbuf(p2q, "qn_row", [1, 16], F32)
        kn_row = sbuf(p2q, "kn_row", [1, 16], F32)

        @sec("sync")
        def _(e):
            s_ld.inc(e.dma_start(out=swapi[:], in_=swapi_in[:, :]), 16)
            s_ld.inc(e.dma_start(out=cos_sb[:], in_=cos_in[:, 0:T // 2]), 16)
            s_ld.inc(e.dma_start(out=sin_sb[:], in_=sin_in[:, 0:T // 2]), 16)

        for h in range(NHL):
            @sec("gpsimd")
            def _(e, h=h):
                if h > 0:
                    e.wait_ge(s_mm.h, s_mm.n)
                for j, w in ((0, wq_in), (1, wk_in), (2, wv_in)):
                    s_gld.inc(e.dma_start(
                        out=s3(wqkv_h[:, j * D:(j + 1) * D], 16),
                        in_=cmaj(w[h])), 16)

            @sec("tensor")
            def _(e, h=h):
                s_ld.wait(e)
                op = e.matmul(pb[6][:, 0:128], ones_mf[:],
                              bvrow[0:1, h * 128:(h + 1) * 128], start=True, stop=True)
                s_mm.inc(op, 1)

            @sec("vector")
            def _(e, h=h):
                s_mm.wait(e)
                op = e.tensor_copy(bvb[:], pb[6][:, 0:128])
                s_v.inc(op, 1)

            for grp in range(16):
                r, wnd = grp // 2, grp % 2

                @sec("sync")
                def _(e, grp=grp, r=r, wnd=wnd, h=h):
                    s_cc.wait(e)
                    if grp >= 2:
                        e.wait_ge(s_mm.h, s_mm.n)
                    if (grp == 8) or (h > 0 and grp == 0):
                        half = 1 if grp == 8 else 0
                        e.wait_ge(s_v.h, s_v.n)
                        s_ld.inc(e.dma_start(out=cos_sb[:],
                                 in_=cos_in[:, half * (T // 2):(half + 1) * (T // 2)]), 16)
                        s_ld.inc(e.dma_start(out=sin_sb[:],
                                 in_=sin_in[:, half * (T // 2):(half + 1) * (T // 2)]), 16)
                    op = e.dma_start(
                        out=s3(xn_sb[grp % 2][:], 16),
                        in_=cmaj(ag1_dst[r, :, wnd * 512:(wnd + 1) * 512]))
                    s_ld.inc(op, 16)

                @sec("tensor")
                def _(e, grp=grp, h=h):
                    s_ld.wait(e)
                    s_gld.wait(e)
                    s_v.wait(e)
                    xb = xn_sb[grp % 2]
                    for cc in range(16):
                        st, sp = cc == 0, cc == 15
                        e.matmul(pb[0][:, :], wqkv_h[:, cc * 128:(cc + 1) * 128],
                                 xb[:, cc * 512:(cc + 1) * 512], start=st, stop=sp)
                        e.matmul(pb[1], wqkv_h[:, D + cc * 128:D + (cc + 1) * 128],
                                 xb[:, cc * 512:(cc + 1) * 512], start=st, stop=sp)
                        for j in range(4):
                            op = e.matmul(pb[2][:, j * 128:(j + 1) * 128],
                                          xb[:, cc * 512 + j * 128:cc * 512 + (j + 1) * 128],
                                          wqkv_h[:, 2 * D + cc * 128:2 * D + (cc + 1) * 128],
                                          start=(cc == 0 and j == 0),
                                          stop=(cc == 15 and j == 3))
                    s_mm.inc(op, 1)

                for psrc, dst_sb, brow, nrow in ((0, q_sb, 0, qn_row), (1, k_sb, 2, kn_row)):
                    @sec("vector")
                    def _(e, h=h, psrc=psrc, brow=brow):
                        s_mm.wait(e)
                        op = e.tensor_scalar(tq_b[:], pb[psrc][:, :],
                                             bqkv[:, brow + h:brow + h + 1], None, ALU.add)
                        s_v.inc(op, 1)

                    @sec("tensor")
                    def _(e):
                        s_v.wait(e)
                        op = e.matmul(pb[5][:, :], swapi[:], tq_b[:], start=True, stop=True)
                        s_mm.inc(op, 1)

                    @sec("vector")
                    def _(e, grp=grp, dst_sb=dst_sb):
                        s_mm.wait(e)
                        g0 = grp * 512
                        c0 = (grp % 8) * 512
                        e.tensor_tensor(rop[:], tq_b[:], cos_sb[:, c0:c0 + 512], ALU.mult)
                        vf(e, e.scalar_tensor_tensor(rdb_sb[:], pb[5][:, :], 0.0,
                                               sin_sb[:, c0:c0 + 512], ALU.bypass, ALU.mult))
                        vf(e, e.tensor_add(dst_sb[:, g0:g0 + 512], rop[:], rdb_sb[:]))
                        op = e.tensor_mul(sqt[:], dst_sb[:, g0:g0 + 512],
                                          dst_sb[:, g0:g0 + 512])
                        s_v.inc(op, 1)

                    @sec("tensor")
                    def _(e):
                        s_v.wait(e)
                        op = e.matmul(pb[3][0:1, :], ones_k[:], sqt[:], start=True, stop=True)
                        s_mm.inc(op, 1)

                    @sec("vector")
                    def _(e, grp=grp, nrow=nrow):
                        s_mm.wait(e)
                        op = e.reduce_max(nrow[0:1, grp:grp + 1], pb[3][0:1, :], axis=AX.X)
                        s_v.inc(op, 1)

                @sec("vector")
                def _(e, grp=grp):
                    g0 = grp * 512
                    for j in range(4):
                        op = e.tensor_add(v_sb[:, g0 + j * 128:g0 + (j + 1) * 128],
                                          pb[2][:, j * 128:(j + 1) * 128], bvb[:])
                    s_v.inc(op, 1)

            @sec("vector")
            def _(e):
                e.reduce_max(scal[:, 4:5], qn_row[:], axis=AX.X)
                vf(e, e.reduce_max(scal[:, 5:6], kn_row[:], axis=AX.X))
                op = e.tensor_tensor(scal[:, 6:7], scal[:, 4:5], scal[:, 5:6], ALU.mult)
                s_v.inc(op, 1)

            @sec("scalar")
            def _(e):
                s_v.wait(e)
                op = e.activation(scal[:, 7:8], scal[:, 6:7], AF.Sqrt)
                s_sc.inc(op, 1)

            @sec("vector")
            def _(e):
                s_sc.wait(e)
                op = e.tensor_scalar(scal[:, 8:9], scal[:, 7:8], -1.02 * ISQ, -1.0,
                                     ALU.mult, ALU.add)
                s_v.inc(op, 1)

            @sec("tensor")
            def _(e):
                s_v.wait(e)
                op = e.matmul(pb[3][:, 0:1], ones_mf[:], scal[:, 8:9], start=True, stop=True)
                s_mm.inc(op, 1)

            @sec("vector")
            def _(e, h=h):
                s_mm.wait(e)
                op = e.tensor_copy(negb_sb[:, h:h + 1], pb[3][:, 0:1])
                s_v.inc(op, 1)

            if debug:
                @sec("gpsimd")
                def _(e, h=h):
                    e.wait_ge(s_v.h, s_v.n)
                    s_gld.inc(e.dma_start(out=dbg["q"][h], in_=q_sb[:]), 16)
                    s_gld.inc(e.dma_start(out=dbg["k"][h], in_=k_sb[:]), 16)
                    s_gld.inc(e.dma_start(out=dbg["v"][h], in_=v_sb[:]), 16)
                    s_gld.inc(e.dma_start(out=dbg["negb"][h].unsqueeze(1),
                                          in_=negb_sb[:, h:h + 1]), 16)

            barrier()

            for qg in range(16):
                q0 = qg * 512
                nkb = 4 * (qg + 1)
                npr = nkb // 2
                exp_base = s_sc.n
                mask_done = {}
                for step in range(npr + 1):
                    if step < npr:
                        g2 = step
                        A = pq[0] if g2 % 2 == 0 else pq[3]

                        @sec("tensor")
                        def _(e, g2=g2, q0=q0, exp_base=exp_base, A=A):
                            if g2 >= 2:
                                e.wait_ge(s_sc.h, exp_base + g2 - 1)
                            kb = 2 * g2
                            e.matmul(A[:, 0:512], k_sb[:, kb * 128:(kb + 1) * 128],
                                     q_sb[:, q0:q0 + 512], start=True, stop=True)
                            op = e.matmul(A[:, 512:1024],
                                          k_sb[:, (kb + 1) * 128:(kb + 2) * 128],
                                          q_sb[:, q0:q0 + 512], start=True, stop=True)
                            s_mm.inc(op, 1)

                        @sec("scalar")
                        def _(e, g2=g2, h=h, A=A):
                            s_mm.wait(e)
                            op = e.activation(p_sb[g2 % 2][:], A[:, 0:1024], AF.Exp,
                                              bias=negb_sb[:, h:h + 1], scale=ISQ)
                            s_sc.inc(op, 1)

                        for half in range(2):
                            kb = 2 * g2 + half
                            if kb >= 4 * qg:
                                @sec("vector")
                                def _(e, g2=g2, kb=kb, qg=qg, half=half):
                                    s_sc.wait(e)
                                    o = kb - 4 * qg
                                    op = e.tensor_mul(
                                        p_sb[g2 % 2][:, half * 512:(half + 1) * 512],
                                        p_sb[g2 % 2][:, half * 512:(half + 1) * 512],
                                        mask_sb[:, o * 512:(o + 1) * 512])
                                    s_v.inc(op, 1)
                                    mask_done[g2] = s_v.n

                    if step >= 1:
                        g2p = step - 1

                        @sec("tensor")
                        def _(e, g2p=g2p, qg=qg, nkb=nkb, exp_base=exp_base):
                            if g2p in mask_done:
                                e.wait_ge(s_v.h, mask_done[g2p])
                            else:
                                e.wait_ge(s_sc.h, exp_base + g2p + 1)
                            if g2p == 0:
                                s_v.wait(e)
                            for half in range(2):
                                kb = 2 * g2p + half
                                st, sp = kb == 0, kb == nkb - 1
                                e.matmul(pb[2], v_sb[:, kb * 128:(kb + 1) * 128],
                                         p_sb[g2p % 2][:, half * 512:(half + 1) * 512],
                                         start=st, stop=sp)
                                op = e.matmul(pb[3][0:1, :], ones_k[:],
                                              p_sb[g2p % 2][:, half * 512:(half + 1) * 512],
                                              start=st, stop=sp)
                            s_mm.inc(op, 1)

                @sec("vector")
                def _(e):
                    s_mm.wait(e)
                    op = e.reciprocal(rd_sb[:], pb[3][0:1, :])
                    s_v.inc(op, 1)

                @sec("tensor")
                def _(e):
                    s_v.wait(e)
                    op = e.matmul(pb[4], ones_mf[:], rd_sb[:], start=True, stop=True)
                    s_mm.inc(op, 1)

                @sec("vector")
                def _(e, h=h, q0=q0):
                    s_mm.wait(e)
                    vf(e, e.tensor_copy(rdb_sb[:], pb[4]))
                    op = e.tensor_mul(z_sb[h][:, q0:q0 + 512], pb[2], rdb_sb[:])
                    s_v.inc(op, 1)

            if debug:
                @sec("gpsimd")
                def _(e, h=h):
                    e.wait_ge(s_v.h, s_v.n)
                    s_gld.inc(e.dma_start(out=dbg["z"][h], in_=z_sb[h][:]), 16)

            barrier()

        p2q.close()
        po_sb = [sbuf(p2, f"po{i}", [128, D], BF16) for i in range(2)]
        for tt in range(64):
            @sec("tensor")
            def _(e, tt=tt):
                if tt >= 2:
                    s_v.wait(e, s_v.n - 1)
                for mc in range(4):
                    bank = (tt % 2) * 4 + mc
                    for h in range(NHL):
                        op = e.matmul(pb[bank], z_sb[h][:, tt * 128:(tt + 1) * 128],
                                      wo_sb[:, h * D + mc * 512:h * D + (mc + 1) * 512],
                                      start=(h == 0), stop=(h == 1))
                s_mm.inc(op, 1)

            @sec("vector")
            def _(e, tt=tt):
                s_mm.wait(e)
                if tt >= 2:
                    s_ld.wait(e)
                for mc in range(4):
                    op = e.tensor_copy(po_sb[tt % 2][:, mc * 512:(mc + 1) * 512],
                                       pb[(tt % 2) * 4 + mc][:, :])
                s_v.inc(op, 1)

            @sec("sync")
            def _(e, tt=tt):
                s_v.wait(e)
                s_ld.inc(e.dma_start(out=part_src[tt * 128:(tt + 1) * 128, :],
                                     in_=po_sb[tt % 2][:]), 16)

        @sec("gpsimd")
        def _(e):
            e.wait_ge(s_ld.h, s_ld.n)
            op = e.collective_compute(
                "ReduceScatter", ALU.add, replica_groups=[list(range(NCORES))],
                ins=[part_src[:, :]], outs=[rs1_dst[:, :]])
            s_cc.inc(op, 1)

        if debug:
            @sec("gpsimd")
            def _(e):
                s_cc.wait(e)
                s_gld.inc(e.dma_start(out=dbg["rs1"][:, :], in_=rs1_dst[:, :]), 16)

        p2.close()
        barrier()

        p5 = ExitStack()
        xt2 = [sbuf(p5, f"xt2_{i}", [128, D], F32) for i in range(2)]
        at2 = [sbuf(p5, f"at2_{i}", [128, D], BF16) for i in range(2)]
        r2t = [sbuf(p5, f"r2t{i}", [128, D], F32) for i in range(2)]
        xn2f = [sbuf(p5, f"xn2f{i}", [128, D], F32) for i in range(2)]
        xqf = sbuf(p5, "xqf", [128, D], F32)
        xqb = [sbuf(p5, f"xqb{i}", [128, D], BF16) for i in range(2)]
        stage2 = [sbuf(p5, f"stage2_{i}", [128, D], BF16) for i in range(2)]
        xn2fm = sbuf(p5, "xn2fm", [128, 16 * 128], F32)
        mv2 = [sbuf(p5, f"mv2{i}", [128, 2], F32) for i in range(2)]
        ve2 = [sbuf(p5, f"ve2{i}", [128, 4], F32) for i in range(2)]
        qs2 = sbuf(p5, "qs2", [128, 8], F32)
        lg_sb = sbuf(p5, "lg", [128, 8], F32)
        mx8 = sbuf(p5, "mx8", [128, 8], F32)
        mi8 = sbuf(p5, "mi8", [128, 8], U32)
        idxf = sbuf(p5, "idxf", [128, 1], F32)

        for t in range(8):
            @sec("sync")
            def _(e, t=t):
                s_cc.wait(e)
                if t >= 2:
                    e.wait_ge(s_v.h, s_v.n)
                s_ld.inc(e.dma_start(out=xt2[t % 2][:],
                                     in_=x_in[t * 128:(t + 1) * 128, :]), 16)
                s_ld.inc(e.dma_start(out=at2[t % 2][:],
                                     in_=rs1_dst[t * 128:(t + 1) * 128, :]), 16)

            @sec("vector")
            def _(e, t=t):
                s_ld.wait(e)
                if t >= 1:
                    s_sld.wait(e)
                    s_mm.wait(e)
                vf(e, e.tensor_add(r2t[t % 2][:], xt2[t % 2][:], at2[t % 2][:]))
                op = e.tensor_add(r2t[t % 2][:], r2t[t % 2][:], bo_bc[:])
                s_v.inc(op, 1)

            @sec("scalar")
            def _(e, t=t):
                s_v.wait(e)
                op = e.dma_start(out=r2_buf[t * 128:(t + 1) * 128, :], in_=r2t[t % 2][:])
                s_sld.inc(op, 16)

            @sec("vector")
            def _(e, t=t):
                vf(e, e.reduce_sum(ve2[t % 2][:, 3:4], r2t[t % 2][:], axis=AX.X))
                e.tensor_scalar(mv2[t % 2][:, 0:1], ve2[t % 2][:, 3:4], 1.0 / D, None, ALU.mult)
                op = e.tensor_scalar(mv2[t % 2][:, 1:2], ve2[t % 2][:, 3:4], -1.0 / D, None, ALU.mult)
                s_v.inc(op, 1)

            @sec("scalar")
            def _(e, t=t):
                s_v.wait(e)
                op = e.activation(xn2f[t % 2][:], r2t[t % 2][:], AF.Square,
                                  bias=mv2[t % 2][:, 1:2], accum_out=ve2[t % 2][:, 0:1])
                s_sc.inc(op, 1)

            @sec("vector")
            def _(e, t=t):
                s_sc.wait(e)
                op = e.tensor_scalar(ve2[t % 2][:, 0:1], ve2[t % 2][:, 0:1], 1.0 / D, EPS,
                                     ALU.mult, ALU.add)
                s_v.inc(op, 1)

            @sec("scalar")
            def _(e, t=t):
                s_v.wait(e)
                op = e.activation(ve2[t % 2][:, 1:2], ve2[t % 2][:, 0:1], AF.Sqrt)
                s_sc.inc(op, 1)

            @sec("vector")
            def _(e, t=t):
                s_sc.wait(e)
                vf(e, e.reciprocal(ve2[t % 2][:, 2:3], ve2[t % 2][:, 1:2]))
                vf(e, e.tensor_scalar(xn2f[t % 2][:], r2t[t % 2][:], mv2[t % 2][:, 0:1],
                                      ve2[t % 2][:, 2:3], ALU.subtract, ALU.mult))
                vf(e, e.reduce_max(qs2[:, 0:1], xn2f[t % 2][:], axis=AX.X,
                                   apply_absolute_value=True))
                vf(e, e.tensor_scalar(qs2[:, 1:2], qs2[:, 0:1], EPS, None, ALU.max))
                vf(e, e.reciprocal(qs2[:, 2:3], qs2[:, 1:2]))
                e.tensor_scalar(qs2[:, 3:4], qs2[:, 2:3], 127.0, None, ALU.mult)
                vf(e, e.tensor_scalar(qs2[:, 4:5], qs2[:, 1:2], 1.0 / 127.0, None, ALU.mult))
                vf(e, e.tensor_scalar(xqf[:], xn2f[t % 2][:], qs2[:, 3:4], None, ALU.mult))
                vf(e, e.tensor_scalar(xqf[:], xqf[:], MAGIC, -MAGIC, ALU.add, ALU.add))
                op = e.tensor_scalar(xqb[t % 2][:], xqf[:], 127.0, -128.0, ALU.min, ALU.max)
                s_v.inc(op, 1)

            @sec("scalar")
            def _(e, t=t):
                s_v.wait(e)
                op = e.dma_start(out=meta_src[0, t * 128:(t + 1) * 128].unsqueeze(1), in_=qs2[:, 4:5])
                s_sld.inc(op, 16)

            @sec("tensor")
            def _(e, t=t):
                s_v.wait(e)
                bank = 4 + (t % 2) * 2
                for cc in range(16):
                    op = e.transpose(
                        pbb(bank + cc // 8)[:, (cc % 8) * 128:(cc % 8 + 1) * 128],
                        xqb[t % 2][:, cc * 128:(cc + 1) * 128], identb[:])
                s_mm.inc(op, 1)

            @sec("vector")
            def _(e, t=t):
                s_mm.wait(e)
                if t >= 2:
                    s_ld.wait(e)
                bank = 4 + (t % 2) * 2
                e.tensor_copy(stage2[t % 2][:, 0:1024], pbb(bank)[:, 0:1024])
                op = e.tensor_copy(stage2[t % 2][:, 1024:2048], pbb(bank + 1)[:, 0:1024])
                s_v.inc(op, 1)

            @sec("sync")
            def _(e, t=t):
                s_v.wait(e)
                op = e.dma_start(
                    out=cmaj(ag2_src[:, t * 128:(t + 1) * 128]),
                    in_=s3(stage2[t % 2][:], 16))
                s_ld.inc(op, 16)

            @sec("tensor")
            def _(e, t=t):
                for cc4 in range(4):
                    for ccl in range(4):
                        cc = cc4 * 4 + ccl
                        op = e.transpose(pb[cc4][:, ccl * 128:(ccl + 1) * 128],
                                         xn2f[t % 2][:, cc * 128:(cc + 1) * 128], identf[:])
                s_mm.inc(op, 1)

            @sec("vector")
            def _(e, t=t):
                s_mm.wait(e)
                for cc4 in range(4):
                    op = e.tensor_copy(xn2fm[:, cc4 * 512:(cc4 + 1) * 512], pb[cc4][:, :])
                s_v.inc(op, 1)

            @sec("tensor")
            def _(e, t=t):
                s_v.wait(e)
                for cc in range(16):
                    op = e.matmul(pb[6][:, 0:8], xn2fm[:, cc * 128:(cc + 1) * 128],
                                  rwt_sb[:, cc * 8:(cc + 1) * 8],
                                  start=(cc == 0), stop=(cc == 15))
                s_mm.inc(op, 1)

            @sec("vector")
            def _(e, t=t):
                s_mm.wait(e)
                vf(e, e.tensor_copy(lg_sb[:], pb[6][:, 0:8]))
                vf(e, e.max(mx8[:], lg_sb[:]))
                vf(e, e.max_index(mi8[:], mx8[:], lg_sb[:]))
                op = e.tensor_copy(idxf[:], mi8[:, 0:1])
                s_v.inc(op, 1)

            @sec("scalar")
            def _(e, t=t):
                s_v.wait(e)
                op = e.dma_start(out=meta_src[1, t * 128:(t + 1) * 128].unsqueeze(1), in_=idxf[:])
                s_sld.inc(op, 16)

        @sec("gpsimd")
        def _(e):
            e.wait_ge(s_ld.h, s_ld.n)
            e.wait_ge(s_sld.h, s_sld.n)
            op = e.collective_compute(
                "AllGather", ALU.bypass, replica_groups=[list(range(NCORES))],
                ins=[meta_src[:, :]], outs=[meta_dst[:, :, :]])
            s_cc.inc(op, 1)
            op = e.collective_compute(
                "AllGather", ALU.bypass, replica_groups=[list(range(NCORES))],
                ins=[ag2_src[:, :]], outs=[ag2_dst[:, :, :]])
            s_cc.inc(op, 1)

        if debug:
            @sec("gpsimd")
            def _(e):
                s_cc.wait(e)
                s_gld.inc(e.dma_start(out=dbg["ag2"][:, :, :], in_=ag2_dst[:, :, :]), 16)
                s_gld.inc(e.dma_start(out=dbg["meta"][:, :, :], in_=meta_dst[:, :, :]), 16)
                s_gld.inc(e.dma_start(out=dbg["r2"][:, :], in_=r2_buf[:, :]), 16)

        p5.close()
        barrier()

        pf = ExitStack()
        act_sg = sbuf(pf, "act_sg", [128, 16 * 1024], BF16)
        h_sg = sbuf(pf, "h_sg", [128, 16 * 1024], F32)
        wblk = [sbuf(pf, f"wblk{i}", [128, 16 * 512], BF16) for i in range(2)]
        sxg = sbuf(pf, "sxg", [128, 1024], F32)
        sxu = sbuf(pf, "sxu", [128, 1024], F32)
        shb = sbuf(pf, "shb", [128, 1024], F32)
        hmax = sbuf(pf, "hmax", [128, 1024], F32)
        srow = sbuf(pf, "srow", [1, 1024], F32)
        irow = sbuf(pf, "irow", [1, 1024], F32)
        srow2 = sbuf(pf, "srow2", [1, 1024], F32)
        hmcl = sbuf(pf, "hmcl", [128, 8], F32)
        invsh = sbuf(pf, "invsh", [128, 8], F32)
        comb = sbuf(pf, "comb", [128, 8], F32)
        idxt = sbuf(pf, "idxt", [128, 8], F32)
        gtile = [sbuf(pf, f"gtile{i}", [128, 512], F32) for i in range(2)]
        utile = [sbuf(pf, f"utile{i}", [128, 512], F32) for i in range(2)]
        hmax2 = sbuf(pf, "hmax2", [128, 1024], F32)
        y_sb = [sbuf(pf, f"y{i}", [128, D], BF16) for i in range(1)]

        ybv = h_sg[:].bitcast(BF16)
        wblk_free = [0, 0]
        gbank_free = [0, 0]
        wd_tgt = [0, 0, 0, 0]
        ymm_done = [0, 0, 0, 0]
        gld_tgt = [0, 0, 0, 0]
        uld_tgt = [0, 0, 0, 0]

        for sg in range(8):
            @sec("sync")
            def _(e, sg=sg):
                s_cc.wait(e)
                if sg > 0:
                    e.wait_ge(s_mm.h, s_mm.n)
                op = e.dma_start(out=s3(act_sg[:], 16),
                                 in_=cmaj(ag2_dst[sg]))
                s_ld.inc(op, 16)
                op = e.dma_start(out=srow[:], in_=meta_dst[sg, 0].unsqueeze(0))
                s_ld.inc(op, 16)
                op = e.dma_start(out=irow[:], in_=meta_dst[sg, 1].unsqueeze(0))
                s_ld.inc(op, 16)

            @sec("tensor")
            def _(e, sg=sg):
                s_ld.wait(e)
                if sg > 0:
                    e.wait_ge(s_v.h, s_v.n)
                for j in range(2):
                    op = e.matmul(pb[6], ones_mf[:], srow[0:1, j * 512:(j + 1) * 512],
                                  start=True, stop=True)
                    s_mm.inc(op, 1)
                    e.wait_ge(s_v.h, s_v.n + j + 1)

            @sec("vector")
            def _(e, sg=sg):
                if sg > 0:
                    s_ld.wait(e)
                base = s_mm.n - 2
                e.memset(hmax[:], 0.0)
                e.memset(hmax2[:], 0.0)
                for j in range(2):
                    s_mm.wait(e, base + j + 1)
                    e.tensor_scalar(sxg[:, j * 512:(j + 1) * 512], pb[6],
                                    winv[:, 0:1], None, ALU.mult)
                    op = e.tensor_scalar(sxu[:, j * 512:(j + 1) * 512], pb[6],
                                         winv[:, 1:2], None, ALU.mult)
                    s_v.inc(op, 1)

            @sec("scalar")
            def _(e, sg=sg):
                e.wait_ge(s_mm.h, wblk_free[0])
                op = e.dma_start(out=s3(wblk[0][:], 16), in_=cmaj(wqt["g"][0]))
                s_sld.inc(op, 16)
                gld_tgt[0] = s_sld.n
                e.wait_ge(s_mm.h, wblk_free[1])
                op = e.dma_start(out=s3(wblk[1][:], 16), in_=cmaj(wqt["u"][0]))
                s_sld.inc(op, 16)
                uld_tgt[0] = s_sld.n

            for ds in range(4):
                for j in range(2):
                    @sec("tensor")
                    def _(e, ds=ds, j=j):
                        e.wait_ge(s_sld.h, gld_tgt[ds])
                        if gbank_free[0]:
                            e.wait_ge(s_v.h, gbank_free[0])
                        if ds == 0 and j == 0:
                            s_v.wait(e)
                            s_ld.wait(e)
                        for hc in range(16):
                            st, sp = hc == 0, hc == 15
                            for dc in range(4):
                                op = e.matmul(pb[dc],
                                              wblk[0][:, hc * 512 + dc * 128:hc * 512 + (dc + 1) * 128],
                                              act_sg[:, hc * 1024 + j * 512:hc * 1024 + (j + 1) * 512],
                                              start=st, stop=sp)
                        s_mm.inc(op, 1)
                        if j == 1:
                            wblk_free[0] = s_mm.n

                    @sec("tensor")
                    def _(e, ds=ds, j=j):
                        e.wait_ge(s_sld.h, uld_tgt[ds])
                        if ds == 0 and j == 0:
                            s_v.wait(e)
                        elif gbank_free[1]:
                            e.wait_ge(s_v.h, gbank_free[1])
                        for hc in range(16):
                            st, sp = hc == 0, hc == 15
                            for dc in range(4):
                                op = e.matmul(pb[4 + dc],
                                              wblk[1][:, hc * 512 + dc * 128:hc * 512 + (dc + 1) * 128],
                                              act_sg[:, hc * 1024 + j * 512:hc * 1024 + (j + 1) * 512],
                                              start=st, stop=sp)
                        s_mm.inc(op, 1)
                        if j == 1:
                            wblk_free[1] = s_mm.n

                    @sec("vector")
                    def _(e, ds=ds, j=j):
                        s_mm.wait(e)
                        hm = (hmax, hmax2)
                        gref = [None, None]
                        for dp in range(2):
                            dcs = (2 * dp, 2 * dp + 1)
                            for i, dc in enumerate(dcs):
                                op = e.tensor_tensor(gtile[i][:], pb[dc],
                                                     sxg[:, j * 512:(j + 1) * 512], ALU.mult)
                            if dp == 1:
                                s_v.inc(op, 1)
                                gbank_free[0] = s_v.n
                            for i, dc in enumerate(dcs):
                                op = e.tensor_tensor(utile[i][:], pb[4 + dc],
                                                     sxu[:, j * 512:(j + 1) * 512], ALU.mult)
                            if dp == 1:
                                s_v.inc(op, 1)
                                gbank_free[1] = s_v.n
                            for i, dc in enumerate(dcs):
                                e.scalar_tensor_tensor(gtile[i][:], gtile[i][:], 0.0,
                                                       gtile[i][:], ALU.max, ALU.mult)
                            for i, dc in enumerate(dcs):
                                col = (ds * 4 + dc) * 1024 + j * 512
                                e.tensor_tensor(h_sg[:, col:col + 512], gtile[i][:],
                                                utile[i][:], ALU.mult)
                            for i, dc in enumerate(dcs):
                                col = (ds * 4 + dc) * 1024 + j * 512
                                e.tensor_mul(gtile[i][:], h_sg[:, col:col + 512],
                                             h_sg[:, col:col + 512])
                            for i, dc in enumerate(dcs):
                                op = e.tensor_max(hm[i][:, j * 512:(j + 1) * 512],
                                                  hm[i][:, j * 512:(j + 1) * 512],
                                                  gtile[i][:])
                        s_v.inc(op, 1)

                if ds < 3:
                    @sec("scalar")
                    def _(e, ds=ds, sg=sg):
                        e.wait_ge(s_mm.h, wblk_free[0])
                        op = e.dma_start(out=s3(wblk[0][:], 16), in_=cmaj(wqt["g"][ds + 1]))
                        s_sld.inc(op, 16)
                        gld_tgt[ds + 1] = s_sld.n
                        e.wait_ge(s_mm.h, wblk_free[1])
                        op = e.dma_start(out=s3(wblk[1][:], 16), in_=cmaj(wqt["u"][ds + 1]))
                        s_sld.inc(op, 16)
                        uld_tgt[ds + 1] = s_sld.n

            @sec("vector")
            def _(e):
                e.wait_ge(s_v.h, s_v.n)
                op = e.tensor_max(hmax[:], hmax[:], hmax2[:])
                vf(e, op)

            @sec("tensor")
            def _(e):
                s_v.wait(e)
                for w in range(8):
                    bank = pb[5] if w < 4 else pb[4]
                    op = e.transpose(bank[:, (w % 4) * 128:(w % 4 + 1) * 128],
                                     hmax[:, w * 128:(w + 1) * 128], identf[:])
                s_mm.inc(op, 1)

            @sec("vector")
            def _(e):
                s_mm.wait(e)
                for w in range(8):
                    bank = pb[5] if w < 4 else pb[4]
                    op = e.reduce_max(hmcl[:, w:w + 1],
                                      bank[:, (w % 4) * 128:(w % 4 + 1) * 128], axis=AX.X)
                s_v.inc(op, 1)

            @sec("scalar")
            def _(e):
                s_v.wait(e)
                op = e.activation(hmcl[:], hmcl[:], AF.Sqrt)
                s_sc.inc(op, 1)

            @sec("vector")
            def _(e):
                s_sc.wait(e)
                vf(e, e.tensor_scalar(hmcl[:], hmcl[:], EPS, None, ALU.max))
                op = e.tensor_scalar(invsh[:], hmcl[:], 1.0 / 127.0, None, ALU.mult)
                s_v.inc(op, 1)

            for w in range(8):
                @sec("tensor")
                def _(e, w=w):
                    s_v.wait(e)
                    op = e.transpose(pb[6 + w // 4][0:1, (w % 4) * 128:(w % 4 + 1) * 128],
                                     hmcl[:, w:w + 1], identf[:])
                    s_mm.inc(op, 1)

            @sec("vector")
            def _(e):
                s_mm.wait(e)
                e.tensor_copy(srow2[0:1, 0:512], pb[6][0:1, 0:512])
                vf(e, e.tensor_copy(srow2[0:1, 512:1024], pb[7][0:1, 0:512]))
                vf(e, e.reciprocal(srow2[:], srow2[:]))
                op = e.tensor_scalar(srow2[:], srow2[:], 127.0, None, ALU.mult)
                s_v.inc(op, 1)

            @sec("tensor")
            def _(e):
                s_v.wait(e)
                for j in range(2):
                    op = e.matmul(pb[7], ones_mf[:], srow2[0:1, j * 512:(j + 1) * 512],
                                  start=True, stop=True)
                    s_mm.inc(op, 1)
                    e.wait_ge(s_v.h, s_v.n + j + 1)

            @sec("vector")
            def _(e):
                base = s_mm.n - 2
                for j in range(2):
                    s_mm.wait(e, base + j + 1)
                    op = e.tensor_copy(shb[:, j * 512:(j + 1) * 512], pb[7])
                    s_v.inc(op, 1)

            @sec("vector")
            def _(e):
                for ch in range(16):
                    for j in range(2):
                        col = ch * 1024 + j * 512
                        e.tensor_tensor(gtile[j][:], h_sg[:, col:col + 512],
                                        shb[:, j * 512:(j + 1) * 512], ALU.mult)
                    for j in range(2):
                        e.tensor_scalar(gtile[j][:], gtile[j][:], MAGIC, -MAGIC,
                                        ALU.add, ALU.add)
                    for j in range(2):
                        col = ch * 1024 + j * 512
                        op = e.tensor_scalar(act_sg[:, col:col + 512], gtile[j][:],
                                             127.0, -128.0, ALU.min, ALU.max)
                s_v.inc(op, 1)

            @sec("tensor")
            def _(e):
                s_v.wait(e)
                s_ld.wait(e)
                for w in range(8):
                    op = e.transpose(pb[4][:, w * 4:w * 4 + 1],
                                     irow[0:1, w * 128:(w + 1) * 128], identf[0:1, 0:1])
                s_mm.inc(op, 1)

            @sec("vector")
            def _(e):
                s_mm.wait(e)
                for w in range(8):
                    e.tensor_copy(idxt[:, w:w + 1], pb[4][:, w * 4:w * 4 + 1])
                for w in range(8):
                    e.tensor_tensor(comb[:, w:w + 1], idxt[:, w:w + 1], eidb[:],
                                    ALU.is_equal)
                for w in range(8):
                    e.tensor_tensor(comb[:, w:w + 1], comb[:, w:w + 1],
                                    invsh[:, w:w + 1], ALU.mult)
                for w in range(8):
                    op = e.tensor_scalar(comb[:, w:w + 1], comb[:, w:w + 1],
                                         winv[:, 2:3], None, ALU.mult)
                s_v.inc(op, 1)

            @sec("scalar")
            def _(e, sg=sg):
                for hfg in range(2):
                    e.wait_ge(s_mm.h, wblk_free[hfg])
                    op = e.dma_start(out=s3(wblk[hfg][:], 16), in_=cmaj(wqt["d"][hfg]))
                    s_sld.inc(op, 16)
                    wd_tgt[hfg] = s_sld.n

            for hfg in range(4):
                @sec("tensor")
                def _(e, hfg=hfg):
                    e.wait_ge(s_sld.h, wd_tgt[hfg])
                    s_v.wait(e)
                    for tt in range(8):
                        for dc in range(16):
                            op = e.matmul(pb[tt],
                                          act_sg[:, dc * 1024 + tt * 128:dc * 1024 + (tt + 1) * 128],
                                          wblk[hfg % 2][:, dc * 512:(dc + 1) * 512],
                                          start=(dc == 0), stop=(dc == 15))
                    s_mm.inc(op, 1)
                    ymm_done[hfg] = s_mm.n

                if hfg < 2:
                    @sec("scalar")
                    def _(e, hfg=hfg):
                        e.wait_ge(s_mm.h, ymm_done[hfg])
                        op = e.dma_start(out=s3(wblk[hfg][:], 16), in_=cmaj(wqt["d"][hfg + 2]))
                        s_sld.inc(op, 16)
                        wd_tgt[hfg + 2] = s_sld.n

                @sec("vector")
                def _(e, hfg=hfg):
                    s_mm.wait(e)
                    for tt in range(8):
                        op = e.tensor_scalar(
                            ybv[:, tt * 2048 + hfg * 512:tt * 2048 + (hfg + 1) * 512],
                            pb[tt], comb[:, tt:tt + 1], None, ALU.mult)
                    s_v.inc(op, 1)

            @sec("sync")
            def _(e, sg=sg):
                s_v.wait(e)
                for tt in range(8):
                    s_ld.inc(e.dma_start(
                        out=y_src[sg * 1024 + tt * 128:sg * 1024 + (tt + 1) * 128, :],
                        in_=ybv[:, tt * 2048:(tt + 1) * 2048]), 16)

            wblk_free[0] = ymm_done[2]
            wblk_free[1] = ymm_done[3]

        @sec("gpsimd")
        def _(e):
            e.wait_ge(s_ld.h, s_ld.n)
            op = e.collective_compute(
                "ReduceScatter", ALU.add, replica_groups=[list(range(NCORES))],
                ins=[y_src[:, :]], outs=[rs2_dst[:, :]])
            s_cc.inc(op, 1)

        if debug:
            @sec("gpsimd")
            def _(e):
                s_cc.wait(e)
                s_gld.inc(e.dma_start(out=dbg["ysrc"][:, :], in_=y_src[0:TC, :]), 16)
                s_gld.inc(e.dma_start(out=dbg["rs2"][:, :], in_=rs2_dst[:, :]), 16)

        pf.close()
        barrier()

        p8 = ExitStack()
        f1 = [sbuf(p8, f"f1_{i}", [128, D], F32) for i in range(2)]
        f2 = [sbuf(p8, f"f2_{i}", [128, D], BF16) for i in range(2)]
        fo = [sbuf(p8, f"fo_{i}", [128, D], F32) for i in range(2)]
        for t in range(8):
            @sec("sync")
            def _(e, t=t):
                s_cc.wait(e)
                if t >= 2:
                    e.wait_ge(s_v.h, s_v.n)
                s_ld.inc(e.dma_start(out=f1[t % 2][:],
                                     in_=r2_buf[t * 128:(t + 1) * 128, :]), 16)
                s_ld.inc(e.dma_start(out=f2[t % 2][:],
                                     in_=rs2_dst[t * 128:(t + 1) * 128, :]), 16)

            @sec("vector")
            def _(e, t=t):
                s_ld.wait(e)
                if t >= 2:
                    s_sld.wait(e)
                op = e.tensor_add(fo[t % 2][:], f1[t % 2][:], f2[t % 2][:])
                s_v.inc(op, 1)

            @sec("scalar")
            def _(e, t=t):
                s_v.wait(e)
                op = e.dma_start(out=out[t * 128:(t + 1) * 128, :], in_=fo[t % 2][:])
                s_sld.inc(op, 16)

        @sec("scalar")
        def _(e):
            e.wait_ge(s_sld.h, s_sld.n)
        @sec("sync")
        def _(e):
            e.wait_ge(s_sld.h, s_sld.n)

        p8.close()

    return nc, dbg



def _host_inputs(inputs):
    x = np.ascontiguousarray(np.asarray(inputs["L0_attn_normalized_resid_pre"], np.float32)[0])
    WQ = np.asarray(inputs["L0_attn_W_Q"], np.float32)
    WK = np.asarray(inputs["L0_attn_W_K"], np.float32)
    WV = np.asarray(inputs["L0_attn_W_V"], np.float32)
    WO = np.asarray(inputs["L0_attn_W_O"], np.float32)
    bQ = np.asarray(inputs["L0_attn_b_Q"], np.float32)
    bK = np.asarray(inputs["L0_attn_b_K"], np.float32)
    bV = np.asarray(inputs["L0_attn_b_V"], np.float32)
    bO = np.asarray(inputs["L0_attn_b_O"], np.float32)
    RW = np.asarray(inputs["L0_ffn_router_w"], np.float32)
    WG = np.asarray(inputs["L0_ffn_wg"], np.float32)
    WU = np.asarray(inputs["L0_ffn_wu"], np.float32)
    WD = np.asarray(inputs["L0_ffn_wd"], np.float32)

    perm = np.concatenate([np.arange(0, 128, 2), np.arange(1, 128, 2)])
    inv_freq = 1.0 / (10000.0 ** (np.arange(0, 128, 2, dtype=np.float32) / 128.0))
    m = np.arange(T, dtype=np.float32)
    freqs = np.outer(m, inv_freq)
    cos64 = np.cos(freqs).T.astype(np.float32)
    sin64 = np.sin(freqs).T.astype(np.float32)
    cos_t = np.ascontiguousarray(np.concatenate([cos64, cos64], axis=0))
    sin_t = np.ascontiguousarray(np.concatenate([-sin64, sin64], axis=0))
    swapi = np.ascontiguousarray(np.roll(np.eye(128, dtype=ml_dtypes.bfloat16), 64, axis=0))

    masks = np.zeros((4, 128, 512), dtype=ml_dtypes.bfloat16)
    for o in range(4):
        kk = np.arange(128)[:, None] + 128 * o
        qq = np.arange(512)[None, :]
        masks[o] = (qq >= kk).astype(ml_dtypes.bfloat16)

    common = {
        "rw": RW, "cos": cos_t, "sin": sin_t, "masks": masks,
        "identb": np.eye(128, dtype=ml_dtypes.bfloat16),
        "identf": np.eye(128, dtype=np.float32),
        "ones_k": np.ones((128, 1), ml_dtypes.bfloat16),
        "ones_m": np.ones((1, 128), ml_dtypes.bfloat16),
        "ones_mf": np.ones((1, 128), np.float32),
        "ones_kf": np.ones((128, 1), np.float32),
        "swapi": swapi,
        "magic": np.ascontiguousarray(np.stack([np.full(128, MAGIC, np.float32),
                                                np.full(128, -MAGIC, np.float32)], axis=1)),
        "bo": np.ascontiguousarray(bO[None, :]),
    }
    in_maps = []
    for c in range(NCORES):
        hs = slice(2 * c, 2 * c + 2)
        d = dict(common)
        d.update({
            "x": np.ascontiguousarray(x[c * TC:(c + 1) * TC]),
            "wq": np.ascontiguousarray(WQ[hs][:, :, perm]),
            "wk": np.ascontiguousarray(WK[hs][:, :, perm]),
            "wv": np.ascontiguousarray(WV[hs]),
            "wo": np.ascontiguousarray(WO[hs]),
            "bq": np.ascontiguousarray(bQ[hs][:, perm]),
            "bk": np.ascontiguousarray(bK[hs][:, perm]),
            "bv": np.ascontiguousarray(bV[hs]),
            "wg": np.ascontiguousarray(WG[c]),
            "wu": np.ascontiguousarray(WU[c]),
            "wd": np.ascontiguousarray(WD[c]),
            "eid": np.full((1, 1), float(c), np.float32),
        })
        in_maps.append(d)
    return in_maps


def kernel(**inputs):
    in_maps = _host_inputs(inputs)
    nc, _ = build_program(debug=DEBUG)
    res = run_bass_kernel_spmd(nc, in_maps, core_ids=list(range(NCORES)))
    outv = np.concatenate([np.asarray(r["out"]) for r in res.results], axis=0)
    return outv[None, :, :].astype(np.float32)
